# version 07
# baseline (speedup 1.0000x reference)
import numpy as np
import ml_dtypes
import concourse.bass as bass
import concourse.mybir as mybir
from concourse.bass_utils import run_bass_kernel_spmd
from concourse.alu_op_type import AluOpType as OP
from contextlib import ExitStack

F32 = mybir.dt.float32
BF16 = mybir.dt.bfloat16
AF = mybir.ActivationFunctionType
AX = mybir.AxisListType

T, D, E, FF, FS = 8192, 1024, 8, 512, 1024
KC = 8
NTB = 64
NTCA = 16
NQT1 = 16
NKB = 64
QB2, NQB2 = 256, 32
TCE, NTCE = 512, 16
NPE = 10
NEGINF = -1.0e30
BIG = 1.0e9
EPS = 1e-5


def n_tile_b(qt, kb):
    return 2 * qt * (qt + 1) + kb


def n_tile_c(qb, kb):
    return qb * (qb + 1) + kb


NB_TOT = n_tile_b(15, 63 + 1 - 60) + 64
NC_TOT = 32 * 33


def build_program(phases=("A", "B", "C", "E"), dbg=False):
    nc = bass.Bass("TRN2", target_bir_lowering=False)

    x_h = nc.dram_tensor("x", [T, D], F32, kind="ExternalInput").ap()
    wq_h = nc.dram_tensor("wq", [D, D], F32, kind="ExternalInput").ap()
    wk_h = nc.dram_tensor("wk", [D, D], F32, kind="ExternalInput").ap()
    wv_h = nc.dram_tensor("wv", [D, D], F32, kind="ExternalInput").ap()
    bq_h = nc.dram_tensor("bq", [D, 1], F32, kind="ExternalInput").ap()
    bk_h = nc.dram_tensor("bk", [D, 1], F32, kind="ExternalInput").ap()
    bv_h = nc.dram_tensor("bv", [1, D], F32, kind="ExternalInput").ap()
    rw_h = nc.dram_tensor("rw", [D, E], F32, kind="ExternalInput").ap()
    wg_h = nc.dram_tensor("wg", [E * D, FF], F32, kind="ExternalInput").ap()
    wu_h = nc.dram_tensor("wu", [E * D, FF], F32, kind="ExternalInput").ap()
    wd_h = nc.dram_tensor("wd", [E * FF, D], F32, kind="ExternalInput").ap()
    wsg_h = nc.dram_tensor("wsg", [D, FS], F32, kind="ExternalInput").ap()
    wsu_h = nc.dram_tensor("wsu", [D, FS], F32, kind="ExternalInput").ap()
    wsd_h = nc.dram_tensor("wsd", [FS, D], F32, kind="ExternalInput").ap()
    ident_h = nc.dram_tensor("ident", [128, 128], BF16, kind="ExternalInput").ap()
    maskm_h = nc.dram_tensor("maskm", [128, 896], F32, kind="ExternalInput").ap()
    sel_h = nc.dram_tensor("sel", [E, E * 128], BF16, kind="ExternalInput").ap()

    out_h = nc.dram_tensor("out", [T, D], F32, kind="ExternalOutput").ap()

    sk = "ExternalOutput" if dbg else "Internal"
    qt_h = nc.dram_tensor("qt_s", [D, T], BF16, kind=sk).ap()
    kt_h = nc.dram_tensor("kt_s", [D, T], BF16, kind=sk).ap()
    v_h = nc.dram_tensor("v_s", [T, D], BF16, kind=sk).ap()
    h1_h = nc.dram_tensor("h1_s", [T, D], F32, kind=sk).ap()
    if dbg:
        dbias_h = nc.dram_tensor("d_bias", [128, NKB], F32, kind="ExternalOutput").ap()
        dmn_h = nc.dram_tensor("d_mn", [128, NKB * 16], F32, kind="ExternalOutput").ap()
        de_h = nc.dram_tensor("d_e", [128, NKB * 16], F32, kind="ExternalOutput").ap()
        ds_h = nc.dram_tensor("d_s", [128, NKB], F32, kind="ExternalOutput").ap()

    es = ExitStack()
    sem = lambda name: es.enter_context(nc.semaphore(name))

    ident = es.enter_context(nc.sbuf_tensor("ident_s", [128, 128], BF16))
    sel = es.enter_context(nc.sbuf_tensor("sel_s", [E, E * 128], BF16))
    ones1 = es.enter_context(nc.sbuf_tensor("ones1", [1, 128], BF16))
    eps_t = es.enter_context(nc.sbuf_tensor("eps_t", [128, 1], F32))
    bias_st = es.enter_context(nc.sbuf_tensor("bias_st", [128, NKB], F32))

    sA_c = sem("sA_c")

    if "A" in phases:
        with ExitStack() as ps:
            xa = ps.enter_context(nc.sbuf_tensor("xa", [128, 8 * 1024], BF16))
            wq_s = ps.enter_context(nc.sbuf_tensor("wq_s", [128, 8 * 1024], BF16))
            wk_s = ps.enter_context(nc.sbuf_tensor("wk_s", [128, 8 * 1024], BF16))
            wv_s = ps.enter_context(nc.sbuf_tensor("wv_s", [128, 8 * 1024], BF16))
            bq_s = ps.enter_context(nc.sbuf_tensor("bq_s", [128, 8], F32))
            bk_s = ps.enter_context(nc.sbuf_tensor("bk_s", [128, 8], F32))
            bv_s = ps.enter_context(nc.sbuf_tensor("bv_s", [1, 1024], BF16))
            xt = ps.enter_context(nc.sbuf_tensor("xt", [128, 2 * 4096], BF16))
            qs = ps.enter_context(nc.sbuf_tensor("qs", [128, 2 * 512], BF16))
            ks = ps.enter_context(nc.sbuf_tensor("ks", [128, 2 * 512], BF16))
            vs = ps.enter_context(nc.sbuf_tensor("vs", [128, 2 * 1024], BF16))
            pq = [ps.enter_context(nc.psum_tensor(f"pq{i}", [128, 512], F32)) for i in range(2)]
            pk = [ps.enter_context(nc.psum_tensor(f"pk{i}", [128, 512], F32)) for i in range(2)]
            pv = ps.enter_context(nc.psum_tensor("pv", [128, 1024], F32))
            ptr = [ps.enter_context(nc.psum_tensor(f"ptr{i}", [128, 128], BF16)) for i in range(2)]

            sA_w = sem("sA_w")
            sA_x = [sem(f"sA_x{s}") for s in range(8)]
            sA_tr = sem("sA_tr")
            sA_trE = sem("sA_trE")
            sA_mm = sem("sA_mm")
            sA_pq = sem("sA_pq")
            sA_pk = sem("sA_pk")
            sA_pv = sem("sA_pv")
            sA_evq = sem("sA_evq")
            sA_evk = sem("sA_evk")
            sA_evv = sem("sA_evv")
            sA_stq = sem("sA_stq")
            sA_stk = sem("sA_stk")
            sA_stv = sem("sA_stv")

            blkA = ps.enter_context(nc.Block())

            @blkA.gpsimd
            def _(g):
                g.dma_start(ident[:, :], ident_h[:, :]).then_inc(sA_c, 16)
                g.dma_start(sel[:, :], sel_h[:, :]).then_inc(sA_c, 16)
                for t_i in range(8):
                    g.dma_start(xa[:, (t_i % 8) * 1024:(t_i % 8 + 1) * 1024],
                                x_h[t_i * 128:(t_i + 1) * 128, :]).then_inc(sA_x[t_i % 8], 16)
                for kc in range(KC):
                    g.dma_start(wq_s[:, kc * 1024:(kc + 1) * 1024],
                                wq_h[kc * 128:(kc + 1) * 128, :]).then_inc(sA_w, 16)
                for kc in range(KC):
                    g.dma_start(wk_s[:, kc * 1024:(kc + 1) * 1024],
                                wk_h[kc * 128:(kc + 1) * 128, :]).then_inc(sA_w, 16)
                for kc in range(KC):
                    g.dma_start(wv_s[:, kc * 1024:(kc + 1) * 1024],
                                wv_h[kc * 128:(kc + 1) * 128, :]).then_inc(sA_w, 16)
                for dc in range(8):
                    g.dma_start(bq_s[:, dc:dc + 1],
                                bq_h[dc * 128:(dc + 1) * 128, 0:1]).then_inc(sA_w, 16)
                for dc in range(8):
                    g.dma_start(bk_s[:, dc:dc + 1],
                                bk_h[dc * 128:(dc + 1) * 128, 0:1]).then_inc(sA_w, 16)
                g.dma_start(bv_s[:, :], bv_h[:, :]).then_inc(sA_w, 16)
                g.memset(ones1[:, :], 1.0).then_inc(sA_w, 1)
                g.memset(eps_t[:, :], EPS).then_inc(sA_w, 1)
                for t_i in range(8, NTB):
                    g.wait_ge(sA_tr, 8 * (t_i - 8) + 8)
                    g.dma_start(xa[:, (t_i % 8) * 1024:(t_i % 8 + 1) * 1024],
                                x_h[t_i * 128:(t_i + 1) * 128, :]).then_inc(sA_x[t_i % 8], 16)

            W_TOT = 41 * 16 + 2

            def emit_tr(t, tc):
                for tbl in range(4):
                    t_i = tc * 4 + tbl
                    t.wait_ge(sA_x[t_i % 8], 16 * (t_i // 8 + 1))
                    for dc in range(8):
                        i = t_i * 8 + dc
                        if i >= 2:
                            t.wait_ge(sA_trE, i - 1)
                        t.transpose(
                            ptr[i % 2][:, :],
                            xa[:, (t_i % 8) * 1024 + dc * 128:(t_i % 8) * 1024 + (dc + 1) * 128],
                            ident[:, :]).then_inc(sA_tr, 1)

            @blkA.tensor
            def _(t):
                t.wait_ge(sA_c, 32)
                emit_tr(t, 0)
                for tc in range(NTCA):
                    par = tc % 2
                    if tc + 1 < NTCA:
                        emit_tr(t, tc + 1)
                    t.wait_ge(sA_trE, 32 * (tc + 1))
                    if tc == 0:
                        t.wait_ge(sA_w, W_TOT)

                    def q_group(dc, which):
                        ws, pp, sev, spp = (wq_s, pq, sA_evq, sA_pq) if which == 0 else (
                            wk_s, pk, sA_evk, sA_pk)
                        g = tc * 8 + dc
                        if g >= 2:
                            t.wait_ge(sev, g - 1)
                        for kc in range(KC):
                            mm = t.matmul(
                                pp[g % 2][:, :],
                                ws[:, kc * 1024 + dc * 128:kc * 1024 + (dc + 1) * 128],
                                xt[:, par * 4096 + kc * 512:par * 4096 + (kc + 1) * 512],
                                start=(kc == 0), stop=(kc == KC - 1))
                        mm.then_inc(spp, 1)
                        return mm

                    def v_group(tbl):
                        vi = tc * 4 + tbl
                        if vi >= 1:
                            t.wait_ge(sA_evv, vi)
                        for kc in range(KC):
                            for h in range(2):
                                t.matmul(
                                    pv[:, h * 512:(h + 1) * 512],
                                    xt[:, par * 4096 + kc * 512 + tbl * 128:
                                       par * 4096 + kc * 512 + (tbl + 1) * 128],
                                    wv_s[:, kc * 1024 + h * 512:kc * 1024 + (h + 1) * 512],
                                    start=(kc == 0), stop=False)
                        for h in range(2):
                            mm = t.matmul(pv[:, h * 512:(h + 1) * 512], ones1[:, :],
                                          bv_s[:, h * 512:(h + 1) * 512],
                                          start=False, stop=True)
                        mm.then_inc(sA_pv, 1)
                        return mm

                    last = None
                    for dc in range(4):
                        last = q_group(dc, 0)
                    last = v_group(0)
                    for dc in range(4, 8):
                        last = q_group(dc, 0)
                    last = v_group(1)
                    for dc in range(4):
                        last = q_group(dc, 1)
                    last = v_group(2)
                    for dc in range(4, 8):
                        last = q_group(dc, 1)
                    last = v_group(3)
                    t.nop().then_inc(sA_mm, 1)

            @blkA.vector
            def _(v):
                for tc in range(NTCA + 1):
                    if tc >= NTCA:
                        break
                    for tbl in range(4):
                        t_i = tc * 4 + tbl
                        for dc in range(8):
                            i = t_i * 8 + dc
                            v.wait_ge(sA_tr, i + 1)
                            if tbl == 0 and dc == 0 and tc >= 2:
                                v.wait_ge(sA_mm, tc - 1)
                            v.tensor_copy(
                                xt[:, (tc % 2) * 4096 + dc * 512 + tbl * 128:
                                   (tc % 2) * 4096 + dc * 512 + (tbl + 1) * 128],
                                ptr[i % 2][:, :]).then_inc(sA_trE, 1)

            @blkA.scalar
            def _(sc):
                for tc in range(NTCA):
                    def q_evac(dc, which):
                        ws_stage, pp, bb, spp, sev, sst = (
                            (qs, pq, bq_s, sA_pq, sA_evq, sA_stq) if which == 0
                            else (ks, pk, bk_s, sA_pk, sA_evk, sA_stk))
                        g = tc * 8 + dc
                        sc.wait_ge(spp, g + 1)
                        if g >= 2:
                            sc.wait_ge(sst, 16 * (g - 1))
                        sc.activation(ws_stage[:, (g % 2) * 512:(g % 2 + 1) * 512],
                                      pp[g % 2][:, :], AF.Identity,
                                      bias=bb[:, dc:dc + 1], scale=1.0).then_inc(sev, 1)

                    def v_evac(tbl):
                        vi = tc * 4 + tbl
                        sc.wait_ge(sA_pv, vi + 1)
                        if vi >= 2:
                            sc.wait_ge(sA_stv, 16 * (vi - 1))
                        sc.activation(vs[:, (vi % 2) * 1024:(vi % 2 + 1) * 1024],
                                      pv[:, :], AF.Copy).then_inc(sA_evv, 1)

                    for dc in range(4):
                        q_evac(dc, 0)
                    v_evac(0)
                    for dc in range(4, 8):
                        q_evac(dc, 0)
                    v_evac(1)
                    for dc in range(4):
                        q_evac(dc, 1)
                    v_evac(2)
                    for dc in range(4, 8):
                        q_evac(dc, 1)
                    v_evac(3)

            @blkA.sync
            def _(s):
                for tc in range(NTCA):
                    def q_store(dc, which):
                        stage, dst, sev, sst = (qs, qt_h, sA_evq, sA_stq) if which == 0 else (
                            ks, kt_h, sA_evk, sA_stk)
                        g = tc * 8 + dc
                        s.wait_ge(sev, g + 1)
                        s.dma_start(dst[dc * 128:(dc + 1) * 128, tc * 512:(tc + 1) * 512],
                                    stage[:, (g % 2) * 512:(g % 2 + 1) * 512]).then_inc(sst, 16)

                    def v_store(tbl):
                        vi = tc * 4 + tbl
                        s.wait_ge(sA_evv, vi + 1)
                        s.dma_start(v_h[vi * 128:(vi + 1) * 128, :],
                                    vs[:, (vi % 2) * 1024:(vi % 2 + 1) * 1024]).then_inc(sA_stv, 16)

                    for dc in range(4):
                        q_store(dc, 0)
                    v_store(0)
                    for dc in range(4, 8):
                        q_store(dc, 0)
                    v_store(1)
                    for dc in range(4):
                        q_store(dc, 1)
                    v_store(2)
                    for dc in range(4, 8):
                        q_store(dc, 1)
                    v_store(3)
                s.wait_ge(sA_stq, 16 * 128)
                s.wait_ge(sA_stk, 16 * 128)
                s.wait_ge(sA_stv, 16 * 64)

    if "B" in phases or "C" in phases:
        ps_bc = ExitStack()
        kt_sb = ps_bc.enter_context(nc.sbuf_tensor("kt_sb", [128, 8 * 8192], BF16))
        maskm = ps_bc.enter_context(nc.sbuf_tensor("maskm_s", [128, 896], F32))
        sBC_kt = sem("sBC_kt")

    if "B" in phases:
        with ExitStack() as ps:
            qt1 = ps.enter_context(nc.sbuf_tensor("qt1", [128, 2 * 4096], BF16))
            mneg_st = ps.enter_context(nc.sbuf_tensor("mneg_st", [128, NKB * 16], F32))
            e_st = ps.enter_context(nc.sbuf_tensor("e_st", [128, NKB * 16], F32))
            tmp_st = ps.enter_context(nc.sbuf_tensor("tmp_st", [128, NKB * 16], F32))
            sc1 = ps.enter_context(nc.sbuf_tensor("sc1", [128, 2 * 512], BF16))
            mnegf_st = ps.enter_context(nc.sbuf_tensor("mnegf_st", [128, NKB], F32))
            s_st = ps.enter_context(nc.sbuf_tensor("s_st", [128, NKB], F32))
            lns_st = ps.enter_context(nc.sbuf_tensor("lns_st", [128, NKB], F32))
            ps1 = [ps.enter_context(nc.psum_tensor(f"ps1_{i}", [128, 512], F32)) for i in range(2)]

            sB_q = sem("sB_q")
            sB_pe = sem("sB_pe")
            sB_v = sem("sB_v")
            sB_a = sem("sB_a")
            sB_vf = sem("sB_vf")
            sB_af = sem("sB_af")
            sB_vf2 = sem("sB_vf2")
            sB_af2 = sem("sB_af2")
            sB_fin = sem("sB_fin")
            sB_vs = sem("sB_vs")

            blkB = ps.enter_context(nc.Block())

            @blkB.sync
            def _(s):
                s.dma_start(maskm[:, :], maskm_h[:, :]).then_inc(sBC_kt, 16)
                for kc in range(KC):
                    s.dma_start(kt_sb[:, kc * 8192:(kc + 1) * 8192],
                                kt_h[kc * 128:(kc + 1) * 128, :]).then_inc(sBC_kt, 16)
                for qt in range(NQT1):
                    if qt >= 2:
                        s.wait_ge(sB_pe, n_tile_b(qt - 1, 0))
                    for kc in range(KC):
                        s.dma_start(qt1[:, (qt % 2) * 4096 + kc * 512:
                                        (qt % 2) * 4096 + (kc + 1) * 512],
                                    qt_h[kc * 128:(kc + 1) * 128,
                                         qt * 512:(qt + 1) * 512]).then_inc(sB_q, 16)
                if dbg and "C" not in phases:
                    s.wait_ge(sB_fin, NKB)
                    s.dma_start(dbias_h[:, :], bias_st[:, :]).then_inc(sB_q, 16)
                    s.dma_start(dmn_h[:, :], mneg_st[:, :]).then_inc(sB_q, 16)
                    s.dma_start(de_h[:, :], e_st[:, :]).then_inc(sB_q, 16)
                    s.dma_start(ds_h[:, :], s_st[:, :]).then_inc(sB_q, 16)
                    s.wait_ge(sB_q, 128 * NQT1 + 64)

            @blkB.tensor
            def _(t):
                t.wait_ge(sBC_kt, 16 * (KC + 1))
                for qt in range(NQT1):
                    t.wait_ge(sB_q, 128 * (qt + 1))
                    for kb in range(4 * qt + 4):
                        n = n_tile_b(qt, kb)
                        if n >= 2:
                            t.wait_ge(sB_a, n - 1)
                        for kc in range(KC):
                            mm = t.matmul(
                                ps1[n % 2][:, :],
                                kt_sb[:, kc * 8192 + kb * 128:kc * 8192 + (kb + 1) * 128],
                                qt1[:, (qt % 2) * 4096 + kc * 512:(qt % 2) * 4096 + (kc + 1) * 512],
                                start=(kc == 0), stop=(kc == KC - 1))
                        mm.then_inc(sB_pe, 1)

            @blkB.vector
            def _(v):
                nvs = [0]

                def vself(instr):
                    nvs[0] += 1
                    instr.then_inc(sB_vs, 1)
                    v.wait_ge(sB_vs, nvs[0])

                v.wait_ge(sBC_kt, 16)
                for qt in range(NQT1):
                    for kb in range(4 * qt + 4):
                        n = n_tile_b(qt, kb)
                        v.wait_ge(sB_pe, n + 1)
                        if kb >= 4 * qt:
                            off = kb * 128 - qt * 512
                            sm = 384 - off
                            vself(v.tensor_add(ps1[n % 2][:, :], ps1[n % 2][:, :],
                                               maskm[:, sm:sm + 512]))
                        v.reduce_max(mneg_st[:, kb * 16 + qt:kb * 16 + qt + 1],
                                     ps1[n % 2][:, :], axis=AX.X,
                                     negate=True).then_inc(sB_v, 1)
                for kb in range(NKB):
                    qt0 = kb // 4
                    v.tensor_reduce(mnegf_st[:, kb:kb + 1],
                                    mneg_st[:, kb * 16 + qt0:kb * 16 + 16],
                                    axis=AX.X, op=OP.min).then_inc(sB_vf, 1)
                for kb in range(NKB):
                    qt0 = kb // 4
                    v.wait_ge(sB_af, kb + 1)
                    vself(v.tensor_tensor(tmp_st[:, kb * 16 + qt0:kb * 16 + 16],
                                          tmp_st[:, kb * 16 + qt0:kb * 16 + 16],
                                          e_st[:, kb * 16 + qt0:kb * 16 + 16], op=OP.mult))
                    v.reduce_sum(s_st[:, kb:kb + 1],
                                 tmp_st[:, kb * 16 + qt0:kb * 16 + 16],
                                 axis=AX.X).then_inc(sB_vf2, 1)
                for kb in range(NKB):
                    v.wait_ge(sB_af2, kb + 1)
                    v.tensor_sub(bias_st[:, kb:kb + 1], mnegf_st[:, kb:kb + 1],
                                 lns_st[:, kb:kb + 1]).then_inc(sB_fin, 1)

            @blkB.scalar
            def _(sc):
                for qt in range(NQT1):
                    for kb in range(4 * qt + 4):
                        n = n_tile_b(qt, kb)
                        sc.wait_ge(sB_v, n + 1)
                        sc.activation(sc1[:, (n % 2) * 512:(n % 2 + 1) * 512],
                                      ps1[n % 2][:, :], AF.Exp,
                                      bias=mneg_st[:, kb * 16 + qt:kb * 16 + qt + 1],
                                      scale=1.0,
                                      accum_out=e_st[:, kb * 16 + qt:kb * 16 + qt + 1]
                                      ).then_inc(sB_a, 1)
                for kb in range(NKB):
                    qt0 = kb // 4
                    sc.wait_ge(sB_vf, kb + 1)
                    sc.activation(tmp_st[:, kb * 16 + qt0:kb * 16 + 16],
                                  mneg_st[:, kb * 16 + qt0:kb * 16 + 16], AF.Exp,
                                  bias=mnegf_st[:, kb:kb + 1],
                                  scale=-1.0).then_inc(sB_af, 1)
                for kb in range(NKB):
                    sc.wait_ge(sB_vf2, kb + 1)
                    sc.activation(lns_st[:, kb:kb + 1], s_st[:, kb:kb + 1], AF.Ln,
                                  scale=32.0).then_inc(sB_af2, 1)

    if "C" in phases:
        with ExitStack() as ps:
            qt2 = ps.enter_context(nc.sbuf_tensor("qt2", [128, 2 * 2048], BF16))
            v2 = ps.enter_context(nc.sbuf_tensor("v2", [128, 4 * 1024], BF16))
            p2 = ps.enter_context(nc.sbuf_tensor("p2", [128, 2 * 256], BF16))
            x2 = ps.enter_context(nc.sbuf_tensor("x2", [128, 2 * 2048], F32))
            t1s = ps.enter_context(nc.sbuf_tensor("t1s", [128, 2 * 1024], F32))
            h1s = ps.enter_context(nc.sbuf_tensor("h1s", [128, 2 * 1024], F32))
            sq2 = ps.enter_context(nc.sbuf_tensor("sq2", [128, 1024], BF16))
            st2 = ps.enter_context(nc.sbuf_tensor("st2", [128, 16], F32))
            ps2 = [ps.enter_context(nc.psum_tensor(f"ps2_{i}", [128, 256], F32)) for i in range(2)]
            pr = [ps.enter_context(nc.psum_tensor(f"pr{i}", [128, 1024], F32)) for i in range(2)]

            sC_q = sem("sC_q")
            sC_x2 = sem("sC_x2")
            sC_v2 = [sem(f"sC_v2{s}") for s in range(4)]
            sC_ps = sem("sC_ps")
            sC_pr = sem("sC_pr")
            sC_vm = sem("sC_vm")
            sC_a = sem("sC_a")
            sC_vt = sem("sC_vt")
            sC_ae = sem("sC_ae")
            sC_vv = sem("sC_vv")
            sC_va = sem("sC_va")
            sC_sh = sem("sC_sh")
            sC_vs = sem("sC_vs")

            blkC = ps.enter_context(nc.Block())

            @blkC.sync
            def _(s):
                if "B" not in phases:
                    s.dma_start(maskm[:, :], maskm_h[:, :]).then_inc(sBC_kt, 16)
                    for kc in range(KC):
                        s.dma_start(kt_sb[:, kc * 8192:(kc + 1) * 8192],
                                    kt_h[kc * 128:(kc + 1) * 128, :]).then_inc(sBC_kt, 16)
                if dbg:
                    s.dma_start(dbias_h[:, :], bias_st[:, :]).then_inc(sC_sh, 16)

                def loads(qb):
                    if qb >= 2:
                        s.wait_ge(sC_pr, n_tile_c(qb - 1, 0))
                    for kc in range(KC):
                        s.dma_start(qt2[:, (qb % 2) * 2048 + kc * 256:
                                        (qb % 2) * 2048 + (kc + 1) * 256],
                                    qt_h[kc * 128:(kc + 1) * 128,
                                         qb * 256:(qb + 1) * 256]).then_inc(sC_q, 16)
                    for ic in range(2):
                        u = 2 * qb + ic
                        if qb >= 2:
                            s.wait_ge(sC_vt, 2 * (qb - 1))
                        s.dma_start(x2[:, (qb % 2) * 2048 + ic * 1024:
                                       (qb % 2) * 2048 + (ic + 1) * 1024],
                                    x_h[u * 128:(u + 1) * 128, :]).then_inc(sC_x2, 16)

                def stores(qb):
                    for ic in range(2):
                        u = 2 * qb + ic
                        s.wait_ge(sC_va, u + 1)
                        s.dma_start(h1_h[u * 128:(u + 1) * 128, :],
                                    h1s[:, (u % 2) * 1024:(u % 2 + 1) * 1024]
                                    ).then_inc(sC_sh, 16)

                loads(0)
                for qb in range(1, NQB2):
                    loads(qb)
                    stores(qb - 1)
                stores(NQB2 - 1)
                s.wait_ge(sC_sh, 16 * 2 * NQB2 + (16 if dbg else 0))

            @blkC.gpsimd
            def _(g):
                for qb in range(NQB2):
                    for kb in range(2 * qb + 2):
                        n = n_tile_c(qb, kb)
                        if n >= 4:
                            g.wait_ge(sC_pr, n - 3)
                        g.dma_start(v2[:, (n % 4) * 1024:(n % 4 + 1) * 1024],
                                    v_h[kb * 128:(kb + 1) * 128, :]).then_inc(sC_v2[n % 4], 16)

            @blkC.tensor
            def _(t):
                t.wait_ge(sBC_kt, 16 * (KC + 1))

                def s_tile(qb, kb):
                    n = n_tile_c(qb, kb)
                    if n >= 2:
                        t.wait_ge(sC_a, n - 1)
                    for kc in range(KC):
                        mm = t.matmul(
                            ps2[n % 2][:, :],
                            kt_sb[:, kc * 8192 + kb * 128:kc * 8192 + (kb + 1) * 128],
                            qt2[:, (qb % 2) * 2048 + kc * 256:(qb % 2) * 2048 + (kc + 1) * 256],
                            start=(kc == 0), stop=(kc == KC - 1))
                    mm.then_inc(sC_ps, 1)

                def r_tile(qb, kb, nkb):
                    n = n_tile_c(qb, kb)
                    t.wait_ge(sC_a, n + 1)
                    t.wait_ge(sC_v2[n % 4], 16 * (n // 4 + 1))
                    if kb == 0:
                        t.wait_ge(sC_vt, 2 * qb)
                    for ic in range(2):
                        for h in range(2):
                            mm = t.matmul(
                                pr[ic][:, h * 512:(h + 1) * 512],
                                p2[:, (n % 2) * 256 + ic * 128:(n % 2) * 256 + (ic + 1) * 128],
                                v2[:, (n % 4) * 1024 + h * 512:(n % 4) * 1024 + (h + 1) * 512],
                                start=(kb == 0), stop=(kb == nkb - 1))
                    mm.then_inc(sC_pr, 1)

                for qb in range(NQB2):
                    nkb = 2 * qb + 2
                    t.wait_ge(sC_q, 128 * (qb + 1))
                    s_tile(qb, 0)
                    s_tile(qb, 1)
                    for kb in range(nkb - 2):
                        r_tile(qb, kb, nkb)
                        s_tile(qb, kb + 2)
                    r_tile(qb, nkb - 2, nkb)
                    r_tile(qb, nkb - 1, nkb)

            @blkC.vector
            def _(v):
                nvs = [0]

                def vself(instr):
                    nvs[0] += 1
                    instr.then_inc(sC_vs, 1)
                    v.wait_ge(sC_vs, nvs[0])

                v.wait_ge(sBC_kt, 16 * (KC + 1))
                for qb in range(NQB2):
                    nkb = 2 * qb + 2
                    for which, kb in enumerate((nkb - 2, nkb - 1)):
                        n = n_tile_c(qb, kb)
                        v.wait_ge(sC_ps, n + 1)
                        off = kb * 128 - qb * 256
                        sm = 384 - off
                        v.tensor_add(ps2[n % 2][:, :], ps2[n % 2][:, :],
                                     maskm[:, sm:sm + 256]).then_inc(sC_vm, 1)
                    for ic in range(2):
                        u = 2 * qb + ic
                        v.wait_ge(sC_pr, n_tile_c(qb, 0) + nkb)
                        v.wait_ge(sC_x2, 16 * (2 * qb + ic + 1))
                        if u >= 4:
                            v.wait_ge(sC_ae, 3 * (u - 2) + 1)
                        v.scalar_tensor_tensor(
                            t1s[:, ic * 1024:(ic + 1) * 1024],
                            x2[:, (qb % 2) * 2048 + ic * 1024:(qb % 2) * 2048 + (ic + 1) * 1024],
                            2.0, pr[ic][:, :], op0=OP.mult, op1=OP.add).then_inc(sC_vt, 1)
                        v.wait_ge(sC_vt, u + 1)
                        c0 = ic * 8
                        vself(v.reduce_sum(st2[:, c0:c0 + 1],
                                           t1s[:, ic * 1024:(ic + 1) * 1024], axis=AX.X))
                        v.wait_ge(sC_ae, 3 * u + 1)
                        vself(v.tensor_scalar(st2[:, c0 + 2:c0 + 3], st2[:, c0:c0 + 1],
                                              1.0 / D, None, op0=OP.mult))
                        vself(v.tensor_tensor(st2[:, c0 + 3:c0 + 4], st2[:, c0 + 2:c0 + 3],
                                              st2[:, c0 + 2:c0 + 3], op=OP.mult))
                        v.scalar_tensor_tensor(st2[:, c0 + 4:c0 + 5], st2[:, c0 + 1:c0 + 2],
                                               1.0 / D, st2[:, c0 + 3:c0 + 4],
                                               op0=OP.mult, op1=OP.subtract).then_inc(sC_vv, 1)
                        v.wait_ge(sC_ae, 3 * u + 3)
                        if u >= 2:
                            v.wait_ge(sC_sh, 16 * (u - 1) + (16 if dbg else 0))
                        v.tensor_scalar(h1s[:, (u % 2) * 1024:(u % 2 + 1) * 1024],
                                        t1s[:, ic * 1024:(ic + 1) * 1024],
                                        st2[:, c0 + 2:c0 + 3], st2[:, c0 + 6:c0 + 7],
                                        op0=OP.subtract, op1=OP.mult).then_inc(sC_va, 1)

            @blkC.scalar
            def _(sc):
                for qb in range(NQB2):
                    nkb = 2 * qb + 2
                    for kb in range(nkb):
                        n = n_tile_c(qb, kb)
                        sc.wait_ge(sC_ps, n + 1)
                        if kb >= nkb - 2:
                            sc.wait_ge(sC_vm, 2 * qb + (kb - (nkb - 2)) + 1)
                        if n >= 2:
                            sc.wait_ge(sC_pr, n - 1)
                        sc.activation(p2[:, (n % 2) * 256:(n % 2 + 1) * 256],
                                      ps2[n % 2][:, :], AF.Exp,
                                      bias=bias_st[:, kb:kb + 1], scale=1.0).then_inc(sC_a, 1)
                    for ic in range(2):
                        u = 2 * qb + ic
                        c0 = ic * 8
                        sc.wait_ge(sC_vt, u + 1)
                        sc.activation(sq2[:, :], t1s[:, ic * 1024:(ic + 1) * 1024], AF.Square,
                                      accum_out=st2[:, c0 + 1:c0 + 2]).then_inc(sC_ae, 1)
                        sc.wait_ge(sC_vv, u + 1)
                        sc.activation(st2[:, c0 + 5:c0 + 6], st2[:, c0 + 4:c0 + 5], AF.Ln,
                                      bias=eps_t[:, 0:1], scale=1.0).then_inc(sC_ae, 1)
                        sc.wait_ge(sC_ae, 3 * u + 2)
                        sc.activation(st2[:, c0 + 6:c0 + 7], st2[:, c0 + 5:c0 + 6], AF.Exp,
                                      scale=-0.5).then_inc(sC_ae, 1)

    if "B" in phases or "C" in phases:
        ps_bc.close()

    if "E" in phases:
        with ExitStack() as ps:
            wd_sb = ps.enter_context(nc.sbuf_tensor("wd_sb", [128, NPE * 4 * 1024], BF16))
            rw_sb = ps.enter_context(nc.sbuf_tensor("rw_sb", [128, 64], BF16))
            h1b = ps.enter_context(nc.sbuf_tensor("h1b", [128, 4 * 1024], BF16))
            h1t = ps.enter_context(nc.sbuf_tensor("h1t", [128, 2 * 4096], BF16))
            wgu = ps.enter_context(nc.sbuf_tensor("wgu", [128, 2 * 8192], BF16))
            a_sb = ps.enter_context(nc.sbuf_tensor("a_sb", [128, NPE * 4 * 512], BF16))
            wt_sb = ps.enter_context(nc.sbuf_tensor("wt_sb", [E, 512], BF16))
            wtil = ps.enter_context(nc.sbuf_tensor("wtil", [128, E * 512], BF16))
            sg_sb = ps.enter_context(nc.sbuf_tensor("sg_sb", [128, 2 * 512], BF16))
            ub_sb = ps.enter_context(nc.sbuf_tensor("ub_sb", [128, 512], BF16))
            h1f = ps.enter_context(nc.sbuf_tensor("h1f", [128, 2 * 1024], F32))
            t2s = ps.enter_context(nc.sbuf_tensor("t2s", [128, 1024], F32))
            st3 = ps.enter_context(nc.sbuf_tensor("st3", [128, 16], F32))
            wtb = ps.enter_context(nc.sbuf_tensor("wtb", [128, 4 * 8], BF16))
            lg_sb = ps.enter_context(nc.sbuf_tensor("lg_sb", [128, 8], F32))
            el_sb = ps.enter_context(nc.sbuf_tensor("el_sb", [128, 8], F32))
            rr = ps.enter_context(nc.sbuf_tensor("rr", [128, 32], F32))
            outs = ps.enter_context(nc.sbuf_tensor("outs", [128, 1024], F32))
            yA = ps.enter_context(nc.psum_tensor("yA", [128, 1024], F32))
            yB = ps.enter_context(nc.psum_tensor("yB", [128, 1024], F32))
            pg = ps.enter_context(nc.psum_tensor("pg", [128, 512], F32))
            pu = ps.enter_context(nc.psum_tensor("pu", [128, 512], F32))
            ptr2 = [ps.enter_context(nc.psum_tensor(f"ptr2_{i}", [128, 128], BF16))
                    for i in range(2)]

            sE_wrw = sem("sE_wrw")
            sE_wwd = sem("sE_wwd")
            sE_h1b = [sem(f"sE_h1b{s}") for s in range(4)]
            sE_wgu = [sem(f"sE_wgu{s}") for s in range(2)]
            sE_tr = sem("sE_tr")
            sE_trE = sem("sE_trE")
            sE_petc = sem("sE_petc")
            sE_pl = sem("sE_pl")
            sE_vcp = sem("sE_vcp")
            sE_ae2 = sem("sE_ae2")
            sE_vwt = sem("sE_vwt")
            sE_pw = sem("sE_pw")
            sE_aw = sem("sE_aw")
            sE_pg = sem("sE_pg")
            sE_pu = sem("sE_pu")
            sE_as = sem("sE_as")
            sE_vu = sem("sE_vu")
            sE_va = sem("sE_va")
            sE_py = sem("sE_py")
            sE_h1f = sem("sE_h1f")
            sE_vt2 = sem("sE_vt2")
            sE_ae3 = sem("sE_ae3")
            sE_vv3 = sem("sE_vv3")
            sE_vo = sem("sE_vo")
            sE_so = sem("sE_so")
            sE_vs = sem("sE_vs")

            blkE = ps.enter_context(nc.Block())

            def wgu_src(pe, kc, which):
                if pe < 8:
                    h = wg_h if which == 0 else wu_h
                    return h[pe * 1024 + kc * 128:pe * 1024 + (kc + 1) * 128, :]
                h = wsg_h if which == 0 else wsu_h
                c0 = (pe - 8) * 512
                return h[kc * 128:(kc + 1) * 128, c0:c0 + 512]

            def wd_src(pe, fc):
                if pe < 8:
                    return wd_h[pe * 512 + fc * 128:pe * 512 + (fc + 1) * 128, :]
                return wsd_h[(pe - 8) * 512 + fc * 128:(pe - 8) * 512 + (fc + 1) * 128, :]

            @blkE.gpsimd
            def _(g):
                if "C" in phases:
                    g.wait_ge(sC_sh, 16 * 2 * NQB2 + (16 if dbg else 0))
                for kc in range(KC):
                    g.dma_start(rw_sb[:, kc * 8:(kc + 1) * 8],
                                rw_h[kc * 128:(kc + 1) * 128, :]).then_inc(sE_wrw, 16)
                for gtb in range(4):
                    g.dma_start(h1b[:, (gtb % 4) * 1024:(gtb % 4 + 1) * 1024],
                                h1_h[gtb * 128:(gtb + 1) * 128, :]).then_inc(sE_h1b[gtb % 4], 16)
                for pe in range(2):
                    for which in range(2):
                        for kc in range(KC):
                            g.dma_start(
                                wgu[:, (pe % 2) * 8192 + which * 4096 + kc * 512:
                                    (pe % 2) * 8192 + which * 4096 + (kc + 1) * 512],
                                wgu_src(pe, kc, which)).then_inc(sE_wgu[pe % 2], 16)
                for pe in range(NPE):
                    for fc in range(4):
                        g.dma_start(wd_sb[:, (pe * 4 + fc) * 1024:(pe * 4 + fc + 1) * 1024],
                                    wd_src(pe, fc)).then_inc(sE_wwd, 16)
                for tc in range(NTCE):
                    for tbl in range(4):
                        gtb = tc * 4 + 4 + tbl
                        if gtb >= NTB:
                            continue
                        g.wait_ge(sE_tr, tc * 36 + (tbl + 1) * 8)
                        g.dma_start(h1b[:, (gtb % 4) * 1024:(gtb % 4 + 1) * 1024],
                                    h1_h[gtb * 128:(gtb + 1) * 128, :]).then_inc(sE_h1b[gtb % 4], 16)
                    for pe in range(NPE):
                        unit = tc * NPE + pe + 2
                        if unit >= NTCE * NPE:
                            continue
                        ntc, npe = unit // NPE, unit % NPE
                        g.wait_ge(sE_pu, 4 * (unit - 1))
                        for which in range(2):
                            for kc in range(KC):
                                g.dma_start(
                                    wgu[:, (unit % 2) * 8192 + which * 4096 + kc * 512:
                                        (unit % 2) * 8192 + which * 4096 + (kc + 1) * 512],
                                    wgu_src(npe, kc, which)).then_inc(sE_wgu[unit % 2], 16)

            @blkE.tensor
            def _(t):
                for tc in range(NTCE):
                    par = tc % 2
                    for tbl in range(4):
                        gtb = tc * 4 + tbl
                        t.wait_ge(sE_h1b[gtb % 4], 16 * (gtb // 4 + 1))
                        for dc in range(8):
                            i3 = tc * 36 + tbl * 8 + dc
                            if i3 >= 2:
                                t.wait_ge(sE_trE, i3 - 1)
                            t.transpose(
                                ptr2[i3 % 2][:, :],
                                h1b[:, (gtb % 4) * 1024 + dc * 128:(gtb % 4) * 1024 + (dc + 1) * 128],
                                ident[:, :]).then_inc(sE_tr, 1)
                    t.wait_ge(sE_trE, tc * 36 + 32)
                    if tc == 0:
                        t.wait_ge(sE_wrw, 16 * KC)
                    for tbl in range(4):
                        idx = tc * 4 + tbl
                        if idx >= 1:
                            t.wait_ge(sE_vcp, idx)
                        if tc >= 1 and tbl == 0:
                            t.wait_ge(sE_as, 40 * tc)
                        for kc in range(KC):
                            mm = t.matmul(
                                pg[:, 0:8],
                                h1t[:, par * 4096 + kc * 512 + tbl * 128:
                                    par * 4096 + kc * 512 + (tbl + 1) * 128],
                                rw_sb[:, kc * 8:(kc + 1) * 8],
                                start=(kc == 0), stop=(kc == KC - 1))
                        mm.then_inc(sE_pl, 1)
                    for tbl in range(4):
                        idx = tc * 4 + tbl
                        i3 = tc * 36 + 32 + tbl
                        t.wait_ge(sE_vwt, idx + 1)
                        if i3 >= 2:
                            t.wait_ge(sE_trE, i3 - 1)
                        t.transpose(ptr2[i3 % 2][0:8, :], wtb[:, tbl * 8:(tbl + 1) * 8],
                                    ident[:, :]).then_inc(sE_tr, 1)
                    t.wait_ge(sE_trE, 36 * (tc + 1))
                    if tc >= 1:
                        t.wait_ge(sE_vu, 40 * tc)
                    for e in range(E):
                        if e >= 2:
                            t.wait_ge(sE_aw, 8 * tc + e - 1)
                        dst = pg if e % 2 == 0 else pu
                        mm = t.matmul(dst[:, :], sel[:, e * 128:(e + 1) * 128],
                                      wt_sb[:, :], start=True, stop=True)
                        mm.then_inc(sE_pw, 1)
                    for pe in range(NPE):
                        unit = tc * NPE + pe
                        w0 = tc * 40 + pe * 4
                        t.wait_ge(sE_wgu[unit % 2], 256 * (unit // 2 + 1))
                        for fc in range(4):
                            w = w0 + fc
                            if pe == 0 and fc == 0:
                                t.wait_ge(sE_aw, 8 * (tc + 1))
                            if w >= 1:
                                t.wait_ge(sE_as, w)
                            for kc in range(KC):
                                mm = t.matmul(
                                    pg[:, :],
                                    wgu[:, (unit % 2) * 8192 + kc * 512 + fc * 128:
                                        (unit % 2) * 8192 + kc * 512 + (fc + 1) * 128],
                                    h1t[:, par * 4096 + kc * 512:par * 4096 + (kc + 1) * 512],
                                    start=(kc == 0), stop=(kc == KC - 1))
                            mm.then_inc(sE_pg, 1)
                            if w >= 1:
                                t.wait_ge(sE_vu, w)
                            for kc in range(KC):
                                mm = t.matmul(
                                    pu[:, :],
                                    wgu[:, (unit % 2) * 8192 + 4096 + kc * 512 + fc * 128:
                                        (unit % 2) * 8192 + 4096 + kc * 512 + (fc + 1) * 128],
                                    h1t[:, par * 4096 + kc * 512:par * 4096 + (kc + 1) * 512],
                                    start=(kc == 0), stop=(kc == KC - 1))
                            mm.then_inc(sE_pu, 1)
                            if pe == NPE - 1 and fc == 3:
                                t.nop().then_inc(sE_petc, 1)
                    t.wait_ge(sE_va, 40 * (tc + 1))
                    if tc == 0:
                        t.wait_ge(sE_wwd, 16 * NPE * 4)
                    for tbl in range(4):
                        gtb = tc * 4 + tbl
                        y = yA if tbl % 2 == 0 else yB
                        if gtb >= 2:
                            t.wait_ge(sE_vt2, gtb - 1)
                        for pe in range(NPE):
                            for fc in range(4):
                                for h in range(2):
                                    mm = t.matmul(
                                        y[:, h * 512:(h + 1) * 512],
                                        a_sb[:, (pe * 4 + fc) * 512 + tbl * 128:
                                             (pe * 4 + fc) * 512 + (tbl + 1) * 128],
                                        wd_sb[:, (pe * 4 + fc) * 1024 + h * 512:
                                              (pe * 4 + fc) * 1024 + (h + 1) * 512],
                                        start=(pe == 0 and fc == 0),
                                        stop=(pe == NPE - 1 and fc == 3))
                        mm.then_inc(sE_py, 1)

            @blkE.vector
            def _(v):
                nvs = [0]

                def vself(instr):
                    nvs[0] += 1
                    instr.then_inc(sE_vs, 1)
                    v.wait_ge(sE_vs, nvs[0])

                for tc in range(NTCE):
                    par = tc % 2
                    for tbl in range(4):
                        for dc in range(8):
                            i3 = tc * 36 + tbl * 8 + dc
                            v.wait_ge(sE_tr, i3 + 1)
                            if tbl == 0 and dc == 0 and tc >= 2:
                                v.wait_ge(sE_petc, tc - 1)
                            v.tensor_copy(
                                h1t[:, par * 4096 + dc * 512 + tbl * 128:
                                    par * 4096 + dc * 512 + (tbl + 1) * 128],
                                ptr2[i3 % 2][:, :]).then_inc(sE_trE, 1)
                    for tbl in range(4):
                        idx = tc * 4 + tbl
                        v.wait_ge(sE_pl, idx + 1)
                        vself(v.tensor_copy(lg_sb[:, :], pg[:, 0:8]))
                        v.reduce_max(rr[:, 0:1], lg_sb[:, :], axis=AX.X,
                                     negate=True).then_inc(sE_vcp, 1)
                        v.wait_ge(sE_ae2, idx + 1)
                        vself(v.reciprocal(rr[:, 2:3], rr[:, 1:2]))
                        vself(v.tensor_scalar(rr[:, 8:16], el_sb[:, :], rr[:, 2:3], None,
                                              op0=OP.mult))
                        vself(v.tensor_reduce(rr[:, 3:4], rr[:, 8:16], axis=AX.X, op=OP.min))
                        vself(v.tensor_scalar(rr[:, 16:24], rr[:, 8:16], rr[:, 3:4], None,
                                              op0=OP.is_le))
                        vself(v.scalar_tensor_tensor(rr[:, 16:24], rr[:, 16:24], BIG,
                                                     rr[:, 8:16], op0=OP.mult, op1=OP.add))
                        vself(v.tensor_reduce(rr[:, 4:5], rr[:, 16:24], axis=AX.X, op=OP.min))
                        vself(v.tensor_scalar(rr[:, 24:32], rr[:, 16:24], rr[:, 4:5], None,
                                              op0=OP.is_le))
                        vself(v.scalar_tensor_tensor(rr[:, 24:32], rr[:, 24:32], BIG,
                                                     rr[:, 16:24], op0=OP.mult, op1=OP.add))
                        vself(v.tensor_reduce(rr[:, 5:6], rr[:, 24:32], axis=AX.X, op=OP.min))
                        vself(v.tensor_scalar(rr[:, 16:24], rr[:, 8:16], rr[:, 5:6], None,
                                              op0=OP.is_ge))
                        v.tensor_tensor(wtb[:, tbl * 8:(tbl + 1) * 8], rr[:, 8:16],
                                        rr[:, 16:24], op=OP.mult).then_inc(sE_vwt, 1)
                    for tbl in range(4):
                        i3 = tc * 36 + 32 + tbl
                        v.wait_ge(sE_tr, i3 + 1)
                        v.tensor_copy(wt_sb[:, tbl * 128:(tbl + 1) * 128],
                                      ptr2[i3 % 2][0:8, :]).then_inc(sE_trE, 1)
                    for pe in range(NPE):
                        for fc in range(4):
                            w = tc * 40 + pe * 4 + fc
                            v.wait_ge(sE_pu, w + 1)
                            if pe == 0 and fc == 0 and tc >= 1:
                                v.wait_ge(sE_py, 4 * tc)
                            if pe < 8:
                                v.wait_ge(sE_aw, 8 * tc + pe + 1)
                                v.tensor_tensor(ub_sb[:, :],
                                                pu[:, :], wtil[:, pe * 512:(pe + 1) * 512],
                                                op=OP.mult).then_inc(sE_vu, 1)
                            else:
                                v.tensor_copy(ub_sb[:, :],
                                              pu[:, :]).then_inc(sE_vu, 1)
                            v.wait_ge(sE_vu, w + 1)
                            v.wait_ge(sE_as, w + 1)
                            v.tensor_tensor(a_sb[:, (pe * 4 + fc) * 512:
                                                 (pe * 4 + fc + 1) * 512],
                                            sg_sb[:, (w % 2) * 512:(w % 2 + 1) * 512],
                                            ub_sb[:, :],
                                            op=OP.mult).then_inc(sE_va, 1)
                    for tbl in range(4):
                        gtb = tc * 4 + tbl
                        y = yA if tbl % 2 == 0 else yB
                        c0 = (gtb % 2) * 8
                        v.wait_ge(sE_py, gtb + 1)
                        v.wait_ge(sE_h1f, 16 * (gtb + 1))
                        if gtb >= 1:
                            v.wait_ge(sE_ae3, 3 * (gtb - 1) + 1)
                        v.tensor_add(t2s[:, :],
                                     y[:, :],
                                     h1f[:, (gtb % 2) * 1024:(gtb % 2 + 1) * 1024]
                                     ).then_inc(sE_vt2, 1)
                        v.wait_ge(sE_vt2, gtb + 1)
                        vself(v.reduce_sum(st3[:, c0:c0 + 1], t2s[:, :], axis=AX.X))
                        v.wait_ge(sE_ae3, 3 * gtb + 1)
                        vself(v.tensor_scalar(st3[:, c0 + 2:c0 + 3], st3[:, c0:c0 + 1],
                                              1.0 / D, None, op0=OP.mult))
                        vself(v.tensor_tensor(st3[:, c0 + 3:c0 + 4], st3[:, c0 + 2:c0 + 3],
                                              st3[:, c0 + 2:c0 + 3], op=OP.mult))
                        v.scalar_tensor_tensor(st3[:, c0 + 4:c0 + 5], st3[:, c0 + 1:c0 + 2],
                                               1.0 / D, st3[:, c0 + 3:c0 + 4],
                                               op0=OP.mult, op1=OP.subtract).then_inc(sE_vv3, 1)
                        v.wait_ge(sE_ae3, 3 * gtb + 3)
                        if gtb >= 1:
                            v.wait_ge(sE_so, 16 * gtb)
                        v.tensor_scalar(outs[:, :],
                                        t2s[:, :],
                                        st3[:, c0 + 2:c0 + 3], st3[:, c0 + 6:c0 + 7],
                                        op0=OP.subtract, op1=OP.mult).then_inc(sE_vo, 1)

            @blkE.scalar
            def _(sc):
                for tc in range(NTCE):
                    for tbl in range(4):
                        idx = tc * 4 + tbl
                        sc.wait_ge(sE_vcp, idx + 1)
                        sc.activation(el_sb[:, :], lg_sb[:, :], AF.Exp,
                                      bias=rr[:, 0:1], scale=1.0,
                                      accum_out=rr[:, 1:2]).then_inc(sE_ae2, 1)
                    for e in range(E):
                        sc.wait_ge(sE_pw, 8 * tc + e + 1)
                        if e == 0 and tc >= 1:
                            sc.wait_ge(sE_vu, 40 * tc)
                        src = pg if e % 2 == 0 else pu
                        sc.activation(wtil[:, e * 512:(e + 1) * 512], src[:, :],
                                      AF.Copy).then_inc(sE_aw, 1)
                    for pe in range(NPE):
                        for fc in range(4):
                            w = tc * 40 + pe * 4 + fc
                            sc.wait_ge(sE_pg, w + 1)
                            if w >= 2:
                                sc.wait_ge(sE_va, w - 1)
                            sc.activation(sg_sb[:, (w % 2) * 512:(w % 2 + 1) * 512],
                                          pg[:, :], AF.Silu).then_inc(sE_as, 1)
                    for tbl in range(4):
                        gtb = tc * 4 + tbl
                        c0 = (gtb % 2) * 8
                        sc.wait_ge(sE_vt2, gtb + 1)
                        if gtb >= 1:
                            sc.wait_ge(sE_so, 16 * gtb)
                        sc.activation(outs[:, :], t2s[:, :], AF.Square,
                                      accum_out=st3[:, c0 + 1:c0 + 2]).then_inc(sE_ae3, 1)
                        sc.wait_ge(sE_vv3, gtb + 1)
                        sc.activation(st3[:, c0 + 5:c0 + 6], st3[:, c0 + 4:c0 + 5], AF.Ln,
                                      bias=eps_t[:, 0:1], scale=1.0).then_inc(sE_ae3, 1)
                        sc.wait_ge(sE_ae3, 3 * gtb + 2)
                        sc.activation(st3[:, c0 + 6:c0 + 7], st3[:, c0 + 5:c0 + 6], AF.Exp,
                                      scale=-0.5).then_inc(sE_ae3, 1)

            @blkE.sync
            def _(s):
                for tc in range(NTCE):
                    for tbl in range(4):
                        gtb = tc * 4 + tbl
                        if gtb >= 2:
                            s.wait_ge(sE_vt2, gtb - 1)
                        s.dma_start(h1f[:, (gtb % 2) * 1024:(gtb % 2 + 1) * 1024],
                                    h1_h[gtb * 128:(gtb + 1) * 128, :]).then_inc(sE_h1f, 16)
                    for tbl in range(4):
                        gtb = tc * 4 + tbl
                        s.wait_ge(sE_vo, gtb + 1)
                        s.dma_start(out_h[gtb * 128:(gtb + 1) * 128, :],
                                    outs[:, :]).then_inc(sE_so, 16)
                if dbg:
                    s.wait_ge(sE_vwt, 64)
                s.wait_ge(sE_so, 16 * 64)

            if dbg:
                pass

    es.close()
    return nc


def make_consts():
    ident = np.eye(128, dtype=ml_dtypes.bfloat16)
    p = np.arange(128)[:, None]
    m = np.arange(896)[None, :]
    maskm = np.where(m >= p + 384, 0.0, NEGINF).astype(np.float32)
    sel = np.zeros((E, E * 128), dtype=ml_dtypes.bfloat16)
    for e in range(E):
        sel[e, e * 128:(e + 1) * 128] = 1.0
    return ident, maskm, sel


def kernel(**inputs):
    f = lambda k: np.ascontiguousarray(np.asarray(inputs[k], dtype=np.float32))
    x = f("L0_attn_minibatch")
    wq, wk, wv = f("L0_attn_Wq"), f("L0_attn_Wk"), f("L0_attn_Wv")
    bq = f("L0_attn_bq").reshape(D, 1)
    bk = f("L0_attn_bk").reshape(D, 1)
    bv = f("L0_attn_bv").reshape(1, D)
    rw = f("L0_ffn_router_w")
    wg = f("L0_ffn_w_gate").reshape(E * D, FF)
    wu = f("L0_ffn_w_up").reshape(E * D, FF)
    wd = f("L0_ffn_w_down").reshape(E * FF, D)
    wsg, wsu, wsd = f("L0_ffn_ws_gate"), f("L0_ffn_ws_up"), f("L0_ffn_ws_down")
    ident, maskm, sel = make_consts()

    nc = build_program()
    base = dict(wq=wq, wk=wk, wv=wv, bq=bq, bk=bk, bv=bv, rw=rw, wg=wg, wu=wu,
                wd=wd, wsg=wsg, wsu=wsu, wsd=wsd, ident=ident, maskm=maskm, sel=sel)
    in_maps = [dict(base, x=np.ascontiguousarray(x[b])) for b in range(8)]
    res = run_bass_kernel_spmd(nc, in_maps, core_ids=list(range(8)))
    return np.stack([np.asarray(r["out"]) for r in res.results], axis=0)


# version 09
# speedup vs baseline: 1.0943x; 1.0943x over previous
import numpy as np
import ml_dtypes
import concourse.bass as bass
import concourse.mybir as mybir
from concourse.bass_utils import run_bass_kernel_spmd
from concourse.alu_op_type import AluOpType as OP
from contextlib import ExitStack

F32 = mybir.dt.float32
BF16 = mybir.dt.bfloat16
AF = mybir.ActivationFunctionType
AX = mybir.AxisListType

T, D, E, FF, FS = 8192, 1024, 8, 512, 1024
KC = 8
NTB = 64
NTCA = 16
NQT1 = 16
NKB = 64
QB2, NQB2 = 256, 32
TCE, NTCE = 512, 16
NPE = 10
NEGINF = -1.0e30
BIG = 1.0e9
EPS = 1e-5


def n_tile_b(qt, kb):
    return 2 * qt * (qt + 1) + kb


def n_tile_c(qb, kb):
    return qb * (qb + 1) + kb


NB_TOT = n_tile_b(15, 63 + 1 - 60) + 64
NC_TOT = 32 * 33


def build_program(phases=("A", "B", "C", "E"), dbg=False):
    nc = bass.Bass("TRN2", target_bir_lowering=False)

    x_h = nc.dram_tensor("x", [T, D], F32, kind="ExternalInput").ap()
    wq_h = nc.dram_tensor("wq", [D, D], F32, kind="ExternalInput").ap()
    wk_h = nc.dram_tensor("wk", [D, D], F32, kind="ExternalInput").ap()
    wv_h = nc.dram_tensor("wv", [D, D], F32, kind="ExternalInput").ap()
    bq_h = nc.dram_tensor("bq", [D, 1], F32, kind="ExternalInput").ap()
    bk_h = nc.dram_tensor("bk", [D, 1], F32, kind="ExternalInput").ap()
    bv_h = nc.dram_tensor("bv", [1, D], F32, kind="ExternalInput").ap()
    rw_h = nc.dram_tensor("rw", [D, E], F32, kind="ExternalInput").ap()
    wg_h = nc.dram_tensor("wg", [E * D, FF], F32, kind="ExternalInput").ap()
    wu_h = nc.dram_tensor("wu", [E * D, FF], F32, kind="ExternalInput").ap()
    wd_h = nc.dram_tensor("wd", [E * FF, D], F32, kind="ExternalInput").ap()
    wsg_h = nc.dram_tensor("wsg", [D, FS], F32, kind="ExternalInput").ap()
    wsu_h = nc.dram_tensor("wsu", [D, FS], F32, kind="ExternalInput").ap()
    wsd_h = nc.dram_tensor("wsd", [FS, D], F32, kind="ExternalInput").ap()
    ident_h = nc.dram_tensor("ident", [128, 128], BF16, kind="ExternalInput").ap()
    maskm_h = nc.dram_tensor("maskm", [128, 896], F32, kind="ExternalInput").ap()
    sel_h = nc.dram_tensor("sel", [E, E * 128], BF16, kind="ExternalInput").ap()

    out_h = nc.dram_tensor("out", [T, D], F32, kind="ExternalOutput").ap()

    sk = "ExternalOutput" if dbg else "Internal"
    qt_h = nc.dram_tensor("qt_s", [D, T], BF16, kind=sk).ap()
    kt_h = nc.dram_tensor("kt_s", [D, T], BF16, kind=sk).ap()
    v_h = nc.dram_tensor("v_s", [T, D], BF16, kind=sk).ap()
    h1_h = nc.dram_tensor("h1_s", [T, D], F32, kind=sk).ap()
    if dbg:
        dbias_h = nc.dram_tensor("d_bias", [128, NKB], F32, kind="ExternalOutput").ap()
        dmn_h = nc.dram_tensor("d_mn", [128, NKB * 16], F32, kind="ExternalOutput").ap()
        de_h = nc.dram_tensor("d_e", [128, NKB * 16], F32, kind="ExternalOutput").ap()
        ds_h = nc.dram_tensor("d_s", [128, NKB], F32, kind="ExternalOutput").ap()

    es = ExitStack()
    sem = lambda name: es.enter_context(nc.semaphore(name))

    ident = es.enter_context(nc.sbuf_tensor("ident_s", [128, 128], BF16))
    sel = es.enter_context(nc.sbuf_tensor("sel_s", [E, E * 128], BF16))
    ones1 = es.enter_context(nc.sbuf_tensor("ones1", [1, 128], BF16))
    eps_t = es.enter_context(nc.sbuf_tensor("eps_t", [128, 1], F32))
    bias_st = es.enter_context(nc.sbuf_tensor("bias_st", [128, NKB], F32))

    sA_c = sem("sA_c")

    if "A" in phases:
        with ExitStack() as ps:
            xa = ps.enter_context(nc.sbuf_tensor("xa", [128, 8 * 1024], BF16))
            wq_s = ps.enter_context(nc.sbuf_tensor("wq_s", [128, 8 * 1024], BF16))
            wk_s = ps.enter_context(nc.sbuf_tensor("wk_s", [128, 8 * 1024], BF16))
            wv_s = ps.enter_context(nc.sbuf_tensor("wv_s", [128, 8 * 1024], BF16))
            bq_s = ps.enter_context(nc.sbuf_tensor("bq_s", [128, 8], F32))
            bk_s = ps.enter_context(nc.sbuf_tensor("bk_s", [128, 8], F32))
            bv_s = ps.enter_context(nc.sbuf_tensor("bv_s", [1, 1024], BF16))
            xt = ps.enter_context(nc.sbuf_tensor("xt", [128, 2 * 4096], BF16))
            qs = ps.enter_context(nc.sbuf_tensor("qs", [128, 2 * 512], BF16))
            ks = ps.enter_context(nc.sbuf_tensor("ks", [128, 2 * 512], BF16))
            vs = ps.enter_context(nc.sbuf_tensor("vs", [128, 2 * 1024], BF16))
            pq = [ps.enter_context(nc.psum_tensor(f"pq{i}", [128, 512], F32)) for i in range(2)]
            pk = [ps.enter_context(nc.psum_tensor(f"pk{i}", [128, 512], F32)) for i in range(2)]
            pv = ps.enter_context(nc.psum_tensor("pv", [128, 1024], F32))
            ptr = [ps.enter_context(nc.psum_tensor(f"ptr{i}", [128, 128], BF16)) for i in range(2)]

            sA_w = sem("sA_w")
            sA_ms = sem("sA_ms")
            sA_x = [sem(f"sA_x{s}") for s in range(8)]
            sA_tr = sem("sA_tr")
            sA_trE = sem("sA_trE")
            sA_mm = sem("sA_mm")
            sA_pq = sem("sA_pq")
            sA_pk = sem("sA_pk")
            sA_pv = sem("sA_pv")
            sA_evq = sem("sA_evq")
            sA_evk = sem("sA_evk")
            sA_evv = sem("sA_evv")
            sA_stq = [sem(f"sA_stq{s}") for s in range(2)]
            sA_stk = [sem(f"sA_stk{s}") for s in range(2)]
            sA_stv = [sem(f"sA_stv{s}") for s in range(2)]

            blkA = ps.enter_context(nc.Block())

            @blkA.gpsimd
            def _(g):
                g.dma_start(ident[:, :], ident_h[:, :]).then_inc(sA_c, 16)
                g.dma_start(sel[:, :], sel_h[:, :]).then_inc(sA_c, 16)
                for t_i in range(8):
                    g.dma_start(xa[:, (t_i % 8) * 1024:(t_i % 8 + 1) * 1024],
                                x_h[t_i * 128:(t_i + 1) * 128, :]).then_inc(sA_x[t_i % 8], 16)
                for kc in range(KC):
                    g.dma_start(wq_s[:, kc * 1024:(kc + 1) * 1024],
                                wq_h[kc * 128:(kc + 1) * 128, :]).then_inc(sA_w, 16)
                for kc in range(KC):
                    g.dma_start(wk_s[:, kc * 1024:(kc + 1) * 1024],
                                wk_h[kc * 128:(kc + 1) * 128, :]).then_inc(sA_w, 16)
                for kc in range(KC):
                    g.dma_start(wv_s[:, kc * 1024:(kc + 1) * 1024],
                                wv_h[kc * 128:(kc + 1) * 128, :]).then_inc(sA_w, 16)
                for dc in range(8):
                    g.dma_start(bq_s[:, dc:dc + 1],
                                bq_h[dc * 128:(dc + 1) * 128, 0:1]).then_inc(sA_w, 16)
                for dc in range(8):
                    g.dma_start(bk_s[:, dc:dc + 1],
                                bk_h[dc * 128:(dc + 1) * 128, 0:1]).then_inc(sA_w, 16)
                g.dma_start(bv_s[:, :], bv_h[:, :]).then_inc(sA_w, 16)
                g.memset(ones1[:, :], 1.0).then_inc(sA_ms, 1)
                g.memset(eps_t[:, :], EPS).then_inc(sA_ms, 1)
                for t_i in range(8, NTB):
                    g.wait_ge(sA_tr, 8 * (t_i - 8) + 8)
                    g.dma_start(xa[:, (t_i % 8) * 1024:(t_i % 8 + 1) * 1024],
                                x_h[t_i * 128:(t_i + 1) * 128, :]).then_inc(sA_x[t_i % 8], 16)

            W_TOT = 41 * 16

            def emit_tr(t, tc):
                for tbl in range(4):
                    t_i = tc * 4 + tbl
                    t.wait_ge(sA_x[t_i % 8], 16 * (t_i // 8 + 1))
                    for dc in range(8):
                        i = t_i * 8 + dc
                        if i >= 2:
                            t.wait_ge(sA_trE, i - 1)
                        t.transpose(
                            ptr[i % 2][:, :],
                            xa[:, (t_i % 8) * 1024 + dc * 128:(t_i % 8) * 1024 + (dc + 1) * 128],
                            ident[:, :]).then_inc(sA_tr, 1)

            @blkA.tensor
            def _(t):
                t.wait_ge(sA_c, 32)
                emit_tr(t, 0)
                for tc in range(NTCA):
                    par = tc % 2
                    if tc + 1 < NTCA:
                        emit_tr(t, tc + 1)
                    t.wait_ge(sA_trE, 32 * (tc + 1))
                    if tc == 0:
                        t.wait_ge(sA_w, W_TOT)
                        t.wait_ge(sA_ms, 2)

                    def q_group(dc, which):
                        ws, pp, sev, spp = (wq_s, pq, sA_evq, sA_pq) if which == 0 else (
                            wk_s, pk, sA_evk, sA_pk)
                        g = tc * 8 + dc
                        if g >= 2:
                            t.wait_ge(sev, g - 1)
                        for kc in range(KC):
                            mm = t.matmul(
                                pp[g % 2][:, :],
                                ws[:, kc * 1024 + dc * 128:kc * 1024 + (dc + 1) * 128],
                                xt[:, par * 4096 + kc * 512:par * 4096 + (kc + 1) * 512],
                                start=(kc == 0), stop=(kc == KC - 1))
                        mm.then_inc(spp, 1)
                        return mm

                    def v_group(tbl):
                        vi = tc * 4 + tbl
                        if vi >= 1:
                            t.wait_ge(sA_evv, vi)
                        for kc in range(KC):
                            for h in range(2):
                                t.matmul(
                                    pv[:, h * 512:(h + 1) * 512],
                                    xt[:, par * 4096 + kc * 512 + tbl * 128:
                                       par * 4096 + kc * 512 + (tbl + 1) * 128],
                                    wv_s[:, kc * 1024 + h * 512:kc * 1024 + (h + 1) * 512],
                                    start=(kc == 0), stop=False)
                        for h in range(2):
                            mm = t.matmul(pv[:, h * 512:(h + 1) * 512], ones1[:, :],
                                          bv_s[:, h * 512:(h + 1) * 512],
                                          start=False, stop=True)
                        mm.then_inc(sA_pv, 1)
                        return mm

                    last = None
                    for dc in range(4):
                        last = q_group(dc, 0)
                    last = v_group(0)
                    for dc in range(4, 8):
                        last = q_group(dc, 0)
                    last = v_group(1)
                    for dc in range(4):
                        last = q_group(dc, 1)
                    last = v_group(2)
                    for dc in range(4, 8):
                        last = q_group(dc, 1)
                    last = v_group(3)
                    t.nop().then_inc(sA_mm, 1)

            @blkA.vector
            def _(v):
                for tc in range(NTCA + 1):
                    if tc >= NTCA:
                        break
                    for tbl in range(4):
                        t_i = tc * 4 + tbl
                        for dc in range(8):
                            i = t_i * 8 + dc
                            v.wait_ge(sA_tr, i + 1)
                            if tbl == 0 and dc == 0 and tc >= 2:
                                v.wait_ge(sA_mm, tc - 1)
                            v.tensor_copy(
                                xt[:, (tc % 2) * 4096 + dc * 512 + tbl * 128:
                                   (tc % 2) * 4096 + dc * 512 + (tbl + 1) * 128],
                                ptr[i % 2][:, :]).then_inc(sA_trE, 1)

            @blkA.scalar
            def _(sc):
                for tc in range(NTCA):
                    def q_evac(dc, which):
                        ws_stage, pp, bb, spp, sev, sst = (
                            (qs, pq, bq_s, sA_pq, sA_evq, sA_stq) if which == 0
                            else (ks, pk, bk_s, sA_pk, sA_evk, sA_stk))
                        g = tc * 8 + dc
                        sc.wait_ge(spp, g + 1)
                        if g >= 2:
                            sc.wait_ge(sst[g % 2], 16 * (g // 2))
                        sc.activation(ws_stage[:, (g % 2) * 512:(g % 2 + 1) * 512],
                                      pp[g % 2][:, :], AF.Identity,
                                      bias=bb[:, dc:dc + 1], scale=1.0).then_inc(sev, 1)

                    def v_evac(tbl):
                        vi = tc * 4 + tbl
                        sc.wait_ge(sA_pv, vi + 1)
                        if vi >= 2:
                            sc.wait_ge(sA_stv[vi % 2], 16 * (vi // 2))
                        sc.activation(vs[:, (vi % 2) * 1024:(vi % 2 + 1) * 1024],
                                      pv[:, :], AF.Copy).then_inc(sA_evv, 1)

                    for dc in range(4):
                        q_evac(dc, 0)
                    v_evac(0)
                    for dc in range(4, 8):
                        q_evac(dc, 0)
                    v_evac(1)
                    for dc in range(4):
                        q_evac(dc, 1)
                    v_evac(2)
                    for dc in range(4, 8):
                        q_evac(dc, 1)
                    v_evac(3)

            @blkA.sync
            def _(s):
                for tc in range(NTCA):
                    def q_store(dc, which):
                        stage, dst, sev, sst = (qs, qt_h, sA_evq, sA_stq) if which == 0 else (
                            ks, kt_h, sA_evk, sA_stk)
                        g = tc * 8 + dc
                        s.wait_ge(sev, g + 1)
                        s.dma_start(dst[dc * 128:(dc + 1) * 128, tc * 512:(tc + 1) * 512],
                                    stage[:, (g % 2) * 512:(g % 2 + 1) * 512]).then_inc(sst[g % 2], 16)

                    def v_store(tbl):
                        vi = tc * 4 + tbl
                        s.wait_ge(sA_evv, vi + 1)
                        s.dma_start(v_h[vi * 128:(vi + 1) * 128, :],
                                    vs[:, (vi % 2) * 1024:(vi % 2 + 1) * 1024]).then_inc(sA_stv[vi % 2], 16)

                    for dc in range(4):
                        q_store(dc, 0)
                    v_store(0)
                    for dc in range(4, 8):
                        q_store(dc, 0)
                    v_store(1)
                    for dc in range(4):
                        q_store(dc, 1)
                    v_store(2)
                    for dc in range(4, 8):
                        q_store(dc, 1)
                    v_store(3)
                for s2 in range(2):
                    s.wait_ge(sA_stq[s2], 16 * 64)
                    s.wait_ge(sA_stk[s2], 16 * 64)
                    s.wait_ge(sA_stv[s2], 16 * 32)

    if "B" in phases or "C" in phases:
        ps_bc = ExitStack()
        kt_sb = ps_bc.enter_context(nc.sbuf_tensor("kt_sb", [128, 8 * 8192], BF16))
        maskm = ps_bc.enter_context(nc.sbuf_tensor("maskm_s", [128, 896], F32))
        sBC_kt = sem("sBC_kt")
        sBC_mk = sem("sBC_mk")

    if "B" in phases:
        with ExitStack() as ps:
            qt1 = ps.enter_context(nc.sbuf_tensor("qt1", [128, 2 * 4096], BF16))
            mneg_st = ps.enter_context(nc.sbuf_tensor("mneg_st", [128, NKB * 16], F32))
            e_st = ps.enter_context(nc.sbuf_tensor("e_st", [128, NKB * 16], F32))
            tmp_st = ps.enter_context(nc.sbuf_tensor("tmp_st", [128, NKB * 16], F32))
            sc1 = ps.enter_context(nc.sbuf_tensor("sc1", [128, 2 * 512], BF16))
            mnegf_st = ps.enter_context(nc.sbuf_tensor("mnegf_st", [128, NKB], F32))
            s_st = ps.enter_context(nc.sbuf_tensor("s_st", [128, NKB], F32))
            lns_st = ps.enter_context(nc.sbuf_tensor("lns_st", [128, NKB], F32))
            ps1 = [ps.enter_context(nc.psum_tensor(f"ps1_{i}", [128, 512], F32)) for i in range(2)]

            sB_q = [sem(f"sB_q{s}") for s in range(2)]
            sB_db = sem("sB_db")
            sB_pe = sem("sB_pe")
            sB_v = sem("sB_v")
            sB_a = sem("sB_a")
            sB_vf = sem("sB_vf")
            sB_af = sem("sB_af")
            sB_vf2 = sem("sB_vf2")
            sB_af2 = sem("sB_af2")
            sB_fin = sem("sB_fin")
            sB_vs = sem("sB_vs")

            blkB = ps.enter_context(nc.Block())

            @blkB.sync
            def _(s):
                s.dma_start(maskm[:, :], maskm_h[:, :]).then_inc(sBC_mk, 16)
                for kc in range(KC):
                    s.dma_start(kt_sb[:, kc * 8192:(kc + 1) * 8192],
                                kt_h[kc * 128:(kc + 1) * 128, :]).then_inc(sBC_kt, 16)
                for qt in range(NQT1):
                    if qt >= 2:
                        s.wait_ge(sB_pe, n_tile_b(qt - 1, 0))
                    for kc in range(KC):
                        s.dma_start(qt1[:, (qt % 2) * 4096 + kc * 512:
                                        (qt % 2) * 4096 + (kc + 1) * 512],
                                    qt_h[kc * 128:(kc + 1) * 128,
                                         qt * 512:(qt + 1) * 512]).then_inc(sB_q[qt % 2], 16)
                if dbg and "C" not in phases:
                    s.wait_ge(sB_fin, NKB)
                    s.dma_start(dbias_h[:, :], bias_st[:, :]).then_inc(sB_db, 16)
                    s.dma_start(dmn_h[:, :], mneg_st[:, :]).then_inc(sB_db, 16)
                    s.dma_start(de_h[:, :], e_st[:, :]).then_inc(sB_db, 16)
                    s.dma_start(ds_h[:, :], s_st[:, :]).then_inc(sB_db, 16)
                    s.wait_ge(sB_db, 64)

            @blkB.tensor
            def _(t):
                t.wait_ge(sBC_kt, 16 * KC)
                for qt in range(NQT1):
                    t.wait_ge(sB_q[qt % 2], 128 * (qt // 2 + 1))
                    for kb in range(4 * qt + 4):
                        n = n_tile_b(qt, kb)
                        if n >= 2:
                            t.wait_ge(sB_a, n - 1)
                        for kc in range(KC):
                            mm = t.matmul(
                                ps1[n % 2][:, :],
                                kt_sb[:, kc * 8192 + kb * 128:kc * 8192 + (kb + 1) * 128],
                                qt1[:, (qt % 2) * 4096 + kc * 512:(qt % 2) * 4096 + (kc + 1) * 512],
                                start=(kc == 0), stop=(kc == KC - 1))
                        mm.then_inc(sB_pe, 1)

            @blkB.vector
            def _(v):
                nvs = [0]

                def vself(instr):
                    nvs[0] += 1
                    instr.then_inc(sB_vs, 1)
                    v.wait_ge(sB_vs, nvs[0])

                v.wait_ge(sBC_mk, 16)
                for qt in range(NQT1):
                    for kb in range(4 * qt + 4):
                        n = n_tile_b(qt, kb)
                        v.wait_ge(sB_pe, n + 1)
                        if kb >= 4 * qt:
                            off = kb * 128 - qt * 512
                            sm = 384 - off
                            vself(v.tensor_add(ps1[n % 2][:, :], ps1[n % 2][:, :],
                                               maskm[:, sm:sm + 512]))
                        v.reduce_max(mneg_st[:, kb * 16 + qt:kb * 16 + qt + 1],
                                     ps1[n % 2][:, :], axis=AX.X,
                                     negate=True).then_inc(sB_v, 1)
                for kb in range(NKB):
                    qt0 = kb // 4
                    v.tensor_reduce(mnegf_st[:, kb:kb + 1],
                                    mneg_st[:, kb * 16 + qt0:kb * 16 + 16],
                                    axis=AX.X, op=OP.min).then_inc(sB_vf, 1)
                for kb in range(NKB):
                    qt0 = kb // 4
                    v.wait_ge(sB_af, kb + 1)
                    vself(v.tensor_tensor(tmp_st[:, kb * 16 + qt0:kb * 16 + 16],
                                          tmp_st[:, kb * 16 + qt0:kb * 16 + 16],
                                          e_st[:, kb * 16 + qt0:kb * 16 + 16], op=OP.mult))
                    v.reduce_sum(s_st[:, kb:kb + 1],
                                 tmp_st[:, kb * 16 + qt0:kb * 16 + 16],
                                 axis=AX.X).then_inc(sB_vf2, 1)
                for kb in range(NKB):
                    v.wait_ge(sB_af2, kb + 1)
                    v.tensor_sub(bias_st[:, kb:kb + 1], mnegf_st[:, kb:kb + 1],
                                 lns_st[:, kb:kb + 1]).then_inc(sB_fin, 1)

            @blkB.scalar
            def _(sc):
                for qt in range(NQT1):
                    for kb in range(4 * qt + 4):
                        n = n_tile_b(qt, kb)
                        sc.wait_ge(sB_v, n + 1)
                        sc.activation(sc1[:, (n % 2) * 512:(n % 2 + 1) * 512],
                                      ps1[n % 2][:, :], AF.Exp,
                                      bias=mneg_st[:, kb * 16 + qt:kb * 16 + qt + 1],
                                      scale=1.0,
                                      accum_out=e_st[:, kb * 16 + qt:kb * 16 + qt + 1]
                                      ).then_inc(sB_a, 1)
                for kb in range(NKB):
                    qt0 = kb // 4
                    sc.wait_ge(sB_vf, kb + 1)
                    sc.activation(tmp_st[:, kb * 16 + qt0:kb * 16 + 16],
                                  mneg_st[:, kb * 16 + qt0:kb * 16 + 16], AF.Exp,
                                  bias=mnegf_st[:, kb:kb + 1],
                                  scale=-1.0).then_inc(sB_af, 1)
                for kb in range(NKB):
                    sc.wait_ge(sB_vf2, kb + 1)
                    sc.activation(lns_st[:, kb:kb + 1], s_st[:, kb:kb + 1], AF.Ln,
                                  scale=32.0).then_inc(sB_af2, 1)

    if "C" in phases:
        with ExitStack() as ps:
            qt2 = ps.enter_context(nc.sbuf_tensor("qt2", [128, 2 * 2048], BF16))
            v2 = ps.enter_context(nc.sbuf_tensor("v2", [128, 4 * 1024], BF16))
            p2 = ps.enter_context(nc.sbuf_tensor("p2", [128, 2 * 256], BF16))
            x2 = ps.enter_context(nc.sbuf_tensor("x2", [128, 2 * 2048], F32))
            t1s = ps.enter_context(nc.sbuf_tensor("t1s", [128, 2 * 1024], F32))
            h1s = ps.enter_context(nc.sbuf_tensor("h1s", [128, 2 * 1024], F32))
            sq2 = ps.enter_context(nc.sbuf_tensor("sq2", [128, 1024], BF16))
            st2 = ps.enter_context(nc.sbuf_tensor("st2", [128, 16], F32))
            ps2 = [ps.enter_context(nc.psum_tensor(f"ps2_{i}", [128, 256], F32)) for i in range(2)]
            pr = [ps.enter_context(nc.psum_tensor(f"pr{i}", [128, 1024], F32)) for i in range(2)]

            sC_q = [sem(f"sC_q{s}") for s in range(2)]
            sC_x2 = [sem(f"sC_x2{s}") for s in range(4)]
            sC_v2 = [sem(f"sC_v2{s}") for s in range(4)]
            sC_ps = sem("sC_ps")
            sC_pr = sem("sC_pr")
            sC_vm = sem("sC_vm")
            sC_a = sem("sC_a")
            sC_vt = sem("sC_vt")
            sC_ae = sem("sC_ae")
            sC_vv = sem("sC_vv")
            sC_va = sem("sC_va")
            sC_sh = [sem(f"sC_sh{s}") for s in range(2)]
            sC_db = sem("sC_db")
            sC_vs = sem("sC_vs")

            blkC = ps.enter_context(nc.Block())

            @blkC.sync
            def _(s):
                if "B" not in phases:
                    s.dma_start(maskm[:, :], maskm_h[:, :]).then_inc(sBC_mk, 16)
                    for kc in range(KC):
                        s.dma_start(kt_sb[:, kc * 8192:(kc + 1) * 8192],
                                    kt_h[kc * 128:(kc + 1) * 128, :]).then_inc(sBC_kt, 16)
                if dbg:
                    s.dma_start(dbias_h[:, :], bias_st[:, :]).then_inc(sC_db, 16)

                def loads(qb):
                    if qb >= 2:
                        s.wait_ge(sC_pr, n_tile_c(qb - 1, 0))
                    for kc in range(KC):
                        s.dma_start(qt2[:, (qb % 2) * 2048 + kc * 256:
                                        (qb % 2) * 2048 + (kc + 1) * 256],
                                    qt_h[kc * 128:(kc + 1) * 128,
                                         qb * 256:(qb + 1) * 256]).then_inc(sC_q[qb % 2], 16)
                    for ic in range(2):
                        u = 2 * qb + ic
                        if qb >= 2:
                            s.wait_ge(sC_vt, 2 * (qb - 1))
                        s.dma_start(x2[:, (qb % 2) * 2048 + ic * 1024:
                                       (qb % 2) * 2048 + (ic + 1) * 1024],
                                    x_h[u * 128:(u + 1) * 128, :]).then_inc(sC_x2[(qb % 2) * 2 + ic], 16)

                def stores(qb):
                    for ic in range(2):
                        u = 2 * qb + ic
                        s.wait_ge(sC_va, u + 1)
                        s.dma_start(h1_h[u * 128:(u + 1) * 128, :],
                                    h1s[:, (u % 2) * 1024:(u % 2 + 1) * 1024]
                                    ).then_inc(sC_sh[u % 2], 16)

                loads(0)
                for qb in range(1, NQB2):
                    loads(qb)
                    stores(qb - 1)
                stores(NQB2 - 1)
                s.wait_ge(sC_sh[0], 16 * NQB2)
                s.wait_ge(sC_sh[1], 16 * NQB2)
                if dbg:
                    s.wait_ge(sC_db, 16)

            @blkC.gpsimd
            def _(g):
                for qb in range(NQB2):
                    for kb in range(2 * qb + 2):
                        n = n_tile_c(qb, kb)
                        if n >= 4:
                            g.wait_ge(sC_pr, n - 3)
                        g.dma_start(v2[:, (n % 4) * 1024:(n % 4 + 1) * 1024],
                                    v_h[kb * 128:(kb + 1) * 128, :]).then_inc(sC_v2[n % 4], 16)

            @blkC.tensor
            def _(t):
                t.wait_ge(sBC_kt, 16 * KC)

                def s_tile(qb, kb):
                    n = n_tile_c(qb, kb)
                    if n >= 2:
                        t.wait_ge(sC_a, n - 1)
                    for kc in range(KC):
                        mm = t.matmul(
                            ps2[n % 2][:, :],
                            kt_sb[:, kc * 8192 + kb * 128:kc * 8192 + (kb + 1) * 128],
                            qt2[:, (qb % 2) * 2048 + kc * 256:(qb % 2) * 2048 + (kc + 1) * 256],
                            start=(kc == 0), stop=(kc == KC - 1))
                    mm.then_inc(sC_ps, 1)

                def r_tile(qb, kb, nkb):
                    n = n_tile_c(qb, kb)
                    t.wait_ge(sC_a, n + 1)
                    t.wait_ge(sC_v2[n % 4], 16 * (n // 4 + 1))
                    if kb == 0:
                        t.wait_ge(sC_vt, 2 * qb)
                    for ic in range(2):
                        for h in range(2):
                            mm = t.matmul(
                                pr[ic][:, h * 512:(h + 1) * 512],
                                p2[:, (n % 2) * 256 + ic * 128:(n % 2) * 256 + (ic + 1) * 128],
                                v2[:, (n % 4) * 1024 + h * 512:(n % 4) * 1024 + (h + 1) * 512],
                                start=(kb == 0), stop=(kb == nkb - 1))
                    mm.then_inc(sC_pr, 1)

                for qb in range(NQB2):
                    nkb = 2 * qb + 2
                    t.wait_ge(sC_q[qb % 2], 128 * (qb // 2 + 1))
                    s_tile(qb, 0)
                    s_tile(qb, 1)
                    for kb in range(nkb - 2):
                        r_tile(qb, kb, nkb)
                        s_tile(qb, kb + 2)
                    r_tile(qb, nkb - 2, nkb)
                    r_tile(qb, nkb - 1, nkb)

            @blkC.vector
            def _(v):
                nvs = [0]

                def vself(instr):
                    nvs[0] += 1
                    instr.then_inc(sC_vs, 1)
                    v.wait_ge(sC_vs, nvs[0])

                v.wait_ge(sBC_mk, 16)
                for qb in range(NQB2):
                    nkb = 2 * qb + 2
                    for which, kb in enumerate((nkb - 2, nkb - 1)):
                        n = n_tile_c(qb, kb)
                        v.wait_ge(sC_ps, n + 1)
                        off = kb * 128 - qb * 256
                        sm = 384 - off
                        v.tensor_add(ps2[n % 2][:, :], ps2[n % 2][:, :],
                                     maskm[:, sm:sm + 256]).then_inc(sC_vm, 1)
                    for ic in range(2):
                        u = 2 * qb + ic
                        v.wait_ge(sC_pr, n_tile_c(qb, 0) + nkb)
                        v.wait_ge(sC_x2[(qb % 2) * 2 + ic], 16 * (qb // 2 + 1))
                        if u >= 4:
                            v.wait_ge(sC_ae, 3 * (u - 2) + 1)
                        v.scalar_tensor_tensor(
                            t1s[:, ic * 1024:(ic + 1) * 1024],
                            x2[:, (qb % 2) * 2048 + ic * 1024:(qb % 2) * 2048 + (ic + 1) * 1024],
                            2.0, pr[ic][:, :], op0=OP.mult, op1=OP.add).then_inc(sC_vt, 1)
                        v.wait_ge(sC_vt, u + 1)
                        c0 = ic * 8
                        vself(v.reduce_sum(st2[:, c0:c0 + 1],
                                           t1s[:, ic * 1024:(ic + 1) * 1024], axis=AX.X))
                        v.wait_ge(sC_ae, 3 * u + 1)
                        vself(v.tensor_scalar(st2[:, c0 + 2:c0 + 3], st2[:, c0:c0 + 1],
                                              1.0 / D, None, op0=OP.mult))
                        vself(v.tensor_tensor(st2[:, c0 + 3:c0 + 4], st2[:, c0 + 2:c0 + 3],
                                              st2[:, c0 + 2:c0 + 3], op=OP.mult))
                        v.scalar_tensor_tensor(st2[:, c0 + 4:c0 + 5], st2[:, c0 + 1:c0 + 2],
                                               1.0 / D, st2[:, c0 + 3:c0 + 4],
                                               op0=OP.mult, op1=OP.subtract).then_inc(sC_vv, 1)
                        v.wait_ge(sC_ae, 3 * u + 3)
                        if u >= 2:
                            v.wait_ge(sC_sh[u % 2], 16 * (u // 2))
                        v.tensor_scalar(h1s[:, (u % 2) * 1024:(u % 2 + 1) * 1024],
                                        t1s[:, ic * 1024:(ic + 1) * 1024],
                                        st2[:, c0 + 2:c0 + 3], st2[:, c0 + 6:c0 + 7],
                                        op0=OP.subtract, op1=OP.mult).then_inc(sC_va, 1)

            @blkC.scalar
            def _(sc):
                for qb in range(NQB2):
                    nkb = 2 * qb + 2
                    for kb in range(nkb):
                        n = n_tile_c(qb, kb)
                        sc.wait_ge(sC_ps, n + 1)
                        if kb >= nkb - 2:
                            sc.wait_ge(sC_vm, 2 * qb + (kb - (nkb - 2)) + 1)
                        if n >= 2:
                            sc.wait_ge(sC_pr, n - 1)
                        sc.activation(p2[:, (n % 2) * 256:(n % 2 + 1) * 256],
                                      ps2[n % 2][:, :], AF.Exp,
                                      bias=bias_st[:, kb:kb + 1], scale=1.0).then_inc(sC_a, 1)
                    for ic in range(2):
                        u = 2 * qb + ic
                        c0 = ic * 8
                        sc.wait_ge(sC_vt, u + 1)
                        sc.activation(sq2[:, :], t1s[:, ic * 1024:(ic + 1) * 1024], AF.Square,
                                      accum_out=st2[:, c0 + 1:c0 + 2]).then_inc(sC_ae, 1)
                        sc.wait_ge(sC_vv, u + 1)
                        sc.activation(st2[:, c0 + 5:c0 + 6], st2[:, c0 + 4:c0 + 5], AF.Ln,
                                      bias=eps_t[:, 0:1], scale=1.0).then_inc(sC_ae, 1)
                        sc.wait_ge(sC_ae, 3 * u + 2)
                        sc.activation(st2[:, c0 + 6:c0 + 7], st2[:, c0 + 5:c0 + 6], AF.Exp,
                                      scale=-0.5).then_inc(sC_ae, 1)

    if "B" in phases or "C" in phases:
        ps_bc.close()

    if "E" in phases:
        with ExitStack() as ps:
            wd_sb = ps.enter_context(nc.sbuf_tensor("wd_sb", [128, NPE * 4 * 1024], BF16))
            rw_sb = ps.enter_context(nc.sbuf_tensor("rw_sb", [128, 64], BF16))
            h1b = ps.enter_context(nc.sbuf_tensor("h1b", [128, 4 * 1024], BF16))
            h1t = ps.enter_context(nc.sbuf_tensor("h1t", [128, 2 * 4096], BF16))
            wgu = ps.enter_context(nc.sbuf_tensor("wgu", [128, 2 * 8192], BF16))
            a_sb = ps.enter_context(nc.sbuf_tensor("a_sb", [128, NPE * 4 * 512], BF16))
            wt_sb = ps.enter_context(nc.sbuf_tensor("wt_sb", [E, 512], BF16))
            wtil = ps.enter_context(nc.sbuf_tensor("wtil", [128, E * 512], BF16))
            sg_sb = ps.enter_context(nc.sbuf_tensor("sg_sb", [128, 2 * 512], BF16))
            ub_sb = ps.enter_context(nc.sbuf_tensor("ub_sb", [128, 512], BF16))
            h1f = ps.enter_context(nc.sbuf_tensor("h1f", [128, 2 * 1024], F32))
            t2s = ps.enter_context(nc.sbuf_tensor("t2s", [128, 1024], F32))
            st3 = ps.enter_context(nc.sbuf_tensor("st3", [128, 16], F32))
            wtb = ps.enter_context(nc.sbuf_tensor("wtb", [128, 4 * 8], BF16))
            lg_sb = ps.enter_context(nc.sbuf_tensor("lg_sb", [128, 8], F32))
            el_sb = ps.enter_context(nc.sbuf_tensor("el_sb", [128, 8], F32))
            rr = ps.enter_context(nc.sbuf_tensor("rr", [128, 32], F32))
            outs = ps.enter_context(nc.sbuf_tensor("outs", [128, 1024], F32))
            yA = ps.enter_context(nc.psum_tensor("yA", [128, 1024], F32))
            yB = ps.enter_context(nc.psum_tensor("yB", [128, 1024], F32))
            pg = ps.enter_context(nc.psum_tensor("pg", [128, 512], F32))
            pu = ps.enter_context(nc.psum_tensor("pu", [128, 512], F32))
            ptr2 = [ps.enter_context(nc.psum_tensor(f"ptr2_{i}", [128, 128], BF16))
                    for i in range(2)]

            sE_wrw = sem("sE_wrw")
            sE_wwd = sem("sE_wwd")
            sE_h1b = [sem(f"sE_h1b{s}") for s in range(4)]
            sE_wgu = [sem(f"sE_wgu{s}") for s in range(2)]
            sE_tr = sem("sE_tr")
            sE_trE = sem("sE_trE")
            sE_petc = sem("sE_petc")
            sE_pl = sem("sE_pl")
            sE_vcp = sem("sE_vcp")
            sE_ae2 = sem("sE_ae2")
            sE_vwt = sem("sE_vwt")
            sE_pw = sem("sE_pw")
            sE_aw = sem("sE_aw")
            sE_pg = sem("sE_pg")
            sE_pu = sem("sE_pu")
            sE_as = sem("sE_as")
            sE_vu = sem("sE_vu")
            sE_va = sem("sE_va")
            sE_py = sem("sE_py")
            sE_h1f = [sem(f"sE_h1f{s}") for s in range(2)]
            sE_vt2 = sem("sE_vt2")
            sE_ae3 = sem("sE_ae3")
            sE_vv3 = sem("sE_vv3")
            sE_vo = sem("sE_vo")
            sE_so = sem("sE_so")
            sE_vs = sem("sE_vs")

            blkE = ps.enter_context(nc.Block())

            def wgu_src(pe, kc, which):
                if pe < 8:
                    h = wg_h if which == 0 else wu_h
                    return h[pe * 1024 + kc * 128:pe * 1024 + (kc + 1) * 128, :]
                h = wsg_h if which == 0 else wsu_h
                c0 = (pe - 8) * 512
                return h[kc * 128:(kc + 1) * 128, c0:c0 + 512]

            def wd_src(pe, fc):
                if pe < 8:
                    return wd_h[pe * 512 + fc * 128:pe * 512 + (fc + 1) * 128, :]
                return wsd_h[(pe - 8) * 512 + fc * 128:(pe - 8) * 512 + (fc + 1) * 128, :]

            @blkE.gpsimd
            def _(g):
                if "C" in phases:
                    g.wait_ge(sC_sh[0], 16 * NQB2)
                    g.wait_ge(sC_sh[1], 16 * NQB2)
                for kc in range(KC):
                    g.dma_start(rw_sb[:, kc * 8:(kc + 1) * 8],
                                rw_h[kc * 128:(kc + 1) * 128, :]).then_inc(sE_wrw, 16)
                for gtb in range(4):
                    g.dma_start(h1b[:, (gtb % 4) * 1024:(gtb % 4 + 1) * 1024],
                                h1_h[gtb * 128:(gtb + 1) * 128, :]).then_inc(sE_h1b[gtb % 4], 16)
                for pe in range(2):
                    for which in range(2):
                        for kc in range(KC):
                            g.dma_start(
                                wgu[:, (pe % 2) * 8192 + which * 4096 + kc * 512:
                                    (pe % 2) * 8192 + which * 4096 + (kc + 1) * 512],
                                wgu_src(pe, kc, which)).then_inc(sE_wgu[pe % 2], 16)
                for pe in range(NPE):
                    for fc in range(4):
                        g.dma_start(wd_sb[:, (pe * 4 + fc) * 1024:(pe * 4 + fc + 1) * 1024],
                                    wd_src(pe, fc)).then_inc(sE_wwd, 16)
                for tc in range(NTCE):
                    for tbl in range(4):
                        gtb = tc * 4 + 4 + tbl
                        if gtb >= NTB:
                            continue
                        g.wait_ge(sE_tr, tc * 36 + (tbl + 1) * 8)
                        g.dma_start(h1b[:, (gtb % 4) * 1024:(gtb % 4 + 1) * 1024],
                                    h1_h[gtb * 128:(gtb + 1) * 128, :]).then_inc(sE_h1b[gtb % 4], 16)
                    for pe in range(NPE):
                        unit = tc * NPE + pe + 2
                        if unit >= NTCE * NPE:
                            continue
                        ntc, npe = unit // NPE, unit % NPE
                        g.wait_ge(sE_pu, 4 * (unit - 1))
                        for which in range(2):
                            for kc in range(KC):
                                g.dma_start(
                                    wgu[:, (unit % 2) * 8192 + which * 4096 + kc * 512:
                                        (unit % 2) * 8192 + which * 4096 + (kc + 1) * 512],
                                    wgu_src(npe, kc, which)).then_inc(sE_wgu[unit % 2], 16)

            @blkE.tensor
            def _(t):
                for tc in range(NTCE):
                    par = tc % 2
                    for tbl in range(4):
                        gtb = tc * 4 + tbl
                        t.wait_ge(sE_h1b[gtb % 4], 16 * (gtb // 4 + 1))
                        for dc in range(8):
                            i3 = tc * 36 + tbl * 8 + dc
                            if i3 >= 2:
                                t.wait_ge(sE_trE, i3 - 1)
                            t.transpose(
                                ptr2[i3 % 2][:, :],
                                h1b[:, (gtb % 4) * 1024 + dc * 128:(gtb % 4) * 1024 + (dc + 1) * 128],
                                ident[:, :]).then_inc(sE_tr, 1)
                    t.wait_ge(sE_trE, tc * 36 + 32)
                    if tc == 0:
                        t.wait_ge(sE_wrw, 16 * KC)
                    for tbl in range(4):
                        idx = tc * 4 + tbl
                        if idx >= 1:
                            t.wait_ge(sE_vcp, idx)
                        if tc >= 1 and tbl == 0:
                            t.wait_ge(sE_as, 40 * tc)
                        for kc in range(KC):
                            mm = t.matmul(
                                pg[:, 0:8],
                                h1t[:, par * 4096 + kc * 512 + tbl * 128:
                                    par * 4096 + kc * 512 + (tbl + 1) * 128],
                                rw_sb[:, kc * 8:(kc + 1) * 8],
                                start=(kc == 0), stop=(kc == KC - 1))
                        mm.then_inc(sE_pl, 1)
                    for tbl in range(4):
                        idx = tc * 4 + tbl
                        i3 = tc * 36 + 32 + tbl
                        t.wait_ge(sE_vwt, idx + 1)
                        if i3 >= 2:
                            t.wait_ge(sE_trE, i3 - 1)
                        t.transpose(ptr2[i3 % 2][0:8, :], wtb[:, tbl * 8:(tbl + 1) * 8],
                                    ident[:, :]).then_inc(sE_tr, 1)
                    t.wait_ge(sE_trE, 36 * (tc + 1))
                    if tc >= 1:
                        t.wait_ge(sE_vu, 40 * tc)
                    for e in range(E):
                        if e >= 2:
                            t.wait_ge(sE_aw, 8 * tc + e - 1)
                        dst = pg if e % 2 == 0 else pu
                        mm = t.matmul(dst[:, :], sel[:, e * 128:(e + 1) * 128],
                                      wt_sb[:, :], start=True, stop=True)
                        mm.then_inc(sE_pw, 1)
                    for pe in range(NPE):
                        unit = tc * NPE + pe
                        w0 = tc * 40 + pe * 4
                        t.wait_ge(sE_wgu[unit % 2], 256 * (unit // 2 + 1))
                        for fc in range(4):
                            w = w0 + fc
                            if pe == 0 and fc == 0:
                                t.wait_ge(sE_aw, 8 * (tc + 1))
                            if w >= 1:
                                t.wait_ge(sE_as, w)
                            for kc in range(KC):
                                mm = t.matmul(
                                    pg[:, :],
                                    wgu[:, (unit % 2) * 8192 + kc * 512 + fc * 128:
                                        (unit % 2) * 8192 + kc * 512 + (fc + 1) * 128],
                                    h1t[:, par * 4096 + kc * 512:par * 4096 + (kc + 1) * 512],
                                    start=(kc == 0), stop=(kc == KC - 1))
                            mm.then_inc(sE_pg, 1)
                            if w >= 1:
                                t.wait_ge(sE_vu, w)
                            for kc in range(KC):
                                mm = t.matmul(
                                    pu[:, :],
                                    wgu[:, (unit % 2) * 8192 + 4096 + kc * 512 + fc * 128:
                                        (unit % 2) * 8192 + 4096 + kc * 512 + (fc + 1) * 128],
                                    h1t[:, par * 4096 + kc * 512:par * 4096 + (kc + 1) * 512],
                                    start=(kc == 0), stop=(kc == KC - 1))
                            mm.then_inc(sE_pu, 1)
                            if pe == NPE - 1 and fc == 3:
                                t.nop().then_inc(sE_petc, 1)
                    t.wait_ge(sE_va, 40 * (tc + 1))
                    if tc == 0:
                        t.wait_ge(sE_wwd, 16 * NPE * 4)
                    for tbl in range(4):
                        gtb = tc * 4 + tbl
                        y = yA if tbl % 2 == 0 else yB
                        if gtb >= 2:
                            t.wait_ge(sE_vt2, gtb - 1)
                        for pe in range(NPE):
                            for fc in range(4):
                                for h in range(2):
                                    mm = t.matmul(
                                        y[:, h * 512:(h + 1) * 512],
                                        a_sb[:, (pe * 4 + fc) * 512 + tbl * 128:
                                             (pe * 4 + fc) * 512 + (tbl + 1) * 128],
                                        wd_sb[:, (pe * 4 + fc) * 1024 + h * 512:
                                              (pe * 4 + fc) * 1024 + (h + 1) * 512],
                                        start=(pe == 0 and fc == 0),
                                        stop=(pe == NPE - 1 and fc == 3))
                        mm.then_inc(sE_py, 1)

            @blkE.vector
            def _(v):
                nvs = [0]

                def vself(instr):
                    nvs[0] += 1
                    instr.then_inc(sE_vs, 1)
                    v.wait_ge(sE_vs, nvs[0])

                for tc in range(NTCE):
                    par = tc % 2
                    for tbl in range(4):
                        for dc in range(8):
                            i3 = tc * 36 + tbl * 8 + dc
                            v.wait_ge(sE_tr, i3 + 1)
                            if tbl == 0 and dc == 0 and tc >= 2:
                                v.wait_ge(sE_petc, tc - 1)
                            v.tensor_copy(
                                h1t[:, par * 4096 + dc * 512 + tbl * 128:
                                    par * 4096 + dc * 512 + (tbl + 1) * 128],
                                ptr2[i3 % 2][:, :]).then_inc(sE_trE, 1)
                    for tbl in range(4):
                        idx = tc * 4 + tbl
                        v.wait_ge(sE_pl, idx + 1)
                        vself(v.tensor_copy(lg_sb[:, :], pg[:, 0:8]))
                        v.reduce_max(rr[:, 0:1], lg_sb[:, :], axis=AX.X,
                                     negate=True).then_inc(sE_vcp, 1)
                        v.wait_ge(sE_ae2, idx + 1)
                        vself(v.reciprocal(rr[:, 2:3], rr[:, 1:2]))
                        vself(v.tensor_scalar(rr[:, 8:16], el_sb[:, :], rr[:, 2:3], None,
                                              op0=OP.mult))
                        vself(v.tensor_reduce(rr[:, 3:4], rr[:, 8:16], axis=AX.X, op=OP.min))
                        vself(v.tensor_scalar(rr[:, 16:24], rr[:, 8:16], rr[:, 3:4], None,
                                              op0=OP.is_le))
                        vself(v.scalar_tensor_tensor(rr[:, 16:24], rr[:, 16:24], BIG,
                                                     rr[:, 8:16], op0=OP.mult, op1=OP.add))
                        vself(v.tensor_reduce(rr[:, 4:5], rr[:, 16:24], axis=AX.X, op=OP.min))
                        vself(v.tensor_scalar(rr[:, 24:32], rr[:, 16:24], rr[:, 4:5], None,
                                              op0=OP.is_le))
                        vself(v.scalar_tensor_tensor(rr[:, 24:32], rr[:, 24:32], BIG,
                                                     rr[:, 16:24], op0=OP.mult, op1=OP.add))
                        vself(v.tensor_reduce(rr[:, 5:6], rr[:, 24:32], axis=AX.X, op=OP.min))
                        vself(v.tensor_scalar(rr[:, 16:24], rr[:, 8:16], rr[:, 5:6], None,
                                              op0=OP.is_ge))
                        v.tensor_tensor(wtb[:, tbl * 8:(tbl + 1) * 8], rr[:, 8:16],
                                        rr[:, 16:24], op=OP.mult).then_inc(sE_vwt, 1)
                    for tbl in range(4):
                        i3 = tc * 36 + 32 + tbl
                        v.wait_ge(sE_tr, i3 + 1)
                        v.tensor_copy(wt_sb[:, tbl * 128:(tbl + 1) * 128],
                                      ptr2[i3 % 2][0:8, :]).then_inc(sE_trE, 1)
                    for pe in range(NPE):
                        for fc in range(4):
                            w = tc * 40 + pe * 4 + fc
                            v.wait_ge(sE_pu, w + 1)
                            if pe == 0 and fc == 0 and tc >= 1:
                                v.wait_ge(sE_py, 4 * tc)
                            if pe < 8:
                                v.wait_ge(sE_aw, 8 * tc + pe + 1)
                                v.tensor_tensor(ub_sb[:, :],
                                                pu[:, :], wtil[:, pe * 512:(pe + 1) * 512],
                                                op=OP.mult).then_inc(sE_vu, 1)
                            else:
                                v.tensor_copy(ub_sb[:, :],
                                              pu[:, :]).then_inc(sE_vu, 1)
                            v.wait_ge(sE_vu, w + 1)
                            v.wait_ge(sE_as, w + 1)
                            v.tensor_tensor(a_sb[:, (pe * 4 + fc) * 512:
                                                 (pe * 4 + fc + 1) * 512],
                                            sg_sb[:, (w % 2) * 512:(w % 2 + 1) * 512],
                                            ub_sb[:, :],
                                            op=OP.mult).then_inc(sE_va, 1)
                    for tbl in range(4):
                        gtb = tc * 4 + tbl
                        y = yA if tbl % 2 == 0 else yB
                        c0 = (gtb % 2) * 8
                        v.wait_ge(sE_py, gtb + 1)
                        v.wait_ge(sE_h1f[gtb % 2], 16 * (gtb // 2 + 1))
                        if gtb >= 1:
                            v.wait_ge(sE_ae3, 3 * (gtb - 1) + 1)
                        v.tensor_add(t2s[:, :],
                                     y[:, :],
                                     h1f[:, (gtb % 2) * 1024:(gtb % 2 + 1) * 1024]
                                     ).then_inc(sE_vt2, 1)
                        v.wait_ge(sE_vt2, gtb + 1)
                        vself(v.reduce_sum(st3[:, c0:c0 + 1], t2s[:, :], axis=AX.X))
                        v.wait_ge(sE_ae3, 3 * gtb + 1)
                        vself(v.tensor_scalar(st3[:, c0 + 2:c0 + 3], st3[:, c0:c0 + 1],
                                              1.0 / D, None, op0=OP.mult))
                        vself(v.tensor_tensor(st3[:, c0 + 3:c0 + 4], st3[:, c0 + 2:c0 + 3],
                                              st3[:, c0 + 2:c0 + 3], op=OP.mult))
                        v.scalar_tensor_tensor(st3[:, c0 + 4:c0 + 5], st3[:, c0 + 1:c0 + 2],
                                               1.0 / D, st3[:, c0 + 3:c0 + 4],
                                               op0=OP.mult, op1=OP.subtract).then_inc(sE_vv3, 1)
                        v.wait_ge(sE_ae3, 3 * gtb + 3)
                        if gtb >= 1:
                            v.wait_ge(sE_so, 16 * gtb)
                        v.tensor_scalar(outs[:, :],
                                        t2s[:, :],
                                        st3[:, c0 + 2:c0 + 3], st3[:, c0 + 6:c0 + 7],
                                        op0=OP.subtract, op1=OP.mult).then_inc(sE_vo, 1)

            @blkE.scalar
            def _(sc):
                for tc in range(NTCE):
                    for tbl in range(4):
                        idx = tc * 4 + tbl
                        sc.wait_ge(sE_vcp, idx + 1)
                        sc.activation(el_sb[:, :], lg_sb[:, :], AF.Exp,
                                      bias=rr[:, 0:1], scale=1.0,
                                      accum_out=rr[:, 1:2]).then_inc(sE_ae2, 1)
                    for e in range(E):
                        sc.wait_ge(sE_pw, 8 * tc + e + 1)
                        if e == 0 and tc >= 1:
                            sc.wait_ge(sE_vu, 40 * tc)
                        src = pg if e % 2 == 0 else pu
                        sc.activation(wtil[:, e * 512:(e + 1) * 512], src[:, :],
                                      AF.Copy).then_inc(sE_aw, 1)
                    for pe in range(NPE):
                        for fc in range(4):
                            w = tc * 40 + pe * 4 + fc
                            sc.wait_ge(sE_pg, w + 1)
                            if w >= 2:
                                sc.wait_ge(sE_va, w - 1)
                            sc.activation(sg_sb[:, (w % 2) * 512:(w % 2 + 1) * 512],
                                          pg[:, :], AF.Silu).then_inc(sE_as, 1)
                    for tbl in range(4):
                        gtb = tc * 4 + tbl
                        c0 = (gtb % 2) * 8
                        sc.wait_ge(sE_vt2, gtb + 1)
                        if gtb >= 1:
                            sc.wait_ge(sE_so, 16 * gtb)
                        sc.activation(outs[:, :], t2s[:, :], AF.Square,
                                      accum_out=st3[:, c0 + 1:c0 + 2]).then_inc(sE_ae3, 1)
                        sc.wait_ge(sE_vv3, gtb + 1)
                        sc.activation(st3[:, c0 + 5:c0 + 6], st3[:, c0 + 4:c0 + 5], AF.Ln,
                                      bias=eps_t[:, 0:1], scale=1.0).then_inc(sE_ae3, 1)
                        sc.wait_ge(sE_ae3, 3 * gtb + 2)
                        sc.activation(st3[:, c0 + 6:c0 + 7], st3[:, c0 + 5:c0 + 6], AF.Exp,
                                      scale=-0.5).then_inc(sE_ae3, 1)

            @blkE.sync
            def _(s):
                for tc in range(NTCE):
                    for tbl in range(4):
                        gtb = tc * 4 + tbl
                        if gtb >= 2:
                            s.wait_ge(sE_vt2, gtb - 1)
                        s.dma_start(h1f[:, (gtb % 2) * 1024:(gtb % 2 + 1) * 1024],
                                    h1_h[gtb * 128:(gtb + 1) * 128, :]).then_inc(sE_h1f[gtb % 2], 16)
                    for tbl in range(4):
                        gtb = tc * 4 + tbl
                        s.wait_ge(sE_vo, gtb + 1)
                        s.dma_start(out_h[gtb * 128:(gtb + 1) * 128, :],
                                    outs[:, :]).then_inc(sE_so, 16)
                if dbg:
                    s.wait_ge(sE_vwt, 64)
                s.wait_ge(sE_so, 16 * 64)

            if dbg:
                pass

    es.close()
    return nc


def make_consts():
    ident = np.eye(128, dtype=ml_dtypes.bfloat16)
    p = np.arange(128)[:, None]
    m = np.arange(896)[None, :]
    maskm = np.where(m >= p + 384, 0.0, NEGINF).astype(np.float32)
    sel = np.zeros((E, E * 128), dtype=ml_dtypes.bfloat16)
    for e in range(E):
        sel[e, e * 128:(e + 1) * 128] = 1.0
    return ident, maskm, sel


def kernel(**inputs):
    f = lambda k: np.ascontiguousarray(np.asarray(inputs[k], dtype=np.float32))
    x = f("L0_attn_minibatch")
    wq, wk, wv = f("L0_attn_Wq"), f("L0_attn_Wk"), f("L0_attn_Wv")
    bq = f("L0_attn_bq").reshape(D, 1)
    bk = f("L0_attn_bk").reshape(D, 1)
    bv = f("L0_attn_bv").reshape(1, D)
    rw = f("L0_ffn_router_w")
    wg = f("L0_ffn_w_gate").reshape(E * D, FF)
    wu = f("L0_ffn_w_up").reshape(E * D, FF)
    wd = f("L0_ffn_w_down").reshape(E * FF, D)
    wsg, wsu, wsd = f("L0_ffn_ws_gate"), f("L0_ffn_ws_up"), f("L0_ffn_ws_down")
    ident, maskm, sel = make_consts()

    nc = build_program()
    base = dict(wq=wq, wk=wk, wv=wv, bq=bq, bk=bk, bv=bv, rw=rw, wg=wg, wu=wu,
                wd=wd, wsg=wsg, wsu=wsu, wsd=wsd, ident=ident, maskm=maskm, sel=sel)
    in_maps = [dict(base, x=np.ascontiguousarray(x[b])) for b in range(8)]
    res = run_bass_kernel_spmd(nc, in_maps, core_ids=list(range(8)))
    return np.stack([np.asarray(r["out"]) for r in res.results], axis=0)


# version 10
# speedup vs baseline: 1.1055x; 1.0102x over previous
import numpy as np
import ml_dtypes
import concourse.bass as bass
import concourse.mybir as mybir
from concourse.bass_utils import run_bass_kernel_spmd
from concourse.alu_op_type import AluOpType as OP
from contextlib import ExitStack

F32 = mybir.dt.float32
BF16 = mybir.dt.bfloat16
AF = mybir.ActivationFunctionType
AX = mybir.AxisListType

T, D, E, FF, FS = 8192, 1024, 8, 512, 1024
KC = 8
NTB = 64
NTCA = 16
NQT1 = 16
NKB = 64
QB2, NQB2 = 256, 32
TCE, NTCE = 512, 16
NPE = 10
NEGINF = -1.0e30
BIG = 1.0e9
EPS = 1e-5


def n_tile_b(qt, kb):
    return 2 * qt * (qt + 1) + kb


def n_tile_c(qb, kb):
    return qb * (qb + 1) + kb


NB_TOT = n_tile_b(15, 63 + 1 - 60) + 64
NC_TOT = 32 * 33


def build_program(phases=("A", "B", "C", "E"), dbg=False):
    nc = bass.Bass("TRN2", target_bir_lowering=False)

    x_h = nc.dram_tensor("x", [T, D], F32, kind="ExternalInput").ap()
    wq_h = nc.dram_tensor("wq", [D, D], F32, kind="ExternalInput").ap()
    wk_h = nc.dram_tensor("wk", [D, D], F32, kind="ExternalInput").ap()
    wv_h = nc.dram_tensor("wv", [D, D], F32, kind="ExternalInput").ap()
    bq_h = nc.dram_tensor("bq", [D, 1], F32, kind="ExternalInput").ap()
    bk_h = nc.dram_tensor("bk", [D, 1], F32, kind="ExternalInput").ap()
    bv_h = nc.dram_tensor("bv", [1, D], F32, kind="ExternalInput").ap()
    rw_h = nc.dram_tensor("rw", [D, E], F32, kind="ExternalInput").ap()
    wg_h = nc.dram_tensor("wg", [E * D, FF], F32, kind="ExternalInput").ap()
    wu_h = nc.dram_tensor("wu", [E * D, FF], F32, kind="ExternalInput").ap()
    wd_h = nc.dram_tensor("wd", [E * FF, D], F32, kind="ExternalInput").ap()
    wsg_h = nc.dram_tensor("wsg", [D, FS], F32, kind="ExternalInput").ap()
    wsu_h = nc.dram_tensor("wsu", [D, FS], F32, kind="ExternalInput").ap()
    wsd_h = nc.dram_tensor("wsd", [FS, D], F32, kind="ExternalInput").ap()
    ident_h = nc.dram_tensor("ident", [128, 128], BF16, kind="ExternalInput").ap()
    maskm_h = nc.dram_tensor("maskm", [128, 896], F32, kind="ExternalInput").ap()
    sel_h = nc.dram_tensor("sel", [E, E * 128], BF16, kind="ExternalInput").ap()

    out_h = nc.dram_tensor("out", [T, D], F32, kind="ExternalOutput").ap()

    sk = "ExternalOutput" if dbg else "Internal"
    qt_h = nc.dram_tensor("qt_s", [D, T], BF16, kind=sk).ap()
    kt_h = nc.dram_tensor("kt_s", [D, T], BF16, kind=sk).ap()
    v_h = nc.dram_tensor("v_s", [T, D], BF16, kind=sk).ap()
    h1_h = nc.dram_tensor("h1_s", [T, D], F32, kind=sk).ap()
    if dbg:
        dbias_h = nc.dram_tensor("d_bias", [128, NKB], F32, kind="ExternalOutput").ap()
        dmn_h = nc.dram_tensor("d_mn", [128, NKB * 16], F32, kind="ExternalOutput").ap()
        de_h = nc.dram_tensor("d_e", [128, NKB * 16], F32, kind="ExternalOutput").ap()
        ds_h = nc.dram_tensor("d_s", [128, NKB], F32, kind="ExternalOutput").ap()

    es = ExitStack()
    sem = lambda name: es.enter_context(nc.semaphore(name))

    ident = es.enter_context(nc.sbuf_tensor("ident_s", [128, 128], BF16))
    sel = es.enter_context(nc.sbuf_tensor("sel_s", [E, E * 128], BF16))
    ones1 = es.enter_context(nc.sbuf_tensor("ones1", [1, 128], BF16))
    eps_t = es.enter_context(nc.sbuf_tensor("eps_t", [128, 1], F32))
    bias_st = es.enter_context(nc.sbuf_tensor("bias_st", [128, NKB], F32))

    sA_c = sem("sA_c")

    if "A" in phases:
        with ExitStack() as ps:
            xa = ps.enter_context(nc.sbuf_tensor("xa", [128, 8 * 1024], BF16))
            wq_s = ps.enter_context(nc.sbuf_tensor("wq_s", [128, 8 * 1024], BF16))
            wk_s = ps.enter_context(nc.sbuf_tensor("wk_s", [128, 8 * 1024], BF16))
            wv_s = ps.enter_context(nc.sbuf_tensor("wv_s", [128, 8 * 1024], BF16))
            bq_s = ps.enter_context(nc.sbuf_tensor("bq_s", [128, 8], F32))
            bk_s = ps.enter_context(nc.sbuf_tensor("bk_s", [128, 8], F32))
            bv_s = ps.enter_context(nc.sbuf_tensor("bv_s", [1, 1024], BF16))
            xt = ps.enter_context(nc.sbuf_tensor("xt", [128, 2 * 4096], BF16))
            qs = ps.enter_context(nc.sbuf_tensor("qs", [128, 2 * 512], BF16))
            ks = ps.enter_context(nc.sbuf_tensor("ks", [128, 2 * 512], BF16))
            vs = ps.enter_context(nc.sbuf_tensor("vs", [128, 2 * 1024], BF16))
            pq = [ps.enter_context(nc.psum_tensor(f"pq{i}", [128, 512], F32)) for i in range(2)]
            pk = [ps.enter_context(nc.psum_tensor(f"pk{i}", [128, 512], F32)) for i in range(2)]
            pv = ps.enter_context(nc.psum_tensor("pv", [128, 1024], F32))
            ptr = [ps.enter_context(nc.psum_tensor(f"ptr{i}", [128, 128], BF16)) for i in range(2)]

            sA_w = sem("sA_w")
            sA_ms = sem("sA_ms")
            sA_x = [sem(f"sA_x{s}") for s in range(8)]
            sA_tr = sem("sA_tr")
            sA_trE = sem("sA_trE")
            sA_mm = sem("sA_mm")
            sA_pq = sem("sA_pq")
            sA_pk = sem("sA_pk")
            sA_pv = sem("sA_pv")
            sA_evq = sem("sA_evq")
            sA_evk = sem("sA_evk")
            sA_evv = sem("sA_evv")
            sA_stq = [sem(f"sA_stq{s}") for s in range(2)]
            sA_stk = [sem(f"sA_stk{s}") for s in range(2)]
            sA_stv = [sem(f"sA_stv{s}") for s in range(2)]

            blkA = ps.enter_context(nc.Block())

            @blkA.gpsimd
            def _(g):
                g.dma_start(ident[:, :], ident_h[:, :]).then_inc(sA_c, 16)
                g.dma_start(sel[:, :], sel_h[:, :]).then_inc(sA_c, 16)
                for t_i in range(8):
                    g.dma_start(xa[:, (t_i % 8) * 1024:(t_i % 8 + 1) * 1024],
                                x_h[t_i * 128:(t_i + 1) * 128, :]).then_inc(sA_x[t_i % 8], 16)
                for kc in range(KC):
                    g.dma_start(wq_s[:, kc * 1024:(kc + 1) * 1024],
                                wq_h[kc * 128:(kc + 1) * 128, :]).then_inc(sA_w, 16)
                for kc in range(KC):
                    g.dma_start(wk_s[:, kc * 1024:(kc + 1) * 1024],
                                wk_h[kc * 128:(kc + 1) * 128, :]).then_inc(sA_w, 16)
                for kc in range(KC):
                    g.dma_start(wv_s[:, kc * 1024:(kc + 1) * 1024],
                                wv_h[kc * 128:(kc + 1) * 128, :]).then_inc(sA_w, 16)
                for dc in range(8):
                    g.dma_start(bq_s[:, dc:dc + 1],
                                bq_h[dc * 128:(dc + 1) * 128, 0:1]).then_inc(sA_w, 16)
                for dc in range(8):
                    g.dma_start(bk_s[:, dc:dc + 1],
                                bk_h[dc * 128:(dc + 1) * 128, 0:1]).then_inc(sA_w, 16)
                g.dma_start(bv_s[:, :], bv_h[:, :]).then_inc(sA_w, 16)
                g.memset(ones1[:, :], 1.0).then_inc(sA_ms, 1)
                g.memset(eps_t[:, :], EPS).then_inc(sA_ms, 1)
                for t_i in range(8, NTB):
                    g.wait_ge(sA_tr, 8 * (t_i - 8) + 8)
                    g.dma_start(xa[:, (t_i % 8) * 1024:(t_i % 8 + 1) * 1024],
                                x_h[t_i * 128:(t_i + 1) * 128, :]).then_inc(sA_x[t_i % 8], 16)

            W_TOT = 41 * 16

            def emit_tr(t, tc):
                for tbl in range(4):
                    t_i = tc * 4 + tbl
                    t.wait_ge(sA_x[t_i % 8], 16 * (t_i // 8 + 1))
                    for dc in range(8):
                        i = t_i * 8 + dc
                        if i >= 2:
                            t.wait_ge(sA_trE, i - 1)
                        t.transpose(
                            ptr[i % 2][:, :],
                            xa[:, (t_i % 8) * 1024 + dc * 128:(t_i % 8) * 1024 + (dc + 1) * 128],
                            ident[:, :]).then_inc(sA_tr, 1)

            @blkA.tensor
            def _(t):
                t.wait_ge(sA_c, 32)
                emit_tr(t, 0)
                for tc in range(NTCA):
                    par = tc % 2
                    if tc + 1 < NTCA:
                        emit_tr(t, tc + 1)
                    t.wait_ge(sA_trE, 32 * (tc + 1))
                    if tc == 0:
                        t.wait_ge(sA_w, W_TOT)
                        t.wait_ge(sA_ms, 2)

                    def q_group(dc, which):
                        ws, pp, sev, spp = (wq_s, pq, sA_evq, sA_pq) if which == 0 else (
                            wk_s, pk, sA_evk, sA_pk)
                        g = tc * 8 + dc
                        if g >= 2:
                            t.wait_ge(sev, g - 1)
                        for kc in range(KC):
                            mm = t.matmul(
                                pp[g % 2][:, :],
                                ws[:, kc * 1024 + dc * 128:kc * 1024 + (dc + 1) * 128],
                                xt[:, par * 4096 + kc * 512:par * 4096 + (kc + 1) * 512],
                                start=(kc == 0), stop=(kc == KC - 1))
                        mm.then_inc(spp, 1)
                        return mm

                    def v_group(tbl):
                        vi = tc * 4 + tbl
                        if vi >= 1:
                            t.wait_ge(sA_evv, vi)
                        for kc in range(KC):
                            for h in range(2):
                                t.matmul(
                                    pv[:, h * 512:(h + 1) * 512],
                                    xt[:, par * 4096 + kc * 512 + tbl * 128:
                                       par * 4096 + kc * 512 + (tbl + 1) * 128],
                                    wv_s[:, kc * 1024 + h * 512:kc * 1024 + (h + 1) * 512],
                                    start=(kc == 0), stop=False)
                        for h in range(2):
                            mm = t.matmul(pv[:, h * 512:(h + 1) * 512], ones1[:, :],
                                          bv_s[:, h * 512:(h + 1) * 512],
                                          start=False, stop=True)
                        mm.then_inc(sA_pv, 1)
                        return mm

                    last = None
                    for dc in range(4):
                        last = q_group(dc, 0)
                    last = v_group(0)
                    for dc in range(4, 8):
                        last = q_group(dc, 0)
                    last = v_group(1)
                    for dc in range(4):
                        last = q_group(dc, 1)
                    last = v_group(2)
                    for dc in range(4, 8):
                        last = q_group(dc, 1)
                    last = v_group(3)
                    t.nop().then_inc(sA_mm, 1)

            @blkA.vector
            def _(v):
                for tc in range(NTCA + 1):
                    if tc >= NTCA:
                        break
                    for tbl in range(4):
                        t_i = tc * 4 + tbl
                        for dc in range(8):
                            i = t_i * 8 + dc
                            v.wait_ge(sA_tr, i + 1)
                            if tbl == 0 and dc == 0 and tc >= 2:
                                v.wait_ge(sA_mm, tc - 1)
                            v.tensor_copy(
                                xt[:, (tc % 2) * 4096 + dc * 512 + tbl * 128:
                                   (tc % 2) * 4096 + dc * 512 + (tbl + 1) * 128],
                                ptr[i % 2][:, :]).then_inc(sA_trE, 1)

            @blkA.scalar
            def _(sc):
                for tc in range(NTCA):
                    def q_evac(dc, which):
                        ws_stage, pp, bb, spp, sev, sst = (
                            (qs, pq, bq_s, sA_pq, sA_evq, sA_stq) if which == 0
                            else (ks, pk, bk_s, sA_pk, sA_evk, sA_stk))
                        g = tc * 8 + dc
                        sc.wait_ge(spp, g + 1)
                        if g >= 2:
                            sc.wait_ge(sst[g % 2], 16 * (g // 2))
                        sc.activation(ws_stage[:, (g % 2) * 512:(g % 2 + 1) * 512],
                                      pp[g % 2][:, :], AF.Identity,
                                      bias=bb[:, dc:dc + 1], scale=1.0).then_inc(sev, 1)

                    def v_evac(tbl):
                        vi = tc * 4 + tbl
                        sc.wait_ge(sA_pv, vi + 1)
                        if vi >= 2:
                            sc.wait_ge(sA_stv[vi % 2], 16 * (vi // 2))
                        sc.activation(vs[:, (vi % 2) * 1024:(vi % 2 + 1) * 1024],
                                      pv[:, :], AF.Copy).then_inc(sA_evv, 1)

                    for dc in range(4):
                        q_evac(dc, 0)
                    v_evac(0)
                    for dc in range(4, 8):
                        q_evac(dc, 0)
                    v_evac(1)
                    for dc in range(4):
                        q_evac(dc, 1)
                    v_evac(2)
                    for dc in range(4, 8):
                        q_evac(dc, 1)
                    v_evac(3)

            @blkA.sync
            def _(s):
                for tc in range(NTCA):
                    def q_store(dc, which):
                        stage, dst, sev, sst = (qs, qt_h, sA_evq, sA_stq) if which == 0 else (
                            ks, kt_h, sA_evk, sA_stk)
                        g = tc * 8 + dc
                        s.wait_ge(sev, g + 1)
                        s.dma_start(dst[dc * 128:(dc + 1) * 128, tc * 512:(tc + 1) * 512],
                                    stage[:, (g % 2) * 512:(g % 2 + 1) * 512]).then_inc(sst[g % 2], 16)

                    def v_store(tbl):
                        vi = tc * 4 + tbl
                        s.wait_ge(sA_evv, vi + 1)
                        s.dma_start(v_h[vi * 128:(vi + 1) * 128, :],
                                    vs[:, (vi % 2) * 1024:(vi % 2 + 1) * 1024]).then_inc(sA_stv[vi % 2], 16)

                    for dc in range(4):
                        q_store(dc, 0)
                    v_store(0)
                    for dc in range(4, 8):
                        q_store(dc, 0)
                    v_store(1)
                    for dc in range(4):
                        q_store(dc, 1)
                    v_store(2)
                    for dc in range(4, 8):
                        q_store(dc, 1)
                    v_store(3)
                for s2 in range(2):
                    s.wait_ge(sA_stq[s2], 16 * 64)
                    s.wait_ge(sA_stk[s2], 16 * 64)
                    s.wait_ge(sA_stv[s2], 16 * 32)

    if "B" in phases or "C" in phases:
        ps_bc = ExitStack()
        kt_sb = ps_bc.enter_context(nc.sbuf_tensor("kt_sb", [128, 8 * 8192], BF16))
        maskm = ps_bc.enter_context(nc.sbuf_tensor("maskm_s", [128, 896], F32))
        sBC_kt = sem("sBC_kt")
        sBC_mk = sem("sBC_mk")

    if "B" in phases:
        with ExitStack() as ps:
            qt1 = ps.enter_context(nc.sbuf_tensor("qt1", [128, 2 * 4096], BF16))
            mneg_st = ps.enter_context(nc.sbuf_tensor("mneg_st", [128, NKB * 16], F32))
            e_st = ps.enter_context(nc.sbuf_tensor("e_st", [128, NKB * 16], F32))
            tmp_st = ps.enter_context(nc.sbuf_tensor("tmp_st", [128, NKB * 16], F32))
            sc1 = ps.enter_context(nc.sbuf_tensor("sc1", [128, 2 * 512], BF16))
            mnegf_st = ps.enter_context(nc.sbuf_tensor("mnegf_st", [128, NKB], F32))
            s_st = ps.enter_context(nc.sbuf_tensor("s_st", [128, NKB], F32))
            lns_st = ps.enter_context(nc.sbuf_tensor("lns_st", [128, NKB], F32))
            ps1 = [ps.enter_context(nc.psum_tensor(f"ps1_{i}", [128, 512], F32)) for i in range(2)]

            sB_q = [sem(f"sB_q{s}") for s in range(2)]
            sB_db = sem("sB_db")
            sB_pe = sem("sB_pe")
            sB_v = sem("sB_v")
            sB_a = sem("sB_a")
            sB_vf = sem("sB_vf")
            sB_af = sem("sB_af")
            sB_vf2 = sem("sB_vf2")
            sB_af2 = sem("sB_af2")
            sB_fin = sem("sB_fin")
            sB_vs = sem("sB_vs")

            blkB = ps.enter_context(nc.Block())

            @blkB.sync
            def _(s):
                s.dma_start(maskm[:, :], maskm_h[:, :]).then_inc(sBC_mk, 16)
                for kc in range(KC):
                    s.dma_start(kt_sb[:, kc * 8192:(kc + 1) * 8192],
                                kt_h[kc * 128:(kc + 1) * 128, :]).then_inc(sBC_kt, 16)
                for qt in range(NQT1):
                    if qt >= 2:
                        s.wait_ge(sB_pe, n_tile_b(qt - 1, 0))
                    for kc in range(KC):
                        s.dma_start(qt1[:, (qt % 2) * 4096 + kc * 512:
                                        (qt % 2) * 4096 + (kc + 1) * 512],
                                    qt_h[kc * 128:(kc + 1) * 128,
                                         qt * 512:(qt + 1) * 512]).then_inc(sB_q[qt % 2], 16)
                if dbg and "C" not in phases:
                    s.wait_ge(sB_fin, NKB)
                    s.dma_start(dbias_h[:, :], bias_st[:, :]).then_inc(sB_db, 16)
                    s.dma_start(dmn_h[:, :], mneg_st[:, :]).then_inc(sB_db, 16)
                    s.dma_start(de_h[:, :], e_st[:, :]).then_inc(sB_db, 16)
                    s.dma_start(ds_h[:, :], s_st[:, :]).then_inc(sB_db, 16)
                    s.wait_ge(sB_db, 64)

            @blkB.tensor
            def _(t):
                t.wait_ge(sBC_kt, 16 * KC)
                for qt in range(NQT1):
                    t.wait_ge(sB_q[qt % 2], 128 * (qt // 2 + 1))
                    for kb in range(4 * qt + 4):
                        n = n_tile_b(qt, kb)
                        if n >= 2:
                            t.wait_ge(sB_a, n - 1)
                        for kc in range(KC):
                            mm = t.matmul(
                                ps1[n % 2][:, :],
                                kt_sb[:, kc * 8192 + kb * 128:kc * 8192 + (kb + 1) * 128],
                                qt1[:, (qt % 2) * 4096 + kc * 512:(qt % 2) * 4096 + (kc + 1) * 512],
                                start=(kc == 0), stop=(kc == KC - 1))
                        mm.then_inc(sB_pe, 1)

            @blkB.vector
            def _(v):
                nvs = [0]

                def vself(instr):
                    nvs[0] += 1
                    instr.then_inc(sB_vs, 1)
                    v.wait_ge(sB_vs, nvs[0])

                v.wait_ge(sBC_mk, 16)
                for qt in range(NQT1):
                    for kb in range(4 * qt + 4):
                        n = n_tile_b(qt, kb)
                        v.wait_ge(sB_pe, n + 1)
                        if kb >= 4 * qt:
                            off = kb * 128 - qt * 512
                            sm = 384 - off
                            vself(v.tensor_add(ps1[n % 2][:, :], ps1[n % 2][:, :],
                                               maskm[:, sm:sm + 512]))
                        v.reduce_max(mneg_st[:, kb * 16 + qt:kb * 16 + qt + 1],
                                     ps1[n % 2][:, :], axis=AX.X,
                                     negate=True).then_inc(sB_v, 1)
                for kb in range(NKB):
                    qt0 = kb // 4
                    v.tensor_reduce(mnegf_st[:, kb:kb + 1],
                                    mneg_st[:, kb * 16 + qt0:kb * 16 + 16],
                                    axis=AX.X, op=OP.min).then_inc(sB_vf, 1)
                for kb in range(NKB):
                    qt0 = kb // 4
                    v.wait_ge(sB_af, kb + 1)
                    vself(v.tensor_tensor(tmp_st[:, kb * 16 + qt0:kb * 16 + 16],
                                          tmp_st[:, kb * 16 + qt0:kb * 16 + 16],
                                          e_st[:, kb * 16 + qt0:kb * 16 + 16], op=OP.mult))
                    v.reduce_sum(s_st[:, kb:kb + 1],
                                 tmp_st[:, kb * 16 + qt0:kb * 16 + 16],
                                 axis=AX.X).then_inc(sB_vf2, 1)
                for kb in range(NKB):
                    v.wait_ge(sB_af2, kb + 1)
                    v.tensor_sub(bias_st[:, kb:kb + 1], mnegf_st[:, kb:kb + 1],
                                 lns_st[:, kb:kb + 1]).then_inc(sB_fin, 1)

            @blkB.scalar
            def _(sc):
                for qt in range(NQT1):
                    for kb in range(4 * qt + 4):
                        n = n_tile_b(qt, kb)
                        sc.wait_ge(sB_v, n + 1)
                        sc.activation(sc1[:, (n % 2) * 512:(n % 2 + 1) * 512],
                                      ps1[n % 2][:, :], AF.Exp,
                                      bias=mneg_st[:, kb * 16 + qt:kb * 16 + qt + 1],
                                      scale=1.0,
                                      accum_out=e_st[:, kb * 16 + qt:kb * 16 + qt + 1]
                                      ).then_inc(sB_a, 1)
                for kb in range(NKB):
                    qt0 = kb // 4
                    sc.wait_ge(sB_vf, kb + 1)
                    sc.activation(tmp_st[:, kb * 16 + qt0:kb * 16 + 16],
                                  mneg_st[:, kb * 16 + qt0:kb * 16 + 16], AF.Exp,
                                  bias=mnegf_st[:, kb:kb + 1],
                                  scale=-1.0).then_inc(sB_af, 1)
                for kb in range(NKB):
                    sc.wait_ge(sB_vf2, kb + 1)
                    sc.activation(lns_st[:, kb:kb + 1], s_st[:, kb:kb + 1], AF.Ln,
                                  scale=32.0).then_inc(sB_af2, 1)

    if "C" in phases:
        with ExitStack() as ps:
            qt2 = ps.enter_context(nc.sbuf_tensor("qt2", [128, 2 * 2048], BF16))
            v2 = ps.enter_context(nc.sbuf_tensor("v2", [128, 4 * 1024], BF16))
            p2 = ps.enter_context(nc.sbuf_tensor("p2", [128, 2 * 256], BF16))
            x2 = ps.enter_context(nc.sbuf_tensor("x2", [128, 2 * 2048], F32))
            t1s = ps.enter_context(nc.sbuf_tensor("t1s", [128, 2 * 1024], F32))
            h1s = ps.enter_context(nc.sbuf_tensor("h1s", [128, 2 * 1024], F32))
            sq2 = ps.enter_context(nc.sbuf_tensor("sq2", [128, 1024], BF16))
            st2 = ps.enter_context(nc.sbuf_tensor("st2", [128, 16], F32))
            ps2 = [ps.enter_context(nc.psum_tensor(f"ps2_{i}", [128, 256], F32)) for i in range(2)]
            pr = [ps.enter_context(nc.psum_tensor(f"pr{i}", [128, 1024], F32)) for i in range(2)]

            sC_q = [sem(f"sC_q{s}") for s in range(2)]
            sC_x2 = [sem(f"sC_x2{s}") for s in range(4)]
            sC_v2 = [sem(f"sC_v2{s}") for s in range(4)]
            sC_ps = sem("sC_ps")
            sC_pr = sem("sC_pr")
            sC_vm = sem("sC_vm")
            sC_a = sem("sC_a")
            sC_vt = sem("sC_vt")
            sC_ae = sem("sC_ae")
            sC_vv = sem("sC_vv")
            sC_va = sem("sC_va")
            sC_sh = [sem(f"sC_sh{s}") for s in range(2)]
            sC_db = sem("sC_db")
            sC_vs = sem("sC_vs")

            blkC = ps.enter_context(nc.Block())

            @blkC.sync
            def _(s):
                if "B" not in phases:
                    s.dma_start(maskm[:, :], maskm_h[:, :]).then_inc(sBC_mk, 16)
                    for kc in range(KC):
                        s.dma_start(kt_sb[:, kc * 8192:(kc + 1) * 8192],
                                    kt_h[kc * 128:(kc + 1) * 128, :]).then_inc(sBC_kt, 16)
                if dbg:
                    s.dma_start(dbias_h[:, :], bias_st[:, :]).then_inc(sC_db, 16)

                def loads(qb):
                    if qb >= 2:
                        s.wait_ge(sC_pr, n_tile_c(qb - 1, 0))
                    for kc in range(KC):
                        s.dma_start(qt2[:, (qb % 2) * 2048 + kc * 256:
                                        (qb % 2) * 2048 + (kc + 1) * 256],
                                    qt_h[kc * 128:(kc + 1) * 128,
                                         qb * 256:(qb + 1) * 256]).then_inc(sC_q[qb % 2], 16)
                    for ic in range(2):
                        u = 2 * qb + ic
                        if qb >= 2:
                            s.wait_ge(sC_vt, 2 * (qb - 1))
                        s.dma_start(x2[:, (qb % 2) * 2048 + ic * 1024:
                                       (qb % 2) * 2048 + (ic + 1) * 1024],
                                    x_h[u * 128:(u + 1) * 128, :]).then_inc(sC_x2[(qb % 2) * 2 + ic], 16)

                def stores(qb):
                    for ic in range(2):
                        u = 2 * qb + ic
                        s.wait_ge(sC_va, u + 1)
                        s.dma_start(h1_h[u * 128:(u + 1) * 128, :],
                                    h1s[:, (u % 2) * 1024:(u % 2 + 1) * 1024]
                                    ).then_inc(sC_sh[u % 2], 16)

                loads(0)
                for qb in range(1, NQB2):
                    loads(qb)
                    stores(qb - 1)
                stores(NQB2 - 1)
                s.wait_ge(sC_sh[0], 16 * NQB2)
                s.wait_ge(sC_sh[1], 16 * NQB2)
                if dbg:
                    s.wait_ge(sC_db, 16)

            @blkC.gpsimd
            def _(g):
                for qb in range(NQB2):
                    for kb in range(2 * qb + 2):
                        n = n_tile_c(qb, kb)
                        if n >= 4:
                            g.wait_ge(sC_pr, n - 3)
                        g.dma_start(v2[:, (n % 4) * 1024:(n % 4 + 1) * 1024],
                                    v_h[kb * 128:(kb + 1) * 128, :]).then_inc(sC_v2[n % 4], 16)

            @blkC.tensor
            def _(t):
                t.wait_ge(sBC_kt, 16 * KC)

                def s_tile(qb, kb):
                    n = n_tile_c(qb, kb)
                    if n >= 2:
                        t.wait_ge(sC_a, n - 1)
                    for kc in range(KC):
                        mm = t.matmul(
                            ps2[n % 2][:, :],
                            kt_sb[:, kc * 8192 + kb * 128:kc * 8192 + (kb + 1) * 128],
                            qt2[:, (qb % 2) * 2048 + kc * 256:(qb % 2) * 2048 + (kc + 1) * 256],
                            start=(kc == 0), stop=(kc == KC - 1))
                    mm.then_inc(sC_ps, 1)

                def r_tile(qb, kb, nkb):
                    n = n_tile_c(qb, kb)
                    t.wait_ge(sC_a, n + 1)
                    t.wait_ge(sC_v2[n % 4], 16 * (n // 4 + 1))
                    if kb == 0:
                        t.wait_ge(sC_vt, 2 * qb)
                    for ic in range(2):
                        for h in range(2):
                            mm = t.matmul(
                                pr[ic][:, h * 512:(h + 1) * 512],
                                p2[:, (n % 2) * 256 + ic * 128:(n % 2) * 256 + (ic + 1) * 128],
                                v2[:, (n % 4) * 1024 + h * 512:(n % 4) * 1024 + (h + 1) * 512],
                                start=(kb == 0), stop=(kb == nkb - 1))
                    mm.then_inc(sC_pr, 1)

                for qb in range(NQB2):
                    nkb = 2 * qb + 2
                    t.wait_ge(sC_q[qb % 2], 128 * (qb // 2 + 1))
                    s_tile(qb, 0)
                    s_tile(qb, 1)
                    for kb in range(nkb - 2):
                        r_tile(qb, kb, nkb)
                        s_tile(qb, kb + 2)
                    r_tile(qb, nkb - 2, nkb)
                    r_tile(qb, nkb - 1, nkb)

            @blkC.vector
            def _(v):
                nvs = [0]

                def vself(instr):
                    nvs[0] += 1
                    instr.then_inc(sC_vs, 1)
                    v.wait_ge(sC_vs, nvs[0])

                v.wait_ge(sBC_mk, 16)
                for qb in range(NQB2):
                    nkb = 2 * qb + 2
                    for which, kb in enumerate((nkb - 2, nkb - 1)):
                        n = n_tile_c(qb, kb)
                        v.wait_ge(sC_ps, n + 1)
                        off = kb * 128 - qb * 256
                        sm = 384 - off
                        v.tensor_add(ps2[n % 2][:, :], ps2[n % 2][:, :],
                                     maskm[:, sm:sm + 256]).then_inc(sC_vm, 1)
                    for ic in range(2):
                        u = 2 * qb + ic
                        v.wait_ge(sC_pr, n_tile_c(qb, 0) + nkb)
                        v.wait_ge(sC_x2[(qb % 2) * 2 + ic], 16 * (qb // 2 + 1))
                        if u >= 4:
                            v.wait_ge(sC_ae, 3 * (u - 2) + 1)
                        v.scalar_tensor_tensor(
                            t1s[:, ic * 1024:(ic + 1) * 1024],
                            x2[:, (qb % 2) * 2048 + ic * 1024:(qb % 2) * 2048 + (ic + 1) * 1024],
                            2.0, pr[ic][:, :], op0=OP.mult, op1=OP.add).then_inc(sC_vt, 1)
                        v.wait_ge(sC_vt, u + 1)
                        c0 = ic * 8
                        vself(v.reduce_sum(st2[:, c0:c0 + 1],
                                           t1s[:, ic * 1024:(ic + 1) * 1024], axis=AX.X))
                        v.wait_ge(sC_ae, 3 * u + 1)
                        vself(v.tensor_scalar(st2[:, c0 + 2:c0 + 3], st2[:, c0:c0 + 1],
                                              1.0 / D, None, op0=OP.mult))
                        vself(v.tensor_tensor(st2[:, c0 + 3:c0 + 4], st2[:, c0 + 2:c0 + 3],
                                              st2[:, c0 + 2:c0 + 3], op=OP.mult))
                        v.scalar_tensor_tensor(st2[:, c0 + 4:c0 + 5], st2[:, c0 + 1:c0 + 2],
                                               1.0 / D, st2[:, c0 + 3:c0 + 4],
                                               op0=OP.mult, op1=OP.subtract).then_inc(sC_vv, 1)
                        v.wait_ge(sC_ae, 3 * u + 3)
                        if u >= 2:
                            v.wait_ge(sC_sh[u % 2], 16 * (u // 2))
                        v.tensor_scalar(h1s[:, (u % 2) * 1024:(u % 2 + 1) * 1024],
                                        t1s[:, ic * 1024:(ic + 1) * 1024],
                                        st2[:, c0 + 2:c0 + 3], st2[:, c0 + 6:c0 + 7],
                                        op0=OP.subtract, op1=OP.mult).then_inc(sC_va, 1)

            @blkC.scalar
            def _(sc):
                for qb in range(NQB2):
                    nkb = 2 * qb + 2
                    for kb in range(nkb):
                        n = n_tile_c(qb, kb)
                        sc.wait_ge(sC_ps, n + 1)
                        if kb >= nkb - 2:
                            sc.wait_ge(sC_vm, 2 * qb + (kb - (nkb - 2)) + 1)
                        if n >= 2:
                            sc.wait_ge(sC_pr, n - 1)
                        sc.activation(p2[:, (n % 2) * 256:(n % 2 + 1) * 256],
                                      ps2[n % 2][:, :], AF.Exp,
                                      bias=bias_st[:, kb:kb + 1], scale=1.0).then_inc(sC_a, 1)
                    for ic in range(2):
                        u = 2 * qb + ic
                        c0 = ic * 8
                        sc.wait_ge(sC_vt, u + 1)
                        sc.activation(sq2[:, :], t1s[:, ic * 1024:(ic + 1) * 1024], AF.Square,
                                      accum_out=st2[:, c0 + 1:c0 + 2]).then_inc(sC_ae, 1)
                        sc.wait_ge(sC_vv, u + 1)
                        sc.activation(st2[:, c0 + 5:c0 + 6], st2[:, c0 + 4:c0 + 5], AF.Ln,
                                      bias=eps_t[:, 0:1], scale=1.0).then_inc(sC_ae, 1)
                        sc.wait_ge(sC_ae, 3 * u + 2)
                        sc.activation(st2[:, c0 + 6:c0 + 7], st2[:, c0 + 5:c0 + 6], AF.Exp,
                                      scale=-0.5).then_inc(sC_ae, 1)

    if "B" in phases or "C" in phases:
        ps_bc.close()

    if "E" in phases:
        with ExitStack() as ps:
            wd_sb = ps.enter_context(nc.sbuf_tensor("wd_sb", [128, NPE * 4 * 1024], BF16))
            rw_sb = ps.enter_context(nc.sbuf_tensor("rw_sb", [128, 64], BF16))
            h1b = ps.enter_context(nc.sbuf_tensor("h1b", [128, 4 * 1024], BF16))
            h1t = ps.enter_context(nc.sbuf_tensor("h1t", [128, 2 * 4096], BF16))
            wgu = ps.enter_context(nc.sbuf_tensor("wgu", [128, 2 * 8192], BF16))
            a_sb = ps.enter_context(nc.sbuf_tensor("a_sb", [128, NPE * 4 * 512], BF16))
            wt_sb = ps.enter_context(nc.sbuf_tensor("wt_sb", [E, 512], BF16))
            wtil = ps.enter_context(nc.sbuf_tensor("wtil", [128, E * 512], BF16))
            sg_sb = ps.enter_context(nc.sbuf_tensor("sg_sb", [128, 2 * 512], BF16))
            ub_sb = ps.enter_context(nc.sbuf_tensor("ub_sb", [128, 512], BF16))
            h1f = ps.enter_context(nc.sbuf_tensor("h1f", [128, 2 * 1024], F32))
            t2s = ps.enter_context(nc.sbuf_tensor("t2s", [128, 1024], F32))
            st3 = ps.enter_context(nc.sbuf_tensor("st3", [128, 16], F32))
            wtb = ps.enter_context(nc.sbuf_tensor("wtb", [128, 4 * 8], BF16))
            lg_sb = ps.enter_context(nc.sbuf_tensor("lg_sb", [128, 8], F32))
            el_sb = ps.enter_context(nc.sbuf_tensor("el_sb", [128, 8], F32))
            rr = ps.enter_context(nc.sbuf_tensor("rr", [128, 32], F32))
            outs = ps.enter_context(nc.sbuf_tensor("outs", [128, 1024], F32))
            yA = ps.enter_context(nc.psum_tensor("yA", [128, 1024], F32))
            yB = ps.enter_context(nc.psum_tensor("yB", [128, 1024], F32))
            pg = ps.enter_context(nc.psum_tensor("pg", [128, 512], F32))
            pu = ps.enter_context(nc.psum_tensor("pu", [128, 512], F32))
            ptr2 = [ps.enter_context(nc.psum_tensor(f"ptr2_{i}", [128, 128], BF16))
                    for i in range(2)]

            sE_wrw = sem("sE_wrw")
            sE_wwd = sem("sE_wwd")
            sE_h1b = [sem(f"sE_h1b{s}") for s in range(4)]
            sE_wgu = [sem(f"sE_wgu{s}") for s in range(2)]
            sE_tr = sem("sE_tr")
            sE_trE = sem("sE_trE")
            sE_petc = sem("sE_petc")
            sE_pl = sem("sE_pl")
            sE_vcp = sem("sE_vcp")
            sE_ae2 = sem("sE_ae2")
            sE_vwt = sem("sE_vwt")
            sE_pw = sem("sE_pw")
            sE_aw = sem("sE_aw")
            sE_pg = sem("sE_pg")
            sE_pu = sem("sE_pu")
            sE_as = sem("sE_as")
            sE_vu = sem("sE_vu")
            sE_va = sem("sE_va")
            sE_py = sem("sE_py")
            sE_h1f = [sem(f"sE_h1f{s}") for s in range(2)]
            sE_vt2 = sem("sE_vt2")
            sE_ae3 = sem("sE_ae3")
            sE_vv3 = sem("sE_vv3")
            sE_vo = sem("sE_vo")
            sE_so = sem("sE_so")
            sE_vs = sem("sE_vs")

            blkE = ps.enter_context(nc.Block())

            def wgu_src(pe, kc, which):
                if pe < 8:
                    h = wg_h if which == 0 else wu_h
                    return h[pe * 1024 + kc * 128:pe * 1024 + (kc + 1) * 128, :]
                h = wsg_h if which == 0 else wsu_h
                c0 = (pe - 8) * 512
                return h[kc * 128:(kc + 1) * 128, c0:c0 + 512]

            def wd_src(pe, fc):
                if pe < 8:
                    return wd_h[pe * 512 + fc * 128:pe * 512 + (fc + 1) * 128, :]
                return wsd_h[(pe - 8) * 512 + fc * 128:(pe - 8) * 512 + (fc + 1) * 128, :]

            @blkE.gpsimd
            def _(g):
                if "C" in phases:
                    g.wait_ge(sC_sh[0], 16 * NQB2)
                    g.wait_ge(sC_sh[1], 16 * NQB2)
                for kc in range(KC):
                    g.dma_start(rw_sb[:, kc * 8:(kc + 1) * 8],
                                rw_h[kc * 128:(kc + 1) * 128, :]).then_inc(sE_wrw, 16)
                for gtb in range(4):
                    g.dma_start(h1b[:, (gtb % 4) * 1024:(gtb % 4 + 1) * 1024],
                                h1_h[gtb * 128:(gtb + 1) * 128, :]).then_inc(sE_h1b[gtb % 4], 16)
                for pe in range(2):
                    for which in range(2):
                        for kc in range(KC):
                            g.dma_start(
                                wgu[:, (pe % 2) * 8192 + which * 4096 + kc * 512:
                                    (pe % 2) * 8192 + which * 4096 + (kc + 1) * 512],
                                wgu_src(pe, kc, which)).then_inc(sE_wgu[pe % 2], 16)
                for pe in range(NPE):
                    for fc in range(4):
                        g.dma_start(wd_sb[:, (pe * 4 + fc) * 1024:(pe * 4 + fc + 1) * 1024],
                                    wd_src(pe, fc)).then_inc(sE_wwd, 16)
                for tc in range(NTCE):
                    for tbl in range(4):
                        gtb = tc * 4 + 4 + tbl
                        if gtb >= NTB:
                            continue
                        g.wait_ge(sE_tr, tc * 36 + (tbl + 1) * 8)
                        g.dma_start(h1b[:, (gtb % 4) * 1024:(gtb % 4 + 1) * 1024],
                                    h1_h[gtb * 128:(gtb + 1) * 128, :]).then_inc(sE_h1b[gtb % 4], 16)
                    for pe in range(NPE):
                        unit = tc * NPE + pe + 2
                        if unit >= NTCE * NPE:
                            continue
                        ntc, npe = unit // NPE, unit % NPE
                        g.wait_ge(sE_pu, 4 * (unit - 1))
                        for which in range(2):
                            for kc in range(KC):
                                g.dma_start(
                                    wgu[:, (unit % 2) * 8192 + which * 4096 + kc * 512:
                                        (unit % 2) * 8192 + which * 4096 + (kc + 1) * 512],
                                    wgu_src(npe, kc, which)).then_inc(sE_wgu[unit % 2], 16)

            @blkE.tensor
            def _(t):
                for tc in range(NTCE):
                    par = tc % 2
                    for tbl in range(4):
                        gtb = tc * 4 + tbl
                        t.wait_ge(sE_h1b[gtb % 4], 16 * (gtb // 4 + 1))
                        for dc in range(8):
                            i3 = tc * 36 + tbl * 8 + dc
                            if i3 >= 2:
                                t.wait_ge(sE_trE, i3 - 1)
                            t.transpose(
                                ptr2[i3 % 2][:, :],
                                h1b[:, (gtb % 4) * 1024 + dc * 128:(gtb % 4) * 1024 + (dc + 1) * 128],
                                ident[:, :]).then_inc(sE_tr, 1)
                    t.wait_ge(sE_trE, tc * 36 + 32)
                    if tc == 0:
                        t.wait_ge(sE_wrw, 16 * KC)
                    for tbl in range(4):
                        idx = tc * 4 + tbl
                        if idx >= 1:
                            t.wait_ge(sE_vcp, idx)
                        if tc >= 1 and tbl == 0:
                            t.wait_ge(sE_as, 40 * tc)
                        for kc in range(KC):
                            mm = t.matmul(
                                pg[:, 0:8],
                                h1t[:, par * 4096 + kc * 512 + tbl * 128:
                                    par * 4096 + kc * 512 + (tbl + 1) * 128],
                                rw_sb[:, kc * 8:(kc + 1) * 8],
                                start=(kc == 0), stop=(kc == KC - 1))
                        mm.then_inc(sE_pl, 1)
                    for tbl in range(4):
                        idx = tc * 4 + tbl
                        i3 = tc * 36 + 32 + tbl
                        t.wait_ge(sE_vwt, idx + 1)
                        if i3 >= 2:
                            t.wait_ge(sE_trE, i3 - 1)
                        t.transpose(ptr2[i3 % 2][0:8, :], wtb[:, tbl * 8:(tbl + 1) * 8],
                                    ident[:, :]).then_inc(sE_tr, 1)
                    t.wait_ge(sE_trE, 36 * (tc + 1))
                    if tc >= 1:
                        t.wait_ge(sE_vu, 40 * tc)
                    for e in range(E):
                        if e >= 2:
                            t.wait_ge(sE_aw, 8 * tc + e - 1)
                        dst = pg if e % 2 == 0 else pu
                        mm = t.matmul(dst[:, :], sel[:, e * 128:(e + 1) * 128],
                                      wt_sb[:, :], start=True, stop=True)
                        mm.then_inc(sE_pw, 1)
                    for pe in range(NPE):
                        unit = tc * NPE + pe
                        w0 = tc * 40 + pe * 4
                        t.wait_ge(sE_wgu[unit % 2], 256 * (unit // 2 + 1))
                        for fc in range(4):
                            w = w0 + fc
                            if pe == 0 and fc == 0:
                                t.wait_ge(sE_aw, 8 * (tc + 1))
                            if w >= 1:
                                t.wait_ge(sE_as, w)
                            for kc in range(KC):
                                mm = t.matmul(
                                    pg[:, :],
                                    wgu[:, (unit % 2) * 8192 + kc * 512 + fc * 128:
                                        (unit % 2) * 8192 + kc * 512 + (fc + 1) * 128],
                                    h1t[:, par * 4096 + kc * 512:par * 4096 + (kc + 1) * 512],
                                    start=(kc == 0), stop=(kc == KC - 1))
                            mm.then_inc(sE_pg, 1)
                            if w >= 1:
                                t.wait_ge(sE_vu, w)
                            for kc in range(KC):
                                mm = t.matmul(
                                    pu[:, :],
                                    wgu[:, (unit % 2) * 8192 + 4096 + kc * 512 + fc * 128:
                                        (unit % 2) * 8192 + 4096 + kc * 512 + (fc + 1) * 128],
                                    h1t[:, par * 4096 + kc * 512:par * 4096 + (kc + 1) * 512],
                                    start=(kc == 0), stop=(kc == KC - 1))
                            mm.then_inc(sE_pu, 1)
                            if pe == NPE - 1 and fc == 3:
                                t.nop().then_inc(sE_petc, 1)
                    t.wait_ge(sE_va, 40 * (tc + 1))
                    if tc == 0:
                        t.wait_ge(sE_wwd, 16 * NPE * 4)
                    for tbl in range(4):
                        gtb = tc * 4 + tbl
                        y = yA if tbl % 2 == 0 else yB
                        if gtb >= 2:
                            t.wait_ge(sE_vt2, gtb - 1)
                        for pe in range(NPE):
                            for fc in range(4):
                                for h in range(2):
                                    mm = t.matmul(
                                        y[:, h * 512:(h + 1) * 512],
                                        a_sb[:, (pe * 4 + fc) * 512 + tbl * 128:
                                             (pe * 4 + fc) * 512 + (tbl + 1) * 128],
                                        wd_sb[:, (pe * 4 + fc) * 1024 + h * 512:
                                              (pe * 4 + fc) * 1024 + (h + 1) * 512],
                                        start=(pe == 0 and fc == 0),
                                        stop=(pe == NPE - 1 and fc == 3))
                        mm.then_inc(sE_py, 1)

            @blkE.vector
            def _(v):
                nvs = [0]

                def vself(instr):
                    nvs[0] += 1
                    instr.then_inc(sE_vs, 1)
                    v.wait_ge(sE_vs, nvs[0])

                for tc in range(NTCE):
                    par = tc % 2
                    for tbl in range(4):
                        for dc in range(8):
                            i3 = tc * 36 + tbl * 8 + dc
                            v.wait_ge(sE_tr, i3 + 1)
                            if tbl == 0 and dc == 0 and tc >= 2:
                                v.wait_ge(sE_petc, tc - 1)
                            v.tensor_copy(
                                h1t[:, par * 4096 + dc * 512 + tbl * 128:
                                    par * 4096 + dc * 512 + (tbl + 1) * 128],
                                ptr2[i3 % 2][:, :]).then_inc(sE_trE, 1)
                    for tbl in range(4):
                        idx = tc * 4 + tbl
                        v.wait_ge(sE_pl, idx + 1)
                        vself(v.tensor_copy(lg_sb[:, :], pg[:, 0:8]))
                        v.reduce_max(rr[:, 0:1], lg_sb[:, :], axis=AX.X,
                                     negate=True).then_inc(sE_vcp, 1)
                        v.wait_ge(sE_ae2, idx + 1)
                        vself(v.reciprocal(rr[:, 2:3], rr[:, 1:2]))
                        vself(v.tensor_scalar(rr[:, 8:16], el_sb[:, :], rr[:, 2:3], None,
                                              op0=OP.mult))
                        vself(v.tensor_reduce(rr[:, 3:4], rr[:, 8:16], axis=AX.X, op=OP.min))
                        vself(v.tensor_scalar(rr[:, 16:24], rr[:, 8:16], rr[:, 3:4], None,
                                              op0=OP.is_le))
                        vself(v.scalar_tensor_tensor(rr[:, 16:24], rr[:, 16:24], BIG,
                                                     rr[:, 8:16], op0=OP.mult, op1=OP.add))
                        vself(v.tensor_reduce(rr[:, 4:5], rr[:, 16:24], axis=AX.X, op=OP.min))
                        vself(v.tensor_scalar(rr[:, 24:32], rr[:, 16:24], rr[:, 4:5], None,
                                              op0=OP.is_le))
                        vself(v.scalar_tensor_tensor(rr[:, 24:32], rr[:, 24:32], BIG,
                                                     rr[:, 16:24], op0=OP.mult, op1=OP.add))
                        vself(v.tensor_reduce(rr[:, 5:6], rr[:, 24:32], axis=AX.X, op=OP.min))
                        vself(v.tensor_scalar(rr[:, 16:24], rr[:, 8:16], rr[:, 5:6], None,
                                              op0=OP.is_ge))
                        v.tensor_tensor(wtb[:, tbl * 8:(tbl + 1) * 8], rr[:, 8:16],
                                        rr[:, 16:24], op=OP.mult).then_inc(sE_vwt, 1)
                    for tbl in range(4):
                        i3 = tc * 36 + 32 + tbl
                        v.wait_ge(sE_tr, i3 + 1)
                        v.tensor_copy(wt_sb[:, tbl * 128:(tbl + 1) * 128],
                                      ptr2[i3 % 2][0:8, :]).then_inc(sE_trE, 1)
                    for pe in range(NPE):
                        for fc in range(4):
                            w = tc * 40 + pe * 4 + fc
                            v.wait_ge(sE_pu, w + 1)
                            if w >= 1:
                                v.wait_ge(sE_va, w)
                            if pe == 0 and fc == 0 and tc >= 1:
                                v.wait_ge(sE_py, 4 * tc)
                            if pe < 8:
                                v.wait_ge(sE_aw, 8 * tc + pe + 1)
                                v.tensor_tensor(ub_sb[:, :],
                                                pu[:, :], wtil[:, pe * 512:(pe + 1) * 512],
                                                op=OP.mult).then_inc(sE_vu, 1)
                            else:
                                v.tensor_copy(ub_sb[:, :],
                                              pu[:, :]).then_inc(sE_vu, 1)
                            v.wait_ge(sE_vu, w + 1)
                            v.wait_ge(sE_as, w + 1)
                            v.tensor_tensor(a_sb[:, (pe * 4 + fc) * 512:
                                                 (pe * 4 + fc + 1) * 512],
                                            sg_sb[:, (w % 2) * 512:(w % 2 + 1) * 512],
                                            ub_sb[:, :],
                                            op=OP.mult).then_inc(sE_va, 1)
                    for tbl in range(4):
                        gtb = tc * 4 + tbl
                        y = yA if tbl % 2 == 0 else yB
                        c0 = (gtb % 2) * 8
                        v.wait_ge(sE_py, gtb + 1)
                        v.wait_ge(sE_h1f[gtb % 2], 16 * (gtb // 2 + 1))
                        if gtb >= 1:
                            v.wait_ge(sE_ae3, 3 * (gtb - 1) + 1)
                            v.wait_ge(sE_vo, gtb)
                        v.tensor_add(t2s[:, :],
                                     y[:, :],
                                     h1f[:, (gtb % 2) * 1024:(gtb % 2 + 1) * 1024]
                                     ).then_inc(sE_vt2, 1)
                        v.wait_ge(sE_vt2, gtb + 1)
                        vself(v.reduce_sum(st3[:, c0:c0 + 1], t2s[:, :], axis=AX.X))
                        v.wait_ge(sE_ae3, 3 * gtb + 1)
                        vself(v.tensor_scalar(st3[:, c0 + 2:c0 + 3], st3[:, c0:c0 + 1],
                                              1.0 / D, None, op0=OP.mult))
                        vself(v.tensor_tensor(st3[:, c0 + 3:c0 + 4], st3[:, c0 + 2:c0 + 3],
                                              st3[:, c0 + 2:c0 + 3], op=OP.mult))
                        v.scalar_tensor_tensor(st3[:, c0 + 4:c0 + 5], st3[:, c0 + 1:c0 + 2],
                                               1.0 / D, st3[:, c0 + 3:c0 + 4],
                                               op0=OP.mult, op1=OP.subtract).then_inc(sE_vv3, 1)
                        v.wait_ge(sE_ae3, 3 * gtb + 3)
                        if gtb >= 1:
                            v.wait_ge(sE_so, 16 * gtb)
                        v.tensor_scalar(outs[:, :],
                                        t2s[:, :],
                                        st3[:, c0 + 2:c0 + 3], st3[:, c0 + 6:c0 + 7],
                                        op0=OP.subtract, op1=OP.mult).then_inc(sE_vo, 1)

            @blkE.scalar
            def _(sc):
                for tc in range(NTCE):
                    for tbl in range(4):
                        idx = tc * 4 + tbl
                        sc.wait_ge(sE_vcp, idx + 1)
                        sc.activation(el_sb[:, :], lg_sb[:, :], AF.Exp,
                                      bias=rr[:, 0:1], scale=1.0,
                                      accum_out=rr[:, 1:2]).then_inc(sE_ae2, 1)
                    for e in range(E):
                        sc.wait_ge(sE_pw, 8 * tc + e + 1)
                        if e == 0 and tc >= 1:
                            sc.wait_ge(sE_vu, 40 * tc)
                        src = pg if e % 2 == 0 else pu
                        sc.activation(wtil[:, e * 512:(e + 1) * 512], src[:, :],
                                      AF.Copy).then_inc(sE_aw, 1)
                    for pe in range(NPE):
                        for fc in range(4):
                            w = tc * 40 + pe * 4 + fc
                            sc.wait_ge(sE_pg, w + 1)
                            if w >= 2:
                                sc.wait_ge(sE_va, w - 1)
                            sc.activation(sg_sb[:, (w % 2) * 512:(w % 2 + 1) * 512],
                                          pg[:, :], AF.Silu).then_inc(sE_as, 1)
                    for tbl in range(4):
                        gtb = tc * 4 + tbl
                        c0 = (gtb % 2) * 8
                        sc.wait_ge(sE_vt2, gtb + 1)
                        if gtb >= 1:
                            sc.wait_ge(sE_so, 16 * gtb)
                        sc.activation(outs[:, :], t2s[:, :], AF.Square,
                                      accum_out=st3[:, c0 + 1:c0 + 2]).then_inc(sE_ae3, 1)
                        sc.wait_ge(sE_vv3, gtb + 1)
                        sc.activation(st3[:, c0 + 5:c0 + 6], st3[:, c0 + 4:c0 + 5], AF.Ln,
                                      bias=eps_t[:, 0:1], scale=1.0).then_inc(sE_ae3, 1)
                        sc.wait_ge(sE_ae3, 3 * gtb + 2)
                        sc.activation(st3[:, c0 + 6:c0 + 7], st3[:, c0 + 5:c0 + 6], AF.Exp,
                                      scale=-0.5).then_inc(sE_ae3, 1)

            @blkE.sync
            def _(s):
                for tc in range(NTCE):
                    for tbl in range(4):
                        gtb = tc * 4 + tbl
                        if gtb >= 2:
                            s.wait_ge(sE_vt2, gtb - 1)
                        s.dma_start(h1f[:, (gtb % 2) * 1024:(gtb % 2 + 1) * 1024],
                                    h1_h[gtb * 128:(gtb + 1) * 128, :]).then_inc(sE_h1f[gtb % 2], 16)
                    for tbl in range(4):
                        gtb = tc * 4 + tbl
                        s.wait_ge(sE_vo, gtb + 1)
                        s.dma_start(out_h[gtb * 128:(gtb + 1) * 128, :],
                                    outs[:, :]).then_inc(sE_so, 16)
                if dbg:
                    s.wait_ge(sE_vwt, 64)
                s.wait_ge(sE_so, 16 * 64)

            if dbg:
                pass

    es.close()
    return nc


def make_consts():
    ident = np.eye(128, dtype=ml_dtypes.bfloat16)
    p = np.arange(128)[:, None]
    m = np.arange(896)[None, :]
    maskm = np.where(m >= p + 384, 0.0, NEGINF).astype(np.float32)
    sel = np.zeros((E, E * 128), dtype=ml_dtypes.bfloat16)
    for e in range(E):
        sel[e, e * 128:(e + 1) * 128] = 1.0
    return ident, maskm, sel


def kernel(**inputs):
    f = lambda k: np.ascontiguousarray(np.asarray(inputs[k], dtype=np.float32))
    x = f("L0_attn_minibatch")
    wq, wk, wv = f("L0_attn_Wq"), f("L0_attn_Wk"), f("L0_attn_Wv")
    bq = f("L0_attn_bq").reshape(D, 1)
    bk = f("L0_attn_bk").reshape(D, 1)
    bv = f("L0_attn_bv").reshape(1, D)
    rw = f("L0_ffn_router_w")
    wg = f("L0_ffn_w_gate").reshape(E * D, FF)
    wu = f("L0_ffn_w_up").reshape(E * D, FF)
    wd = f("L0_ffn_w_down").reshape(E * FF, D)
    wsg, wsu, wsd = f("L0_ffn_ws_gate"), f("L0_ffn_ws_up"), f("L0_ffn_ws_down")
    ident, maskm, sel = make_consts()

    nc = build_program()
    base = dict(wq=wq, wk=wk, wv=wv, bq=bq, bk=bk, bv=bv, rw=rw, wg=wg, wu=wu,
                wd=wd, wsg=wsg, wsu=wsu, wsd=wsd, ident=ident, maskm=maskm, sel=sel)
    in_maps = [dict(base, x=np.ascontiguousarray(x[b])) for b in range(8)]
    res = run_bass_kernel_spmd(nc, in_maps, core_ids=list(range(8)))
    return np.stack([np.asarray(r["out"]) for r in res.results], axis=0)


# version 13
# speedup vs baseline: 1.2335x; 1.1158x over previous
import numpy as np
import ml_dtypes
import concourse.bass as bass
import concourse.mybir as mybir
from concourse.bass_utils import run_bass_kernel_spmd
from concourse.alu_op_type import AluOpType as OP
from contextlib import ExitStack

F32 = mybir.dt.float32
BF16 = mybir.dt.bfloat16
F16 = mybir.dt.float16
AF = mybir.ActivationFunctionType
AX = mybir.AxisListType

T, D, E, FF, FS = 8192, 1024, 8, 512, 1024
KC = 8
NTB = 64
NTCA = 16
NQT1 = 16
NKB = 64
QB2, NQB2 = 256, 32
TCE, NTCE = 512, 16
NPE = 10
NEGINF = -30000.0
BIG = 1.0e9
EPS = 1e-5


def n_tile_b(qt, kb):
    return 2 * qt * (qt + 1) + kb


def n_tile_c(qb, kb):
    return qb * (qb + 1) + kb


NB_TOT = n_tile_b(15, 0) + 64
NC_TOT = 32 * 33


def build_program(phases=("A", "B", "C", "E"), dbg=False):
    nc = bass.Bass("TRN2", target_bir_lowering=False)

    x_h = nc.dram_tensor("x", [T, D], F32, kind="ExternalInput").ap()
    wq_h = nc.dram_tensor("wq", [D, D], F32, kind="ExternalInput").ap()
    wk_h = nc.dram_tensor("wk", [D, D], F32, kind="ExternalInput").ap()
    wv_h = nc.dram_tensor("wv", [D, D], F32, kind="ExternalInput").ap()
    bq_h = nc.dram_tensor("bq", [D, 1], F32, kind="ExternalInput").ap()
    bk_h = nc.dram_tensor("bk", [D, 1], F32, kind="ExternalInput").ap()
    bv_h = nc.dram_tensor("bv", [1, D], F32, kind="ExternalInput").ap()
    rw_h = nc.dram_tensor("rw", [D, E], F32, kind="ExternalInput").ap()
    wg_h = nc.dram_tensor("wg", [E * D, FF], F32, kind="ExternalInput").ap()
    wu_h = nc.dram_tensor("wu", [E * D, FF], F32, kind="ExternalInput").ap()
    wd_h = nc.dram_tensor("wd", [E * FF, D], F32, kind="ExternalInput").ap()
    wsg_h = nc.dram_tensor("wsg", [D, FS], F32, kind="ExternalInput").ap()
    wsu_h = nc.dram_tensor("wsu", [D, FS], F32, kind="ExternalInput").ap()
    wsd_h = nc.dram_tensor("wsd", [FS, D], F32, kind="ExternalInput").ap()
    ident_h = nc.dram_tensor("ident", [128, 128], BF16, kind="ExternalInput").ap()
    maskm_h = nc.dram_tensor("maskm", [128, 896], F32, kind="ExternalInput").ap()
    sel_h = nc.dram_tensor("sel", [E, E * 128], BF16, kind="ExternalInput").ap()

    out_h = nc.dram_tensor("out", [T, D], F32, kind="ExternalOutput").ap()

    sk = "ExternalOutput" if dbg else "Internal"
    s_h = nc.dram_tensor("s_s", [NB_TOT * 128, 512], F16, kind="Internal").ap()
    qt_h = nc.dram_tensor("qt_s", [D, T], BF16, kind=sk).ap()
    kt_h = nc.dram_tensor("kt_s", [D, T], BF16, kind=sk).ap()
    v_h = nc.dram_tensor("v_s", [T, D], BF16, kind=sk).ap()
    h1_h = nc.dram_tensor("h1_s", [T, D], F32, kind=sk).ap()
    if dbg:
        dbias_h = nc.dram_tensor("d_bias", [128, NKB], F32, kind="ExternalOutput").ap()
        dmn_h = nc.dram_tensor("d_mn", [128, NKB * 16], F32, kind="ExternalOutput").ap()
        de_h = nc.dram_tensor("d_e", [128, NKB * 16], F32, kind="ExternalOutput").ap()
        ds_h = nc.dram_tensor("d_s", [128, NKB], F32, kind="ExternalOutput").ap()

    es = ExitStack()
    sem = lambda name: es.enter_context(nc.semaphore(name))

    ident = es.enter_context(nc.sbuf_tensor("ident_s", [128, 128], BF16))
    sel = es.enter_context(nc.sbuf_tensor("sel_s", [E, E * 128], BF16))
    ones1 = es.enter_context(nc.sbuf_tensor("ones1", [1, 128], BF16))
    eps_t = es.enter_context(nc.sbuf_tensor("eps_t", [128, 1], F32))
    bias_st = es.enter_context(nc.sbuf_tensor("bias_st", [128, NKB], F32))

    sA_c = sem("sA_c")

    if "A" in phases:
        with ExitStack() as ps:
            xa = ps.enter_context(nc.sbuf_tensor("xa", [128, 4 * 1024], BF16))
            wq_s = ps.enter_context(nc.sbuf_tensor("wq_s", [128, 8 * 1024], BF16))
            wk_s = ps.enter_context(nc.sbuf_tensor("wk_s", [128, 8 * 1024], BF16))
            wv_s = ps.enter_context(nc.sbuf_tensor("wv_s", [128, 8 * 1024], BF16))
            bq_s = ps.enter_context(nc.sbuf_tensor("bq_s", [128, 8], F32))
            bk_s = ps.enter_context(nc.sbuf_tensor("bk_s", [128, 8], F32))
            bv_s = ps.enter_context(nc.sbuf_tensor("bv_s", [1, 1024], BF16))
            xt = ps.enter_context(nc.sbuf_tensor("xt", [128, 2 * 4096], BF16))
            qs = ps.enter_context(nc.sbuf_tensor("qs", [128, 2 * 512], BF16))
            ks = ps.enter_context(nc.sbuf_tensor("ks", [128, 2 * 512], BF16))
            vs = ps.enter_context(nc.sbuf_tensor("vs", [128, 2 * 1024], BF16))
            pq = [ps.enter_context(nc.psum_tensor(f"pq{i}", [128, 512], F32)) for i in range(2)]
            pk = [ps.enter_context(nc.psum_tensor(f"pk{i}", [128, 512], F32)) for i in range(2)]
            pv = ps.enter_context(nc.psum_tensor("pv", [128, 1024], F32))
            ptr = [ps.enter_context(nc.psum_tensor(f"ptr{i}", [128, 128], BF16)) for i in range(2)]

            sA_w = sem("sA_w")
            sA_ms = sem("sA_ms")
            sA_x = [sem(f"sA_x{s}") for s in range(4)]
            sA_tr = sem("sA_tr")
            sA_trE = sem("sA_trE")
            sA_mm = sem("sA_mm")
            sA_pq = sem("sA_pq")
            sA_pk = sem("sA_pk")
            sA_pv = sem("sA_pv")
            sA_evq = sem("sA_evq")
            sA_evk = sem("sA_evk")
            sA_evv = sem("sA_evv")
            sA_stq = [sem(f"sA_stq{s}") for s in range(2)]
            sA_stk = [sem(f"sA_stk{s}") for s in range(2)]
            sA_stv = [sem(f"sA_stv{s}") for s in range(2)]

            blkA = ps.enter_context(nc.Block())

            @blkA.gpsimd
            def _(g):
                g.dma_start(ident[:, :], ident_h[:, :]).then_inc(sA_c, 16)
                g.dma_start(sel[:, :], sel_h[:, :]).then_inc(sA_c, 16)
                for t_i in range(4):
                    g.dma_start(xa[:, (t_i % 4) * 1024:(t_i % 4 + 1) * 1024],
                                x_h[t_i * 128:(t_i + 1) * 128, :]).then_inc(sA_x[t_i % 4], 16)
                for kc in range(KC):
                    g.dma_start(wq_s[:, kc * 1024:(kc + 1) * 1024],
                                wq_h[kc * 128:(kc + 1) * 128, :]).then_inc(sA_w, 16)
                for kc in range(KC):
                    g.dma_start(wk_s[:, kc * 1024:(kc + 1) * 1024],
                                wk_h[kc * 128:(kc + 1) * 128, :]).then_inc(sA_w, 16)
                for kc in range(KC):
                    g.dma_start(wv_s[:, kc * 1024:(kc + 1) * 1024],
                                wv_h[kc * 128:(kc + 1) * 128, :]).then_inc(sA_w, 16)
                for dc in range(8):
                    g.dma_start(bq_s[:, dc:dc + 1],
                                bq_h[dc * 128:(dc + 1) * 128, 0:1]).then_inc(sA_w, 16)
                for dc in range(8):
                    g.dma_start(bk_s[:, dc:dc + 1],
                                bk_h[dc * 128:(dc + 1) * 128, 0:1]).then_inc(sA_w, 16)
                g.dma_start(bv_s[:, :], bv_h[:, :]).then_inc(sA_w, 16)
                g.memset(ones1[:, :], 1.0).then_inc(sA_ms, 1)
                g.memset(eps_t[:, :], EPS).then_inc(sA_ms, 1)
                for t_i in range(4, NTB):
                    g.wait_ge(sA_tr, 8 * (t_i - 4) + 8)
                    g.dma_start(xa[:, (t_i % 4) * 1024:(t_i % 4 + 1) * 1024],
                                x_h[t_i * 128:(t_i + 1) * 128, :]).then_inc(sA_x[t_i % 4], 16)

            W_TOT = 41 * 16

            def emit_tr(t, tc):
                for tbl in range(4):
                    t_i = tc * 4 + tbl
                    t.wait_ge(sA_x[t_i % 4], 16 * (t_i // 4 + 1))
                    for dc in range(8):
                        i = t_i * 8 + dc
                        if i >= 2:
                            t.wait_ge(sA_trE, i - 1)
                        t.transpose(
                            ptr[i % 2][:, :],
                            xa[:, (t_i % 4) * 1024 + dc * 128:(t_i % 4) * 1024 + (dc + 1) * 128],
                            ident[:, :]).then_inc(sA_tr, 1)

            @blkA.tensor
            def _(t):
                t.wait_ge(sA_c, 32)
                emit_tr(t, 0)
                for tc in range(NTCA):
                    par = tc % 2
                    if tc + 1 < NTCA:
                        emit_tr(t, tc + 1)
                    t.wait_ge(sA_trE, 32 * (tc + 1))
                    if tc == 0:
                        t.wait_ge(sA_w, W_TOT)
                        t.wait_ge(sA_ms, 2)

                    def q_group(dc, which):
                        ws, pp, sev, spp = (wq_s, pq, sA_evq, sA_pq) if which == 0 else (
                            wk_s, pk, sA_evk, sA_pk)
                        g = tc * 8 + dc
                        if g >= 2:
                            t.wait_ge(sev, g - 1)
                        for kc in range(KC):
                            mm = t.matmul(
                                pp[g % 2][:, :],
                                ws[:, kc * 1024 + dc * 128:kc * 1024 + (dc + 1) * 128],
                                xt[:, par * 4096 + kc * 512:par * 4096 + (kc + 1) * 512],
                                start=(kc == 0), stop=(kc == KC - 1))
                        mm.then_inc(spp, 1)
                        return mm

                    def v_group(tbl):
                        vi = tc * 4 + tbl
                        if vi >= 1:
                            t.wait_ge(sA_evv, vi)
                        for kc in range(KC):
                            for h in range(2):
                                t.matmul(
                                    pv[:, h * 512:(h + 1) * 512],
                                    xt[:, par * 4096 + kc * 512 + tbl * 128:
                                       par * 4096 + kc * 512 + (tbl + 1) * 128],
                                    wv_s[:, kc * 1024 + h * 512:kc * 1024 + (h + 1) * 512],
                                    start=(kc == 0), stop=False)
                        for h in range(2):
                            mm = t.matmul(pv[:, h * 512:(h + 1) * 512], ones1[:, :],
                                          bv_s[:, h * 512:(h + 1) * 512],
                                          start=False, stop=True)
                        mm.then_inc(sA_pv, 1)
                        return mm

                    last = None
                    for dc in range(4):
                        last = q_group(dc, 0)
                    last = v_group(0)
                    for dc in range(4, 8):
                        last = q_group(dc, 0)
                    last = v_group(1)
                    for dc in range(4):
                        last = q_group(dc, 1)
                    last = v_group(2)
                    for dc in range(4, 8):
                        last = q_group(dc, 1)
                    last = v_group(3)
                    t.nop().then_inc(sA_mm, 1)

            @blkA.vector
            def _(v):
                for tc in range(NTCA + 1):
                    if tc >= NTCA:
                        break
                    for tbl in range(4):
                        t_i = tc * 4 + tbl
                        for dc in range(8):
                            i = t_i * 8 + dc
                            v.wait_ge(sA_tr, i + 1)
                            if tbl == 0 and dc == 0 and tc >= 2:
                                v.wait_ge(sA_mm, tc - 1)
                            v.tensor_copy(
                                xt[:, (tc % 2) * 4096 + dc * 512 + tbl * 128:
                                   (tc % 2) * 4096 + dc * 512 + (tbl + 1) * 128],
                                ptr[i % 2][:, :]).then_inc(sA_trE, 1)

            @blkA.scalar
            def _(sc):
                for tc in range(NTCA):
                    def q_evac(dc, which):
                        ws_stage, pp, bb, spp, sev, sst = (
                            (qs, pq, bq_s, sA_pq, sA_evq, sA_stq) if which == 0
                            else (ks, pk, bk_s, sA_pk, sA_evk, sA_stk))
                        g = tc * 8 + dc
                        sc.wait_ge(spp, g + 1)
                        if g >= 2:
                            sc.wait_ge(sst[g % 2], 16 * (g // 2))
                        sc.activation(ws_stage[:, (g % 2) * 512:(g % 2 + 1) * 512],
                                      pp[g % 2][:, :], AF.Identity,
                                      bias=bb[:, dc:dc + 1], scale=1.0).then_inc(sev, 1)

                    def v_evac(tbl):
                        vi = tc * 4 + tbl
                        sc.wait_ge(sA_pv, vi + 1)
                        if vi >= 2:
                            sc.wait_ge(sA_stv[vi % 2], 16 * (vi // 2))
                        sc.activation(vs[:, (vi % 2) * 1024:(vi % 2 + 1) * 1024],
                                      pv[:, :], AF.Copy).then_inc(sA_evv, 1)

                    for dc in range(4):
                        q_evac(dc, 0)
                    v_evac(0)
                    for dc in range(4, 8):
                        q_evac(dc, 0)
                    v_evac(1)
                    for dc in range(4):
                        q_evac(dc, 1)
                    v_evac(2)
                    for dc in range(4, 8):
                        q_evac(dc, 1)
                    v_evac(3)

            @blkA.sync
            def _(s):
                for tc in range(NTCA):
                    def q_store(dc, which):
                        stage, dst, sev, sst = (qs, qt_h, sA_evq, sA_stq) if which == 0 else (
                            ks, kt_h, sA_evk, sA_stk)
                        g = tc * 8 + dc
                        s.wait_ge(sev, g + 1)
                        s.dma_start(dst[dc * 128:(dc + 1) * 128, tc * 512:(tc + 1) * 512],
                                    stage[:, (g % 2) * 512:(g % 2 + 1) * 512]).then_inc(sst[g % 2], 16)

                    def v_store(tbl):
                        vi = tc * 4 + tbl
                        s.wait_ge(sA_evv, vi + 1)
                        s.dma_start(v_h[vi * 128:(vi + 1) * 128, :],
                                    vs[:, (vi % 2) * 1024:(vi % 2 + 1) * 1024]).then_inc(sA_stv[vi % 2], 16)

                    for dc in range(4):
                        q_store(dc, 0)
                    v_store(0)
                    for dc in range(4, 8):
                        q_store(dc, 0)
                    v_store(1)
                    for dc in range(4):
                        q_store(dc, 1)
                    v_store(2)
                    for dc in range(4, 8):
                        q_store(dc, 1)
                    v_store(3)
                for s2 in range(2):
                    s.wait_ge(sA_stq[s2], 16 * 64)
                    s.wait_ge(sA_stk[s2], 16 * 64)
                    s.wait_ge(sA_stv[s2], 16 * 32)

    if "B" in phases:
        with ExitStack() as ps:
            kt_sb = ps.enter_context(nc.sbuf_tensor("kt_sb", [128, 8 * 8192], BF16))
            maskm = ps.enter_context(nc.sbuf_tensor("maskm_s", [128, 896], F32))
            qt1 = ps.enter_context(nc.sbuf_tensor("qt1", [128, 2 * 4096], BF16))
            mneg_st = ps.enter_context(nc.sbuf_tensor("mneg_st", [128, NKB * 16], F32))
            e_st = ps.enter_context(nc.sbuf_tensor("e_st", [128, NKB * 16], F32))
            tmp_st = ps.enter_context(nc.sbuf_tensor("tmp_st", [128, NKB * 16], F32))
            sp_st = ps.enter_context(nc.sbuf_tensor("sp_st", [128, 4 * 512], F16))
            sc1 = ps.enter_context(nc.sbuf_tensor("sc1", [128, 512], BF16))
            mnegf_st = ps.enter_context(nc.sbuf_tensor("mnegf_st", [128, NKB], F32))
            s_st = ps.enter_context(nc.sbuf_tensor("s_st", [128, NKB], F32))
            lns_st = ps.enter_context(nc.sbuf_tensor("lns_st", [128, NKB], F32))
            ps1 = [ps.enter_context(nc.psum_tensor(f"ps1_{i}", [128, 512], F32)) for i in range(4)]

            sBC_kt = sem("sBC_kt")
            sBC_mk = sem("sBC_mk")
            sB_q = [sem(f"sB_q{s}") for s in range(2)]
            sB_db = sem("sB_db") if dbg else None
            sB_pe = sem("sB_pe")
            sB_v = sem("sB_v")
            sB_a = sem("sB_a")
            sB_cp = sem("sB_cp")
            sB_sp = [sem(f"sB_sp{s}") for s in range(4)]
            sB_vf = sem("sB_vf")
            sB_af = sem("sB_af")
            sB_vf2 = sem("sB_vf2")
            sB_af2 = sem("sB_af2")
            sB_fin = sem("sB_fin")
            sB_vs = sem("sB_vs")

            blkB = ps.enter_context(nc.Block())

            @blkB.gpsimd
            def _(g):
                for qt in range(NQT1):
                    for kb in range(4 * qt + 4):
                        n = n_tile_b(qt, kb)
                        g.wait_ge(sB_cp, n + 1)
                        g.dma_start(s_h[n * 128:(n + 1) * 128, :],
                                    sp_st[:, (n % 4) * 512:(n % 4 + 1) * 512]
                                    ).then_inc(sB_sp[n % 4], 16)

            @blkB.sync
            def _(s):
                s.dma_start(maskm[:, :], maskm_h[:, :]).then_inc(sBC_mk, 16)
                for kc in range(KC):
                    s.dma_start(kt_sb[:, kc * 8192:(kc + 1) * 8192],
                                kt_h[kc * 128:(kc + 1) * 128, :]).then_inc(sBC_kt, 16)

                def loads(qt):
                    if qt >= 2:
                        s.wait_ge(sB_pe, n_tile_b(qt - 1, 0))
                    for kc in range(KC):
                        s.dma_start(qt1[:, (qt % 2) * 4096 + kc * 512:
                                        (qt % 2) * 4096 + (kc + 1) * 512],
                                    qt_h[kc * 128:(kc + 1) * 128,
                                         qt * 512:(qt + 1) * 512]).then_inc(sB_q[qt % 2], 16)

                loads(0)
                loads(1)
                for qt in range(2, NQT1):
                    loads(qt)
                if dbg and "C" not in phases:
                    s.wait_ge(sB_fin, NKB)
                    s.dma_start(dbias_h[:, :], bias_st[:, :]).then_inc(sB_db, 16)
                    s.dma_start(dmn_h[:, :], mneg_st[:, :]).then_inc(sB_db, 16)
                    s.dma_start(de_h[:, :], e_st[:, :]).then_inc(sB_db, 16)
                    s.dma_start(ds_h[:, :], s_st[:, :]).then_inc(sB_db, 16)
                    s.wait_ge(sB_db, 64)

            @blkB.tensor
            def _(t):
                t.wait_ge(sBC_kt, 16 * KC)
                for qt in range(NQT1):
                    t.wait_ge(sB_q[qt % 2], 128 * (qt // 2 + 1))
                    for kb in range(4 * qt + 4):
                        n = n_tile_b(qt, kb)
                        if n >= 4:
                            t.wait_ge(sB_v, n - 3)
                            t.wait_ge(sB_cp, n - 3)
                        for kc in range(KC):
                            mm = t.matmul(
                                ps1[n % 4][:, :],
                                kt_sb[:, kc * 8192 + kb * 128:kc * 8192 + (kb + 1) * 128],
                                qt1[:, (qt % 2) * 4096 + kc * 512:(qt % 2) * 4096 + (kc + 1) * 512],
                                start=(kc == 0), stop=(kc == KC - 1))
                        mm.then_inc(sB_pe, 1)

            @blkB.vector
            def _(v):
                nvs = [0]

                def vself(instr):
                    nvs[0] += 1
                    instr.then_inc(sB_vs, 1)
                    v.wait_ge(sB_vs, nvs[0])

                v.wait_ge(sBC_mk, 16)
                for qt in range(NQT1):
                    for kb in range(4 * qt + 4):
                        n = n_tile_b(qt, kb)
                        v.wait_ge(sB_pe, n + 1)
                        if kb >= 4 * qt:
                            off = kb * 128 - qt * 512
                            sm = 384 - off
                            vself(v.tensor_add(ps1[n % 4][:, :], ps1[n % 4][:, :],
                                               maskm[:, sm:sm + 512]))
                        v.reduce_max(mneg_st[:, kb * 16 + qt:kb * 16 + qt + 1],
                                     ps1[n % 4][:, :], axis=AX.X,
                                     negate=True).then_inc(sB_v, 1)
                for kb in range(NKB):
                    qt0 = kb // 4
                    v.tensor_reduce(mnegf_st[:, kb:kb + 1],
                                    mneg_st[:, kb * 16 + qt0:kb * 16 + 16],
                                    axis=AX.X, op=OP.min).then_inc(sB_vf, 1)
                for kb in range(NKB):
                    qt0 = kb // 4
                    v.wait_ge(sB_af, kb + 1)
                    vself(v.tensor_tensor(tmp_st[:, kb * 16 + qt0:kb * 16 + 16],
                                          tmp_st[:, kb * 16 + qt0:kb * 16 + 16],
                                          e_st[:, kb * 16 + qt0:kb * 16 + 16], op=OP.mult))
                    v.reduce_sum(s_st[:, kb:kb + 1],
                                 tmp_st[:, kb * 16 + qt0:kb * 16 + 16],
                                 axis=AX.X).then_inc(sB_vf2, 1)
                for kb in range(NKB):
                    v.wait_ge(sB_af2, kb + 1)
                    v.tensor_sub(bias_st[:, kb:kb + 1], mnegf_st[:, kb:kb + 1],
                                 lns_st[:, kb:kb + 1]).then_inc(sB_fin, 1)

            @blkB.scalar
            def _(sc):
                for qt in range(NQT1):
                    for kb in range(4 * qt + 4):
                        n = n_tile_b(qt, kb)
                        sc.wait_ge(sB_v, n + 1)
                        if n >= 1:
                            sc.wait_ge(sB_a, n)
                        sc.activation(sc1[:, :],
                                      ps1[n % 4][:, :], AF.Exp,
                                      bias=mneg_st[:, kb * 16 + qt:kb * 16 + qt + 1],
                                      scale=1.0,
                                      accum_out=e_st[:, kb * 16 + qt:kb * 16 + qt + 1]
                                      ).then_inc(sB_a, 1)
                        if n >= 4:
                            sc.wait_ge(sB_sp[n % 4], 16 * (n // 4))
                        sc.activation(sp_st[:, (n % 4) * 512:(n % 4 + 1) * 512],
                                      ps1[n % 4][:, :], AF.Copy).then_inc(sB_cp, 1)
                for kb in range(NKB):
                    qt0 = kb // 4
                    sc.wait_ge(sB_vf, kb + 1)
                    sc.activation(tmp_st[:, kb * 16 + qt0:kb * 16 + 16],
                                  mneg_st[:, kb * 16 + qt0:kb * 16 + 16], AF.Exp,
                                  bias=mnegf_st[:, kb:kb + 1],
                                  scale=-1.0).then_inc(sB_af, 1)
                for kb in range(NKB):
                    sc.wait_ge(sB_vf2, kb + 1)
                    sc.activation(lns_st[:, kb:kb + 1], s_st[:, kb:kb + 1], AF.Ln,
                                  scale=32.0).then_inc(sB_af2, 1)

    if "C" in phases:
        assert "B" in phases, "phase C consumes phase B's S spill"
        with ExitStack() as ps:
            v_sb = ps.enter_context(nc.sbuf_tensor("v_sb", [128, NKB * 1024], BF16))
            s2 = ps.enter_context(nc.sbuf_tensor("s2", [128, 6 * 512], F16))
            p2 = ps.enter_context(nc.sbuf_tensor("p2", [128, 2 * 512], BF16))
            x2 = ps.enter_context(nc.sbuf_tensor("x2", [128, 4 * 1024], F32))
            t1s = ps.enter_context(nc.sbuf_tensor("t1s", [128, 2 * 1024], F32))
            h1s = ps.enter_context(nc.sbuf_tensor("h1s", [128, 2 * 1024], F32))
            sq2 = ps.enter_context(nc.sbuf_tensor("sq2", [128, 1024], BF16))
            st2 = ps.enter_context(nc.sbuf_tensor("st2", [128, 16], F32))
            pr = [ps.enter_context(nc.psum_tensor(f"pr{i}", [128, 1024], F32)) for i in range(4)]

            sC_v = [sem(f"sC_v{s}") for s in range(4)]
            sC_s = [sem(f"sC_s{s}") for s in range(6)]
            sC_x2 = [sem(f"sC_x2{s}") for s in range(4)]
            sC_pr = sem("sC_pr")
            sC_a = sem("sC_a")
            sC_vt = sem("sC_vt")
            sC_ae = sem("sC_ae")
            sC_vv = sem("sC_vv")
            sC_va = sem("sC_va")
            sC_sh = [sem(f"sC_sh{s}") for s in range(2)]
            sC_db = sem("sC_db") if dbg else None
            sC_vs = sem("sC_vs")

            TILES_C = [(qt, kb) for qt in range(NQT1) for kb in range(4 * qt + 4)]

            blkC = ps.enter_context(nc.Block())

            @blkC.sync
            def _(s):
                for s_ in range(4):
                    s.wait_ge(sB_sp[s_], 16 * (NB_TOT // 4))
                if dbg:
                    s.wait_ge(sB_fin, NKB)
                    s.dma_start(dbias_h[:, :], bias_st[:, :]).then_inc(sC_db, 16)

                def loads(qb):
                    for ic in range(4):
                        u = 4 * qb + ic
                        if qb >= 1:
                            s.wait_ge(sC_vt, 4 * (qb - 1) + ic + 1)
                        s.dma_start(x2[:, ic * 1024:(ic + 1) * 1024],
                                    x_h[u * 128:(u + 1) * 128, :]).then_inc(sC_x2[ic], 16)

                def stores(qb):
                    for ic in range(4):
                        u = 4 * qb + ic
                        s.wait_ge(sC_va, u + 1)
                        s.dma_start(h1_h[u * 128:(u + 1) * 128, :],
                                    h1s[:, (u % 2) * 1024:(u % 2 + 1) * 1024]
                                    ).then_inc(sC_sh[u % 2], 16)

                loads(0)
                for qb in range(1, NQT1):
                    stores(qb - 1)
                    loads(qb)
                stores(NQT1 - 1)
                s.wait_ge(sC_sh[0], 16 * 32)
                s.wait_ge(sC_sh[1], 16 * 32)
                if dbg:
                    s.wait_ge(sC_db, 16)

            @blkC.gpsimd
            def _(g):
                if "A" in phases:
                    g.wait_ge(sA_stv[0], 16 * 32)
                    g.wait_ge(sA_stv[1], 16 * 32)
                g.wait_ge(sB_fin, NKB)
                for s_ in range(4):
                    g.wait_ge(sB_sp[s_], 16 * (NB_TOT // 4))
                for kb in range(NKB):
                    g.dma_start(v_sb[:, kb * 1024:(kb + 1) * 1024],
                                v_h[kb * 128:(kb + 1) * 128, :]).then_inc(sC_v[kb // 16], 16)

            @blkC.tensor
            def _(t):
                for qb in range(NQT1):
                    nkb = 4 * qb + 4
                    for kb in range(nkb):
                        n = n_tile_b(qb, kb)
                        t.wait_ge(sC_a, n + 1)
                        t.wait_ge(sC_v[kb // 16], 16 * 16)
                        if kb == 0 and qb >= 1:
                            t.wait_ge(sC_vt, 4 * qb)
                        for ic in range(4):
                            for h in range(2):
                                mm = t.matmul(
                                    pr[ic][:, h * 512:(h + 1) * 512],
                                    p2[:, (n % 2) * 512 + ic * 128:(n % 2) * 512 + (ic + 1) * 128],
                                    v_sb[:, kb * 1024 + h * 512:kb * 1024 + (h + 1) * 512],
                                    start=(kb == 0), stop=(kb == nkb - 1))
                        mm.then_inc(sC_pr, 1)

            @blkC.scalar
            def _(sc):
                for s_ in range(4):
                    sc.wait_ge(sB_sp[s_], 16 * (NB_TOT // 4))
                sc.wait_ge(sB_fin, NKB)

                def issue_s(i):
                    if i >= NB_TOT:
                        return
                    if i >= 6:
                        sc.wait_ge(sC_a, i - 5)
                    sc.dma_start(s2[:, (i % 6) * 512:(i % 6 + 1) * 512],
                                 s_h[i * 128:(i + 1) * 128, :]).then_inc(sC_s[i % 6], 16)

                for i in range(6):
                    issue_s(i)

                def ln_chain(qb):
                    for ic in range(4):
                        u = 4 * qb + ic
                        c0 = (u % 2) * 8
                        sc.wait_ge(sC_vt, u + 1)
                        if u >= 1:
                            sc.wait_ge(sC_ae, 3 * (u - 1) + 1)
                        sc.activation(sq2[:, :], t1s[:, (u % 2) * 1024:(u % 2 + 1) * 1024],
                                      AF.Square,
                                      accum_out=st2[:, c0 + 1:c0 + 2]).then_inc(sC_ae, 1)
                        sc.wait_ge(sC_vv, u + 1)
                        sc.activation(st2[:, c0 + 5:c0 + 6], st2[:, c0 + 4:c0 + 5], AF.Ln,
                                      bias=eps_t[:, 0:1], scale=1.0).then_inc(sC_ae, 1)
                        sc.wait_ge(sC_ae, 3 * u + 2)
                        sc.activation(st2[:, c0 + 6:c0 + 7], st2[:, c0 + 5:c0 + 6], AF.Exp,
                                      scale=-0.5).then_inc(sC_ae, 1)

                for qb in range(NQT1):
                    for kb in range(4 * qb + 4):
                        n = n_tile_b(qb, kb)
                        sc.wait_ge(sC_s[n % 6], 16 * (n // 6 + 1))
                        if n >= 2:
                            sc.wait_ge(sC_pr, n - 1)
                        sc.activation(p2[:, (n % 2) * 512:(n % 2 + 1) * 512],
                                      s2[:, (n % 6) * 512:(n % 6 + 1) * 512], AF.Exp,
                                      bias=bias_st[:, kb:kb + 1], scale=1.0).then_inc(sC_a, 1)
                        issue_s(n + 6)
                    ln_chain(qb)

            @blkC.vector
            def _(v):
                nvs = [0]

                def vself(instr):
                    nvs[0] += 1
                    instr.then_inc(sC_vs, 1)
                    v.wait_ge(sC_vs, nvs[0])

                for qb in range(NQT1):
                    for ic in range(4):
                        u = 4 * qb + ic
                        c0 = (u % 2) * 8
                        v.wait_ge(sC_pr, n_tile_b(qb, 0) + 4 * qb + 4)
                        v.wait_ge(sC_x2[ic], 16 * (qb + 1))
                        if u >= 2:
                            v.wait_ge(sC_ae, 3 * (u - 2) + 1)
                            v.wait_ge(sC_va, u - 1)
                        v.scalar_tensor_tensor(
                            t1s[:, (u % 2) * 1024:(u % 2 + 1) * 1024],
                            x2[:, ic * 1024:(ic + 1) * 1024],
                            2.0, pr[ic][:, :], op0=OP.mult, op1=OP.add).then_inc(sC_vt, 1)
                        v.wait_ge(sC_vt, u + 1)
                        vself(v.reduce_sum(st2[:, c0:c0 + 1],
                                           t1s[:, (u % 2) * 1024:(u % 2 + 1) * 1024], axis=AX.X))
                        v.wait_ge(sC_ae, 3 * u + 1)
                        vself(v.tensor_scalar(st2[:, c0 + 2:c0 + 3], st2[:, c0:c0 + 1],
                                              1.0 / D, None, op0=OP.mult))
                        vself(v.tensor_tensor(st2[:, c0 + 3:c0 + 4], st2[:, c0 + 2:c0 + 3],
                                              st2[:, c0 + 2:c0 + 3], op=OP.mult))
                        v.scalar_tensor_tensor(st2[:, c0 + 4:c0 + 5], st2[:, c0 + 1:c0 + 2],
                                               1.0 / D, st2[:, c0 + 3:c0 + 4],
                                               op0=OP.mult, op1=OP.subtract).then_inc(sC_vv, 1)
                        v.wait_ge(sC_ae, 3 * u + 3)
                        if u >= 2:
                            v.wait_ge(sC_sh[u % 2], 16 * (u // 2))
                        v.tensor_scalar(h1s[:, (u % 2) * 1024:(u % 2 + 1) * 1024],
                                        t1s[:, (u % 2) * 1024:(u % 2 + 1) * 1024],
                                        st2[:, c0 + 2:c0 + 3], st2[:, c0 + 6:c0 + 7],
                                        op0=OP.subtract, op1=OP.mult).then_inc(sC_va, 1)

    if "E" in phases:
        with ExitStack() as ps:
            wd_sb = ps.enter_context(nc.sbuf_tensor("wd_sb", [128, NPE * 4 * 1024], BF16))
            rw_sb = ps.enter_context(nc.sbuf_tensor("rw_sb", [128, 64], BF16))
            h1b = ps.enter_context(nc.sbuf_tensor("h1b", [128, 4 * 1024], BF16))
            h1t = ps.enter_context(nc.sbuf_tensor("h1t", [128, 2 * 4096], BF16))
            wgu = ps.enter_context(nc.sbuf_tensor("wgu", [128, 2 * 8192], BF16))
            a_sb = ps.enter_context(nc.sbuf_tensor("a_sb", [128, NPE * 4 * 512], BF16))
            wt_sb = ps.enter_context(nc.sbuf_tensor("wt_sb", [E, 512], BF16))
            wtil = ps.enter_context(nc.sbuf_tensor("wtil", [128, E * 512], BF16))
            sg_sb = ps.enter_context(nc.sbuf_tensor("sg_sb", [128, 2 * 512], BF16))
            ub_sb = ps.enter_context(nc.sbuf_tensor("ub_sb", [128, 512], BF16))
            h1f = ps.enter_context(nc.sbuf_tensor("h1f", [128, 2 * 1024], F32))
            t2s = ps.enter_context(nc.sbuf_tensor("t2s", [128, 1024], F32))
            st3 = ps.enter_context(nc.sbuf_tensor("st3", [128, 16], F32))
            wtb = ps.enter_context(nc.sbuf_tensor("wtb", [128, 4 * 8], BF16))
            lg_sb = ps.enter_context(nc.sbuf_tensor("lg_sb", [128, 8], F32))
            el_sb = ps.enter_context(nc.sbuf_tensor("el_sb", [128, 8], F32))
            rr = ps.enter_context(nc.sbuf_tensor("rr", [128, 32], F32))
            outs = ps.enter_context(nc.sbuf_tensor("outs", [128, 1024], F32))
            yA = ps.enter_context(nc.psum_tensor("yA", [128, 1024], F32))
            yB = ps.enter_context(nc.psum_tensor("yB", [128, 1024], F32))
            pg = ps.enter_context(nc.psum_tensor("pg", [128, 512], F32))
            pu = ps.enter_context(nc.psum_tensor("pu", [128, 512], F32))
            ptr2 = [ps.enter_context(nc.psum_tensor(f"ptr2_{i}", [128, 128], BF16))
                    for i in range(2)]

            sE_wrw = sem("sE_wrw")
            sE_wwd = sem("sE_wwd")
            sE_h1b = [sem(f"sE_h1b{s}") for s in range(4)]
            sE_wgu = [sem(f"sE_wgu{s}") for s in range(2)]
            sE_tr = sem("sE_tr")
            sE_trE = sem("sE_trE")
            sE_petc = sem("sE_petc")
            sE_pl = sem("sE_pl")
            sE_vcp = sem("sE_vcp")
            sE_ae2 = sem("sE_ae2")
            sE_vwt = sem("sE_vwt")
            sE_pw = sem("sE_pw")
            sE_aw = sem("sE_aw")
            sE_pg = sem("sE_pg")
            sE_pu = sem("sE_pu")
            sE_as = sem("sE_as")
            sE_vu = sem("sE_vu")
            sE_va = sem("sE_va")
            sE_py = sem("sE_py")
            sE_h1f = [sem(f"sE_h1f{s}") for s in range(2)]
            sE_vt2 = sem("sE_vt2")
            sE_ae3 = sem("sE_ae3")
            sE_vv3 = sem("sE_vv3")
            sE_vo = sem("sE_vo")
            sE_so = sem("sE_so")
            sE_vs = sem("sE_vs")

            blkE = ps.enter_context(nc.Block())

            def wgu_src(pe, kc, which):
                if pe < 8:
                    h = wg_h if which == 0 else wu_h
                    return h[pe * 1024 + kc * 128:pe * 1024 + (kc + 1) * 128, :]
                h = wsg_h if which == 0 else wsu_h
                c0 = (pe - 8) * 512
                return h[kc * 128:(kc + 1) * 128, c0:c0 + 512]

            def wd_src(pe, fc):
                if pe < 8:
                    return wd_h[pe * 512 + fc * 128:pe * 512 + (fc + 1) * 128, :]
                return wsd_h[(pe - 8) * 512 + fc * 128:(pe - 8) * 512 + (fc + 1) * 128, :]

            @blkE.gpsimd
            def _(g):
                if "C" in phases:
                    g.wait_ge(sC_sh[0], 16 * NQB2)
                    g.wait_ge(sC_sh[1], 16 * NQB2)
                for kc in range(KC):
                    g.dma_start(rw_sb[:, kc * 8:(kc + 1) * 8],
                                rw_h[kc * 128:(kc + 1) * 128, :]).then_inc(sE_wrw, 16)
                for gtb in range(4):
                    g.dma_start(h1b[:, (gtb % 4) * 1024:(gtb % 4 + 1) * 1024],
                                h1_h[gtb * 128:(gtb + 1) * 128, :]).then_inc(sE_h1b[gtb % 4], 16)
                for pe in range(2):
                    for which in range(2):
                        for kc in range(KC):
                            g.dma_start(
                                wgu[:, (pe % 2) * 8192 + which * 4096 + kc * 512:
                                    (pe % 2) * 8192 + which * 4096 + (kc + 1) * 512],
                                wgu_src(pe, kc, which)).then_inc(sE_wgu[pe % 2], 16)
                for pe in range(NPE):
                    for fc in range(4):
                        g.dma_start(wd_sb[:, (pe * 4 + fc) * 1024:(pe * 4 + fc + 1) * 1024],
                                    wd_src(pe, fc)).then_inc(sE_wwd, 16)
                for tc in range(NTCE):
                    for tbl in range(4):
                        gtb = tc * 4 + 4 + tbl
                        if gtb >= NTB:
                            continue
                        g.wait_ge(sE_tr, tc * 36 + (tbl + 1) * 8)
                        g.dma_start(h1b[:, (gtb % 4) * 1024:(gtb % 4 + 1) * 1024],
                                    h1_h[gtb * 128:(gtb + 1) * 128, :]).then_inc(sE_h1b[gtb % 4], 16)
                    for pe in range(NPE):
                        unit = tc * NPE + pe + 2
                        if unit >= NTCE * NPE:
                            continue
                        ntc, npe = unit // NPE, unit % NPE
                        g.wait_ge(sE_pu, 4 * (unit - 1))
                        for which in range(2):
                            for kc in range(KC):
                                g.dma_start(
                                    wgu[:, (unit % 2) * 8192 + which * 4096 + kc * 512:
                                        (unit % 2) * 8192 + which * 4096 + (kc + 1) * 512],
                                    wgu_src(npe, kc, which)).then_inc(sE_wgu[unit % 2], 16)

            @blkE.tensor
            def _(t):
                for tc in range(NTCE):
                    par = tc % 2
                    for tbl in range(4):
                        gtb = tc * 4 + tbl
                        t.wait_ge(sE_h1b[gtb % 4], 16 * (gtb // 4 + 1))
                        for dc in range(8):
                            i3 = tc * 36 + tbl * 8 + dc
                            if i3 >= 2:
                                t.wait_ge(sE_trE, i3 - 1)
                            t.transpose(
                                ptr2[i3 % 2][:, :],
                                h1b[:, (gtb % 4) * 1024 + dc * 128:(gtb % 4) * 1024 + (dc + 1) * 128],
                                ident[:, :]).then_inc(sE_tr, 1)
                    t.wait_ge(sE_trE, tc * 36 + 32)
                    if tc == 0:
                        t.wait_ge(sE_wrw, 16 * KC)
                    for tbl in range(4):
                        idx = tc * 4 + tbl
                        if idx >= 1:
                            t.wait_ge(sE_vcp, idx)
                        if tc >= 1 and tbl == 0:
                            t.wait_ge(sE_as, 40 * tc)
                        for kc in range(KC):
                            mm = t.matmul(
                                pg[:, 0:8],
                                h1t[:, par * 4096 + kc * 512 + tbl * 128:
                                    par * 4096 + kc * 512 + (tbl + 1) * 128],
                                rw_sb[:, kc * 8:(kc + 1) * 8],
                                start=(kc == 0), stop=(kc == KC - 1))
                        mm.then_inc(sE_pl, 1)
                    for tbl in range(4):
                        idx = tc * 4 + tbl
                        i3 = tc * 36 + 32 + tbl
                        t.wait_ge(sE_vwt, idx + 1)
                        if i3 >= 2:
                            t.wait_ge(sE_trE, i3 - 1)
                        t.transpose(ptr2[i3 % 2][0:8, :], wtb[:, tbl * 8:(tbl + 1) * 8],
                                    ident[:, :]).then_inc(sE_tr, 1)
                    t.wait_ge(sE_trE, 36 * (tc + 1))
                    if tc >= 1:
                        t.wait_ge(sE_vu, 40 * tc)
                    for e in range(E):
                        if e >= 2:
                            t.wait_ge(sE_aw, 8 * tc + e - 1)
                        dst = pg if e % 2 == 0 else pu
                        mm = t.matmul(dst[:, :], sel[:, e * 128:(e + 1) * 128],
                                      wt_sb[:, :], start=True, stop=True)
                        mm.then_inc(sE_pw, 1)
                    for pe in range(NPE):
                        unit = tc * NPE + pe
                        w0 = tc * 40 + pe * 4
                        t.wait_ge(sE_wgu[unit % 2], 256 * (unit // 2 + 1))
                        for fc in range(4):
                            w = w0 + fc
                            if pe == 0 and fc == 0:
                                t.wait_ge(sE_aw, 8 * (tc + 1))
                            if w >= 1:
                                t.wait_ge(sE_as, w)
                            for kc in range(KC):
                                mm = t.matmul(
                                    pg[:, :],
                                    wgu[:, (unit % 2) * 8192 + kc * 512 + fc * 128:
                                        (unit % 2) * 8192 + kc * 512 + (fc + 1) * 128],
                                    h1t[:, par * 4096 + kc * 512:par * 4096 + (kc + 1) * 512],
                                    start=(kc == 0), stop=(kc == KC - 1))
                            mm.then_inc(sE_pg, 1)
                            if w >= 1:
                                t.wait_ge(sE_vu, w)
                            for kc in range(KC):
                                mm = t.matmul(
                                    pu[:, :],
                                    wgu[:, (unit % 2) * 8192 + 4096 + kc * 512 + fc * 128:
                                        (unit % 2) * 8192 + 4096 + kc * 512 + (fc + 1) * 128],
                                    h1t[:, par * 4096 + kc * 512:par * 4096 + (kc + 1) * 512],
                                    start=(kc == 0), stop=(kc == KC - 1))
                            mm.then_inc(sE_pu, 1)
                            if pe == NPE - 1 and fc == 3:
                                t.nop().then_inc(sE_petc, 1)
                    t.wait_ge(sE_va, 40 * (tc + 1))
                    if tc == 0:
                        t.wait_ge(sE_wwd, 16 * NPE * 4)
                    for tbl in range(4):
                        gtb = tc * 4 + tbl
                        y = yA if tbl % 2 == 0 else yB
                        if gtb >= 2:
                            t.wait_ge(sE_vt2, gtb - 1)
                        for pe in range(NPE):
                            for fc in range(4):
                                for h in range(2):
                                    mm = t.matmul(
                                        y[:, h * 512:(h + 1) * 512],
                                        a_sb[:, (pe * 4 + fc) * 512 + tbl * 128:
                                             (pe * 4 + fc) * 512 + (tbl + 1) * 128],
                                        wd_sb[:, (pe * 4 + fc) * 1024 + h * 512:
                                              (pe * 4 + fc) * 1024 + (h + 1) * 512],
                                        start=(pe == 0 and fc == 0),
                                        stop=(pe == NPE - 1 and fc == 3))
                        mm.then_inc(sE_py, 1)

            @blkE.vector
            def _(v):
                nvs = [0]

                def vself(instr):
                    nvs[0] += 1
                    instr.then_inc(sE_vs, 1)
                    v.wait_ge(sE_vs, nvs[0])

                for tc in range(NTCE):
                    par = tc % 2
                    for tbl in range(4):
                        for dc in range(8):
                            i3 = tc * 36 + tbl * 8 + dc
                            v.wait_ge(sE_tr, i3 + 1)
                            if tbl == 0 and dc == 0 and tc >= 2:
                                v.wait_ge(sE_petc, tc - 1)
                            v.tensor_copy(
                                h1t[:, par * 4096 + dc * 512 + tbl * 128:
                                    par * 4096 + dc * 512 + (tbl + 1) * 128],
                                ptr2[i3 % 2][:, :]).then_inc(sE_trE, 1)
                    for tbl in range(4):
                        idx = tc * 4 + tbl
                        v.wait_ge(sE_pl, idx + 1)
                        vself(v.tensor_copy(lg_sb[:, :], pg[:, 0:8]))
                        v.reduce_max(rr[:, 0:1], lg_sb[:, :], axis=AX.X,
                                     negate=True).then_inc(sE_vcp, 1)
                        v.wait_ge(sE_ae2, idx + 1)
                        vself(v.reciprocal(rr[:, 2:3], rr[:, 1:2]))
                        vself(v.tensor_scalar(rr[:, 8:16], el_sb[:, :], rr[:, 2:3], None,
                                              op0=OP.mult))
                        vself(v.tensor_reduce(rr[:, 3:4], rr[:, 8:16], axis=AX.X, op=OP.min))
                        vself(v.tensor_scalar(rr[:, 16:24], rr[:, 8:16], rr[:, 3:4], None,
                                              op0=OP.is_le))
                        vself(v.scalar_tensor_tensor(rr[:, 16:24], rr[:, 16:24], BIG,
                                                     rr[:, 8:16], op0=OP.mult, op1=OP.add))
                        vself(v.tensor_reduce(rr[:, 4:5], rr[:, 16:24], axis=AX.X, op=OP.min))
                        vself(v.tensor_scalar(rr[:, 24:32], rr[:, 16:24], rr[:, 4:5], None,
                                              op0=OP.is_le))
                        vself(v.scalar_tensor_tensor(rr[:, 24:32], rr[:, 24:32], BIG,
                                                     rr[:, 16:24], op0=OP.mult, op1=OP.add))
                        vself(v.tensor_reduce(rr[:, 5:6], rr[:, 24:32], axis=AX.X, op=OP.min))
                        vself(v.tensor_scalar(rr[:, 16:24], rr[:, 8:16], rr[:, 5:6], None,
                                              op0=OP.is_ge))
                        v.tensor_tensor(wtb[:, tbl * 8:(tbl + 1) * 8], rr[:, 8:16],
                                        rr[:, 16:24], op=OP.mult).then_inc(sE_vwt, 1)
                    for tbl in range(4):
                        i3 = tc * 36 + 32 + tbl
                        v.wait_ge(sE_tr, i3 + 1)
                        v.tensor_copy(wt_sb[:, tbl * 128:(tbl + 1) * 128],
                                      ptr2[i3 % 2][0:8, :]).then_inc(sE_trE, 1)
                    for pe in range(NPE):
                        for fc in range(4):
                            w = tc * 40 + pe * 4 + fc
                            v.wait_ge(sE_pu, w + 1)
                            if w >= 1:
                                v.wait_ge(sE_va, w)
                            if pe == 0 and fc == 0 and tc >= 1:
                                v.wait_ge(sE_py, 4 * tc)
                            if pe < 8:
                                v.wait_ge(sE_aw, 8 * tc + pe + 1)
                                v.tensor_tensor(ub_sb[:, :],
                                                pu[:, :], wtil[:, pe * 512:(pe + 1) * 512],
                                                op=OP.mult).then_inc(sE_vu, 1)
                            else:
                                v.tensor_copy(ub_sb[:, :],
                                              pu[:, :]).then_inc(sE_vu, 1)
                            v.wait_ge(sE_vu, w + 1)
                            v.wait_ge(sE_as, w + 1)
                            v.tensor_tensor(a_sb[:, (pe * 4 + fc) * 512:
                                                 (pe * 4 + fc + 1) * 512],
                                            sg_sb[:, (w % 2) * 512:(w % 2 + 1) * 512],
                                            ub_sb[:, :],
                                            op=OP.mult).then_inc(sE_va, 1)
                    for tbl in range(4):
                        gtb = tc * 4 + tbl
                        y = yA if tbl % 2 == 0 else yB
                        c0 = (gtb % 2) * 8
                        v.wait_ge(sE_py, gtb + 1)
                        v.wait_ge(sE_h1f[gtb % 2], 16 * (gtb // 2 + 1))
                        if gtb >= 1:
                            v.wait_ge(sE_ae3, 3 * (gtb - 1) + 1)
                            v.wait_ge(sE_vo, gtb)
                        v.tensor_add(t2s[:, :],
                                     y[:, :],
                                     h1f[:, (gtb % 2) * 1024:(gtb % 2 + 1) * 1024]
                                     ).then_inc(sE_vt2, 1)
                        v.wait_ge(sE_vt2, gtb + 1)
                        vself(v.reduce_sum(st3[:, c0:c0 + 1], t2s[:, :], axis=AX.X))
                        v.wait_ge(sE_ae3, 3 * gtb + 1)
                        vself(v.tensor_scalar(st3[:, c0 + 2:c0 + 3], st3[:, c0:c0 + 1],
                                              1.0 / D, None, op0=OP.mult))
                        vself(v.tensor_tensor(st3[:, c0 + 3:c0 + 4], st3[:, c0 + 2:c0 + 3],
                                              st3[:, c0 + 2:c0 + 3], op=OP.mult))
                        v.scalar_tensor_tensor(st3[:, c0 + 4:c0 + 5], st3[:, c0 + 1:c0 + 2],
                                               1.0 / D, st3[:, c0 + 3:c0 + 4],
                                               op0=OP.mult, op1=OP.subtract).then_inc(sE_vv3, 1)
                        v.wait_ge(sE_ae3, 3 * gtb + 3)
                        if gtb >= 1:
                            v.wait_ge(sE_so, 16 * gtb)
                        v.tensor_scalar(outs[:, :],
                                        t2s[:, :],
                                        st3[:, c0 + 2:c0 + 3], st3[:, c0 + 6:c0 + 7],
                                        op0=OP.subtract, op1=OP.mult).then_inc(sE_vo, 1)

            @blkE.scalar
            def _(sc):
                for tc in range(NTCE):
                    for tbl in range(4):
                        idx = tc * 4 + tbl
                        sc.wait_ge(sE_vcp, idx + 1)
                        sc.activation(el_sb[:, :], lg_sb[:, :], AF.Exp,
                                      bias=rr[:, 0:1], scale=1.0,
                                      accum_out=rr[:, 1:2]).then_inc(sE_ae2, 1)
                    for e in range(E):
                        sc.wait_ge(sE_pw, 8 * tc + e + 1)
                        if e == 0 and tc >= 1:
                            sc.wait_ge(sE_vu, 40 * tc)
                        src = pg if e % 2 == 0 else pu
                        sc.activation(wtil[:, e * 512:(e + 1) * 512], src[:, :],
                                      AF.Copy).then_inc(sE_aw, 1)
                    for pe in range(NPE):
                        for fc in range(4):
                            w = tc * 40 + pe * 4 + fc
                            sc.wait_ge(sE_pg, w + 1)
                            if w >= 2:
                                sc.wait_ge(sE_va, w - 1)
                            sc.activation(sg_sb[:, (w % 2) * 512:(w % 2 + 1) * 512],
                                          pg[:, :], AF.Silu).then_inc(sE_as, 1)
                    for tbl in range(4):
                        gtb = tc * 4 + tbl
                        c0 = (gtb % 2) * 8
                        sc.wait_ge(sE_vt2, gtb + 1)
                        if gtb >= 1:
                            sc.wait_ge(sE_so, 16 * gtb)
                        sc.activation(outs[:, :], t2s[:, :], AF.Square,
                                      accum_out=st3[:, c0 + 1:c0 + 2]).then_inc(sE_ae3, 1)
                        sc.wait_ge(sE_vv3, gtb + 1)
                        sc.activation(st3[:, c0 + 5:c0 + 6], st3[:, c0 + 4:c0 + 5], AF.Ln,
                                      bias=eps_t[:, 0:1], scale=1.0).then_inc(sE_ae3, 1)
                        sc.wait_ge(sE_ae3, 3 * gtb + 2)
                        sc.activation(st3[:, c0 + 6:c0 + 7], st3[:, c0 + 5:c0 + 6], AF.Exp,
                                      scale=-0.5).then_inc(sE_ae3, 1)

            @blkE.sync
            def _(s):
                for tc in range(NTCE):
                    for tbl in range(4):
                        gtb = tc * 4 + tbl
                        if gtb >= 2:
                            s.wait_ge(sE_vt2, gtb - 1)
                        s.dma_start(h1f[:, (gtb % 2) * 1024:(gtb % 2 + 1) * 1024],
                                    h1_h[gtb * 128:(gtb + 1) * 128, :]).then_inc(sE_h1f[gtb % 2], 16)
                    for tbl in range(4):
                        gtb = tc * 4 + tbl
                        s.wait_ge(sE_vo, gtb + 1)
                        s.dma_start(out_h[gtb * 128:(gtb + 1) * 128, :],
                                    outs[:, :]).then_inc(sE_so, 16)
                if dbg:
                    s.wait_ge(sE_vwt, 64)
                s.wait_ge(sE_so, 16 * 64)

            if dbg:
                pass

    es.close()
    return nc


def make_consts():
    ident = np.eye(128, dtype=ml_dtypes.bfloat16)
    p = np.arange(128)[:, None]
    m = np.arange(896)[None, :]
    maskm = np.where(m >= p + 384, 0.0, NEGINF).astype(np.float32)
    sel = np.zeros((E, E * 128), dtype=ml_dtypes.bfloat16)
    for e in range(E):
        sel[e, e * 128:(e + 1) * 128] = 1.0
    return ident, maskm, sel


def kernel(**inputs):
    f = lambda k: np.ascontiguousarray(np.asarray(inputs[k], dtype=np.float32))
    x = f("L0_attn_minibatch")
    wq, wk, wv = f("L0_attn_Wq"), f("L0_attn_Wk"), f("L0_attn_Wv")
    bq = f("L0_attn_bq").reshape(D, 1)
    bk = f("L0_attn_bk").reshape(D, 1)
    bv = f("L0_attn_bv").reshape(1, D)
    rw = f("L0_ffn_router_w")
    wg = f("L0_ffn_w_gate").reshape(E * D, FF)
    wu = f("L0_ffn_w_up").reshape(E * D, FF)
    wd = f("L0_ffn_w_down").reshape(E * FF, D)
    wsg, wsu, wsd = f("L0_ffn_ws_gate"), f("L0_ffn_ws_up"), f("L0_ffn_ws_down")
    ident, maskm, sel = make_consts()

    nc = build_program()
    base = dict(wq=wq, wk=wk, wv=wv, bq=bq, bk=bk, bv=bv, rw=rw, wg=wg, wu=wu,
                wd=wd, wsg=wsg, wsu=wsu, wsd=wsd, ident=ident, maskm=maskm, sel=sel)
    in_maps = [dict(base, x=np.ascontiguousarray(x[b])) for b in range(8)]
    res = run_bass_kernel_spmd(nc, in_maps, core_ids=list(range(8)))
    return np.stack([np.asarray(r["out"]) for r in res.results], axis=0)


# version 17
# speedup vs baseline: 1.3765x; 1.1160x over previous
import numpy as np
import ml_dtypes
import concourse.bass as bass
import concourse.mybir as mybir
from concourse.bass_utils import run_bass_kernel_spmd
from concourse.alu_op_type import AluOpType as OP
from contextlib import ExitStack

F32 = mybir.dt.float32
BF16 = mybir.dt.bfloat16
F16 = mybir.dt.float16
AF = mybir.ActivationFunctionType
AX = mybir.AxisListType

T, D, E, FF, FS = 8192, 1024, 8, 512, 1024
KC = 8
NTB = 64
NTCA = 16
NQT1 = 16
NKB = 64
QB2, NQB2 = 256, 32
TCE, NTCE = 512, 16
NPE = 10
NEGINF = -30000.0
BIG = 1.0e9
EPS = 1e-5


def n_tile_b(qt, kb):
    return 2 * qt * (qt + 1) + kb


def n_tile_c(qb, kb):
    return qb * (qb + 1) + kb


NB_TOT = n_tile_b(15, 0) + 64
NC_TOT = 32 * 33


def build_program(phases=("A", "B", "C", "E"), dbg=False):
    nc = bass.Bass("TRN2", target_bir_lowering=False)

    x_h = nc.dram_tensor("x", [T, D], F32, kind="ExternalInput").ap()
    wq_h = nc.dram_tensor("wq", [D, D], F32, kind="ExternalInput").ap()
    wk_h = nc.dram_tensor("wk", [D, D], F32, kind="ExternalInput").ap()
    wv_h = nc.dram_tensor("wv", [D, D], F32, kind="ExternalInput").ap()
    bq_h = nc.dram_tensor("bq", [D, 1], F32, kind="ExternalInput").ap()
    bk_h = nc.dram_tensor("bk", [D, 1], F32, kind="ExternalInput").ap()
    bv_h = nc.dram_tensor("bv", [1, D], F32, kind="ExternalInput").ap()
    rw_h = nc.dram_tensor("rw", [D, E], F32, kind="ExternalInput").ap()
    wg_h = nc.dram_tensor("wg", [E * D, FF], F32, kind="ExternalInput").ap()
    wu_h = nc.dram_tensor("wu", [E * D, FF], F32, kind="ExternalInput").ap()
    wd_h = nc.dram_tensor("wd", [E * FF, D], F32, kind="ExternalInput").ap()
    wsg_h = nc.dram_tensor("wsg", [D, FS], F32, kind="ExternalInput").ap()
    wsu_h = nc.dram_tensor("wsu", [D, FS], F32, kind="ExternalInput").ap()
    wsd_h = nc.dram_tensor("wsd", [FS, D], F32, kind="ExternalInput").ap()
    ident_h = nc.dram_tensor("ident", [128, 128], BF16, kind="ExternalInput").ap()
    maskm_h = nc.dram_tensor("maskm", [128, 896], F32, kind="ExternalInput").ap()
    sel_h = nc.dram_tensor("sel", [E, E * 128], BF16, kind="ExternalInput").ap()

    out_h = nc.dram_tensor("out", [T, D], F32, kind="ExternalOutput").ap()

    sk = "ExternalOutput" if dbg else "Internal"
    s_h = nc.dram_tensor("s_s", [NB_TOT * 128, 512], F16, kind="Internal").ap()
    wgupk_h = nc.dram_tensor("wgu_pk", [128, NPE * 8192], BF16, kind="Internal").ap()
    qt_h = nc.dram_tensor("qt_s", [D, T], BF16, kind=sk).ap()
    kt_h = nc.dram_tensor("kt_s", [D, T], BF16, kind=sk).ap()
    v_h = nc.dram_tensor("v_s", [T, D], BF16, kind=sk).ap()
    h1_h = nc.dram_tensor("h1_s", [T, D], F32, kind=sk).ap()
    if dbg:
        dbias_h = nc.dram_tensor("d_bias", [128, NKB], F32, kind="ExternalOutput").ap()
        dmn_h = nc.dram_tensor("d_mn", [128, NKB * 16], F32, kind="ExternalOutput").ap()
        de_h = nc.dram_tensor("d_e", [128, NKB * 16], F32, kind="ExternalOutput").ap()
        ds_h = nc.dram_tensor("d_s", [128, NKB], F32, kind="ExternalOutput").ap()

    es = ExitStack()
    sem = lambda name: es.enter_context(nc.semaphore(name))

    ident = es.enter_context(nc.sbuf_tensor("ident_s", [128, 128], BF16))
    sel = es.enter_context(nc.sbuf_tensor("sel_s", [E, E * 128], BF16))
    ones1 = es.enter_context(nc.sbuf_tensor("ones1", [1, 128], BF16))
    eps_t = es.enter_context(nc.sbuf_tensor("eps_t", [128, 1], F32))
    bias_st = es.enter_context(nc.sbuf_tensor("bias_st", [128, NKB], F32))

    sA_c = sem("sA_c")

    if "A" in phases:
        with ExitStack() as ps:
            xa = ps.enter_context(nc.sbuf_tensor("xa", [128, 4 * 1024], BF16))
            wq_s = ps.enter_context(nc.sbuf_tensor("wq_s", [128, 8 * 1024], BF16))
            wk_s = ps.enter_context(nc.sbuf_tensor("wk_s", [128, 8 * 1024], BF16))
            wv_s = ps.enter_context(nc.sbuf_tensor("wv_s", [128, 8 * 1024], BF16))
            bq_s = ps.enter_context(nc.sbuf_tensor("bq_s", [128, 8], F32))
            bk_s = ps.enter_context(nc.sbuf_tensor("bk_s", [128, 8], F32))
            bv_s = ps.enter_context(nc.sbuf_tensor("bv_s", [1, 1024], BF16))
            xt = ps.enter_context(nc.sbuf_tensor("xt", [128, 2 * 4096], BF16))
            qs = ps.enter_context(nc.sbuf_tensor("qs", [128, 2 * 512], BF16))
            ks = ps.enter_context(nc.sbuf_tensor("ks", [128, 2 * 512], BF16))
            vs = ps.enter_context(nc.sbuf_tensor("vs", [128, 2 * 1024], BF16))
            pq = [ps.enter_context(nc.psum_tensor(f"pq{i}", [128, 512], F32)) for i in range(2)]
            pk = [ps.enter_context(nc.psum_tensor(f"pk{i}", [128, 512], F32)) for i in range(2)]
            pv = ps.enter_context(nc.psum_tensor("pv", [128, 1024], F32))
            ptr = [ps.enter_context(nc.psum_tensor(f"ptr{i}", [128, 128], BF16)) for i in range(2)]

            sA_w = sem("sA_w")
            sA_ms = sem("sA_ms")
            sA_x = [sem(f"sA_x{s}") for s in range(4)]
            sA_tr = sem("sA_tr")
            sA_trE = sem("sA_trE")
            sA_mm = sem("sA_mm")
            sA_pq = sem("sA_pq")
            sA_pk = sem("sA_pk")
            sA_pv = sem("sA_pv")
            sA_evq = sem("sA_evq")
            sA_evk = sem("sA_evk")
            sA_evv = sem("sA_evv")
            sA_stq = [sem(f"sA_stq{s}") for s in range(2)]
            sA_stk = [sem(f"sA_stk{s}") for s in range(2)]
            sA_stv = [sem(f"sA_stv{s}") for s in range(2)]

            blkA = ps.enter_context(nc.Block())

            @blkA.gpsimd
            def _(g):
                g.dma_start(ident[:, :], ident_h[:, :]).then_inc(sA_c, 16)
                g.dma_start(sel[:, :], sel_h[:, :]).then_inc(sA_c, 16)
                for t_i in range(4):
                    g.dma_start(xa[:, (t_i % 4) * 1024:(t_i % 4 + 1) * 1024],
                                x_h[t_i * 128:(t_i + 1) * 128, :]).then_inc(sA_x[t_i % 4], 16)
                for kc in range(KC):
                    g.dma_start(wq_s[:, kc * 1024:(kc + 1) * 1024],
                                wq_h[kc * 128:(kc + 1) * 128, :]).then_inc(sA_w, 16)
                for kc in range(KC):
                    g.dma_start(wk_s[:, kc * 1024:(kc + 1) * 1024],
                                wk_h[kc * 128:(kc + 1) * 128, :]).then_inc(sA_w, 16)
                for kc in range(KC):
                    g.dma_start(wv_s[:, kc * 1024:(kc + 1) * 1024],
                                wv_h[kc * 128:(kc + 1) * 128, :]).then_inc(sA_w, 16)
                for dc in range(8):
                    g.dma_start(bq_s[:, dc:dc + 1],
                                bq_h[dc * 128:(dc + 1) * 128, 0:1]).then_inc(sA_w, 16)
                for dc in range(8):
                    g.dma_start(bk_s[:, dc:dc + 1],
                                bk_h[dc * 128:(dc + 1) * 128, 0:1]).then_inc(sA_w, 16)
                g.dma_start(bv_s[:, :], bv_h[:, :]).then_inc(sA_w, 16)
                g.memset(ones1[:, :], 1.0).then_inc(sA_ms, 1)
                g.memset(eps_t[:, :], EPS).then_inc(sA_ms, 1)
                for t_i in range(4, NTB):
                    g.wait_ge(sA_tr, 8 * (t_i - 4) + 8)
                    g.dma_start(xa[:, (t_i % 4) * 1024:(t_i % 4 + 1) * 1024],
                                x_h[t_i * 128:(t_i + 1) * 128, :]).then_inc(sA_x[t_i % 4], 16)

            W_TOT = 41 * 16

            def emit_tr(t, tc):
                for tbl in range(4):
                    t_i = tc * 4 + tbl
                    t.wait_ge(sA_x[t_i % 4], 16 * (t_i // 4 + 1))
                    for dc in range(8):
                        i = t_i * 8 + dc
                        if i >= 2:
                            t.wait_ge(sA_trE, i - 1)
                        t.transpose(
                            ptr[i % 2][:, :],
                            xa[:, (t_i % 4) * 1024 + dc * 128:(t_i % 4) * 1024 + (dc + 1) * 128],
                            ident[:, :]).then_inc(sA_tr, 1)

            @blkA.tensor
            def _(t):
                t.wait_ge(sA_c, 32)
                emit_tr(t, 0)
                for tc in range(NTCA):
                    par = tc % 2
                    if tc + 1 < NTCA:
                        emit_tr(t, tc + 1)
                    t.wait_ge(sA_trE, 32 * (tc + 1))
                    if tc == 0:
                        t.wait_ge(sA_w, W_TOT)
                        t.wait_ge(sA_ms, 2)

                    def q_group(dc, which):
                        ws, pp, sev, spp = (wq_s, pq, sA_evq, sA_pq) if which == 0 else (
                            wk_s, pk, sA_evk, sA_pk)
                        g = tc * 8 + dc
                        if g >= 2:
                            t.wait_ge(sev, g - 1)
                        for kc in range(KC):
                            mm = t.matmul(
                                pp[g % 2][:, :],
                                ws[:, kc * 1024 + dc * 128:kc * 1024 + (dc + 1) * 128],
                                xt[:, par * 4096 + kc * 512:par * 4096 + (kc + 1) * 512],
                                start=(kc == 0), stop=(kc == KC - 1))
                        mm.then_inc(spp, 1)
                        return mm

                    def v_group(tbl):
                        vi = tc * 4 + tbl
                        if vi >= 1:
                            t.wait_ge(sA_evv, vi)
                        for kc in range(KC):
                            for h in range(2):
                                t.matmul(
                                    pv[:, h * 512:(h + 1) * 512],
                                    xt[:, par * 4096 + kc * 512 + tbl * 128:
                                       par * 4096 + kc * 512 + (tbl + 1) * 128],
                                    wv_s[:, kc * 1024 + h * 512:kc * 1024 + (h + 1) * 512],
                                    start=(kc == 0), stop=False)
                        for h in range(2):
                            mm = t.matmul(pv[:, h * 512:(h + 1) * 512], ones1[:, :],
                                          bv_s[:, h * 512:(h + 1) * 512],
                                          start=False, stop=True)
                        mm.then_inc(sA_pv, 1)
                        return mm

                    last = None
                    for dc in range(4):
                        last = q_group(dc, 0)
                    last = v_group(0)
                    for dc in range(4, 8):
                        last = q_group(dc, 0)
                    last = v_group(1)
                    for dc in range(4):
                        last = q_group(dc, 1)
                    last = v_group(2)
                    for dc in range(4, 8):
                        last = q_group(dc, 1)
                    last = v_group(3)
                    t.nop().then_inc(sA_mm, 1)

            @blkA.vector
            def _(v):
                for tc in range(NTCA + 1):
                    if tc >= NTCA:
                        break
                    for tbl in range(4):
                        t_i = tc * 4 + tbl
                        for dc in range(8):
                            i = t_i * 8 + dc
                            v.wait_ge(sA_tr, i + 1)
                            if tbl == 0 and dc == 0 and tc >= 2:
                                v.wait_ge(sA_mm, tc - 1)
                            v.tensor_copy(
                                xt[:, (tc % 2) * 4096 + dc * 512 + tbl * 128:
                                   (tc % 2) * 4096 + dc * 512 + (tbl + 1) * 128],
                                ptr[i % 2][:, :]).then_inc(sA_trE, 1)

            @blkA.scalar
            def _(sc):
                for tc in range(NTCA):
                    def q_evac(dc, which):
                        ws_stage, pp, bb, spp, sev, sst = (
                            (qs, pq, bq_s, sA_pq, sA_evq, sA_stq) if which == 0
                            else (ks, pk, bk_s, sA_pk, sA_evk, sA_stk))
                        g = tc * 8 + dc
                        sc.wait_ge(spp, g + 1)
                        if g >= 2:
                            sc.wait_ge(sst[g % 2], 16 * (g // 2))
                        sc.activation(ws_stage[:, (g % 2) * 512:(g % 2 + 1) * 512],
                                      pp[g % 2][:, :], AF.Identity,
                                      bias=bb[:, dc:dc + 1], scale=1.0).then_inc(sev, 1)

                    def v_evac(tbl):
                        vi = tc * 4 + tbl
                        sc.wait_ge(sA_pv, vi + 1)
                        if vi >= 2:
                            sc.wait_ge(sA_stv[vi % 2], 16 * (vi // 2))
                        sc.activation(vs[:, (vi % 2) * 1024:(vi % 2 + 1) * 1024],
                                      pv[:, :], AF.Copy).then_inc(sA_evv, 1)

                    for dc in range(4):
                        q_evac(dc, 0)
                    v_evac(0)
                    for dc in range(4, 8):
                        q_evac(dc, 0)
                    v_evac(1)
                    for dc in range(4):
                        q_evac(dc, 1)
                    v_evac(2)
                    for dc in range(4, 8):
                        q_evac(dc, 1)
                    v_evac(3)

            @blkA.sync
            def _(s):
                for tc in range(NTCA):
                    def q_store(dc, which):
                        stage, dst, sev, sst = (qs, qt_h, sA_evq, sA_stq) if which == 0 else (
                            ks, kt_h, sA_evk, sA_stk)
                        g = tc * 8 + dc
                        s.wait_ge(sev, g + 1)
                        s.dma_start(dst[dc * 128:(dc + 1) * 128, tc * 512:(tc + 1) * 512],
                                    stage[:, (g % 2) * 512:(g % 2 + 1) * 512]).then_inc(sst[g % 2], 16)

                    def v_store(tbl):
                        vi = tc * 4 + tbl
                        s.wait_ge(sA_evv, vi + 1)
                        s.dma_start(v_h[vi * 128:(vi + 1) * 128, :],
                                    vs[:, (vi % 2) * 1024:(vi % 2 + 1) * 1024]).then_inc(sA_stv[vi % 2], 16)

                    for dc in range(4):
                        q_store(dc, 0)
                    v_store(0)
                    for dc in range(4, 8):
                        q_store(dc, 0)
                    v_store(1)
                    for dc in range(4):
                        q_store(dc, 1)
                    v_store(2)
                    for dc in range(4, 8):
                        q_store(dc, 1)
                    v_store(3)
                for s2 in range(2):
                    s.wait_ge(sA_stq[s2], 16 * 64)
                    s.wait_ge(sA_stk[s2], 16 * 64)
                    s.wait_ge(sA_stv[s2], 16 * 32)

    if "B" in phases:
        with ExitStack() as ps:
            kt_sb = ps.enter_context(nc.sbuf_tensor("kt_sb", [128, 8 * 8192], BF16))
            maskm = ps.enter_context(nc.sbuf_tensor("maskm_s", [128, 896], F32))
            qt1 = ps.enter_context(nc.sbuf_tensor("qt1", [128, 2 * 4096], BF16))
            mneg_st = ps.enter_context(nc.sbuf_tensor("mneg_st", [128, NKB * 16], F32))
            e_st = ps.enter_context(nc.sbuf_tensor("e_st", [128, NKB * 16], F32))
            tmp_st = ps.enter_context(nc.sbuf_tensor("tmp_st", [128, NKB * 16], F32))
            sp_st = ps.enter_context(nc.sbuf_tensor("sp_st", [128, 4 * 512], F16))
            sc1 = ps.enter_context(nc.sbuf_tensor("sc1", [128, 512], BF16))
            wgp = ps.enter_context(nc.sbuf_tensor("wgp", [128, 2 * 4096], BF16))
            mnegf_st = ps.enter_context(nc.sbuf_tensor("mnegf_st", [128, NKB], F32))
            s_st = ps.enter_context(nc.sbuf_tensor("s_st", [128, NKB], F32))
            lns_st = ps.enter_context(nc.sbuf_tensor("lns_st", [128, NKB], F32))
            ps1 = [ps.enter_context(nc.psum_tensor(f"ps1_{i}", [128, 512], F32)) for i in range(4)]

            sBC_kt = sem("sBC_kt")
            sBC_mk = sem("sBC_mk")
            sB_q = [sem(f"sB_q{s}") for s in range(2)]
            sB_db = sem("sB_db") if dbg else None
            sB_pe = sem("sB_pe")
            sB_v = sem("sB_v")
            sB_a = sem("sB_a")
            sB_cp = sem("sB_cp")
            sB_sp = [sem(f"sB_sp{s}") for s in range(4)]
            sB_pl = [sem(f"sB_pl{s}") for s in range(2)]
            sB_pk = [sem(f"sB_pk{s}") for s in range(2)]
            sB_vf = sem("sB_vf")
            sB_af = sem("sB_af")
            sB_vf2 = sem("sB_vf2")
            sB_af2 = sem("sB_af2")
            sB_fin = sem("sB_fin")
            sB_vs = sem("sB_vs")

            blkB = ps.enter_context(nc.Block())

            def pack_src(pe, which):
                if pe < 8:
                    h = wg_h if which == 0 else wu_h
                    return h[pe * 1024:(pe + 1) * 1024, :].rearrange(
                        "(k p) f -> p k f", p=128)
                h = wsg_h if which == 0 else wsu_h
                c0 = (pe - 8) * 512
                return h[:, c0:c0 + 512].rearrange("(k p) f -> p k f", p=128)

            @blkB.gpsimd
            def _(g):
                def pack_item(i):
                    pe, which = i // 2, i % 2
                    if i >= 2:
                        g.wait_ge(sB_pk[i % 2], 16 * (i // 2))
                    g.dma_start(wgp[:, (i % 2) * 4096:(i % 2 + 1) * 4096]
                                .rearrange("p (k f) -> p k f", k=8),
                                pack_src(pe, which)).then_inc(sB_pl[i % 2], 16)
                    g.wait_ge(sB_pl[i % 2], 16 * (i // 2 + 1))
                    g.dma_start(wgupk_h[:, (pe * 2 + which) * 4096:
                                        (pe * 2 + which + 1) * 4096],
                                wgp[:, (i % 2) * 4096:(i % 2 + 1) * 4096]
                                ).then_inc(sB_pk[i % 2], 16)

                for qt in range(NQT1):
                    for kb in range(4 * qt + 4):
                        n = n_tile_b(qt, kb)
                        if n % 27 == 0 and n // 27 < 20:
                            pack_item(n // 27)
                        g.wait_ge(sB_cp, n + 1)
                        g.dma_start(s_h[n * 128:(n + 1) * 128, :],
                                    sp_st[:, (n % 4) * 512:(n % 4 + 1) * 512]
                                    ).then_inc(sB_sp[n % 4], 16)

            @blkB.sync
            def _(s):
                s.dma_start(maskm[:, :], maskm_h[:, :]).then_inc(sBC_mk, 16)
                for kc in range(KC):
                    s.dma_start(kt_sb[:, kc * 8192:(kc + 1) * 8192],
                                kt_h[kc * 128:(kc + 1) * 128, :]).then_inc(sBC_kt, 16)

                def loads(qt):
                    if qt >= 2:
                        s.wait_ge(sB_pe, n_tile_b(qt - 1, 0))
                    for kc in range(KC):
                        s.dma_start(qt1[:, (qt % 2) * 4096 + kc * 512:
                                        (qt % 2) * 4096 + (kc + 1) * 512],
                                    qt_h[kc * 128:(kc + 1) * 128,
                                         qt * 512:(qt + 1) * 512]).then_inc(sB_q[qt % 2], 16)

                loads(0)
                loads(1)
                for qt in range(2, NQT1):
                    loads(qt)
                if dbg and "C" not in phases:
                    s.wait_ge(sB_fin, NKB)
                    s.dma_start(dbias_h[:, :], bias_st[:, :]).then_inc(sB_db, 16)
                    s.dma_start(dmn_h[:, :], mneg_st[:, :]).then_inc(sB_db, 16)
                    s.dma_start(de_h[:, :], e_st[:, :]).then_inc(sB_db, 16)
                    s.dma_start(ds_h[:, :], s_st[:, :]).then_inc(sB_db, 16)
                    s.wait_ge(sB_db, 64)

            @blkB.tensor
            def _(t):
                t.wait_ge(sBC_kt, 16 * KC)
                for qt in range(NQT1):
                    t.wait_ge(sB_q[qt % 2], 128 * (qt // 2 + 1))
                    for kb in range(4 * qt + 4):
                        n = n_tile_b(qt, kb)
                        if n >= 4:
                            t.wait_ge(sB_v, n - 3)
                            t.wait_ge(sB_cp, n - 3)
                        for kc in range(KC):
                            mm = t.matmul(
                                ps1[n % 4][:, :],
                                kt_sb[:, kc * 8192 + kb * 128:kc * 8192 + (kb + 1) * 128],
                                qt1[:, (qt % 2) * 4096 + kc * 512:(qt % 2) * 4096 + (kc + 1) * 512],
                                start=(kc == 0), stop=(kc == KC - 1))
                        mm.then_inc(sB_pe, 1)

            @blkB.vector
            def _(v):
                nvs = [0]

                def vself(instr):
                    nvs[0] += 1
                    instr.then_inc(sB_vs, 1)
                    v.wait_ge(sB_vs, nvs[0])

                v.wait_ge(sBC_mk, 16)
                for qt in range(NQT1):
                    for kb in range(4 * qt + 4):
                        n = n_tile_b(qt, kb)
                        v.wait_ge(sB_pe, n + 1)
                        if kb >= 4 * qt:
                            off = kb * 128 - qt * 512
                            sm = 384 - off
                            vself(v.tensor_add(ps1[n % 4][:, :], ps1[n % 4][:, :],
                                               maskm[:, sm:sm + 512]))
                        v.reduce_max(mneg_st[:, kb * 16 + qt:kb * 16 + qt + 1],
                                     ps1[n % 4][:, :], axis=AX.X,
                                     negate=True).then_inc(sB_v, 1)
                for kb in range(NKB):
                    qt0 = kb // 4
                    v.tensor_reduce(mnegf_st[:, kb:kb + 1],
                                    mneg_st[:, kb * 16 + qt0:kb * 16 + 16],
                                    axis=AX.X, op=OP.min).then_inc(sB_vf, 1)
                for kb in range(NKB):
                    qt0 = kb // 4
                    v.wait_ge(sB_af, kb + 1)
                    vself(v.tensor_tensor(tmp_st[:, kb * 16 + qt0:kb * 16 + 16],
                                          tmp_st[:, kb * 16 + qt0:kb * 16 + 16],
                                          e_st[:, kb * 16 + qt0:kb * 16 + 16], op=OP.mult))
                    v.reduce_sum(s_st[:, kb:kb + 1],
                                 tmp_st[:, kb * 16 + qt0:kb * 16 + 16],
                                 axis=AX.X).then_inc(sB_vf2, 1)
                for kb in range(NKB):
                    v.wait_ge(sB_af2, kb + 1)
                    v.tensor_sub(bias_st[:, kb:kb + 1], mnegf_st[:, kb:kb + 1],
                                 lns_st[:, kb:kb + 1]).then_inc(sB_fin, 1)

            @blkB.scalar
            def _(sc):
                for qt in range(NQT1):
                    for kb in range(4 * qt + 4):
                        n = n_tile_b(qt, kb)
                        sc.wait_ge(sB_v, n + 1)
                        if n >= 1:
                            sc.wait_ge(sB_a, n)
                        sc.activation(sc1[:, :],
                                      ps1[n % 4][:, :], AF.Exp,
                                      bias=mneg_st[:, kb * 16 + qt:kb * 16 + qt + 1],
                                      scale=1.0,
                                      accum_out=e_st[:, kb * 16 + qt:kb * 16 + qt + 1]
                                      ).then_inc(sB_a, 1)
                        if n >= 4:
                            sc.wait_ge(sB_sp[n % 4], 16 * (n // 4))
                        sc.activation(sp_st[:, (n % 4) * 512:(n % 4 + 1) * 512],
                                      ps1[n % 4][:, :], AF.Copy).then_inc(sB_cp, 1)
                for kb in range(NKB):
                    qt0 = kb // 4
                    sc.wait_ge(sB_vf, kb + 1)
                    sc.activation(tmp_st[:, kb * 16 + qt0:kb * 16 + 16],
                                  mneg_st[:, kb * 16 + qt0:kb * 16 + 16], AF.Exp,
                                  bias=mnegf_st[:, kb:kb + 1],
                                  scale=-1.0).then_inc(sB_af, 1)
                for kb in range(NKB):
                    sc.wait_ge(sB_vf2, kb + 1)
                    sc.activation(lns_st[:, kb:kb + 1], s_st[:, kb:kb + 1], AF.Ln,
                                  scale=32.0).then_inc(sB_af2, 1)

    if "C" in phases:
        assert "B" in phases, "phase C consumes phase B's S spill"
        with ExitStack() as ps:
            v_sb = ps.enter_context(nc.sbuf_tensor("v_sb", [128, NKB * 1024], BF16))
            s2 = ps.enter_context(nc.sbuf_tensor("s2", [128, 6 * 512], F16))
            p2 = ps.enter_context(nc.sbuf_tensor("p2", [128, 2 * 512], BF16))
            x2 = ps.enter_context(nc.sbuf_tensor("x2", [128, 4 * 1024], F32))
            t1s = ps.enter_context(nc.sbuf_tensor("t1s", [128, 2 * 1024], F32))
            h1s = ps.enter_context(nc.sbuf_tensor("h1s", [128, 2 * 1024], F32))
            sq2 = ps.enter_context(nc.sbuf_tensor("sq2", [128, 1024], BF16))
            st2 = ps.enter_context(nc.sbuf_tensor("st2", [128, 16], F32))
            pr = [ps.enter_context(nc.psum_tensor(f"pr{i}", [128, 1024], F32)) for i in range(4)]

            sC_v = [sem(f"sC_v{s}") for s in range(4)]
            sC_s = [sem(f"sC_s{s}") for s in range(6)]
            sC_x2 = [sem(f"sC_x2{s}") for s in range(4)]
            sC_pr = sem("sC_pr")
            sC_a = sem("sC_a")
            sC_vt = sem("sC_vt")
            sC_ae = sem("sC_ae")
            sC_vv = sem("sC_vv")
            sC_va = sem("sC_va")
            sC_sh = [sem(f"sC_sh{s}") for s in range(2)]
            sC_db = sem("sC_db") if dbg else None
            sC_vs = sem("sC_vs")

            TILES_C = [(qt, kb) for qt in range(NQT1) for kb in range(4 * qt + 4)]

            blkC = ps.enter_context(nc.Block())

            @blkC.sync
            def _(s):
                for s_ in range(4):
                    s.wait_ge(sB_sp[s_], 16 * (NB_TOT // 4))
                for s_ in range(2):
                    s.wait_ge(sB_pk[s_], 16 * 10)
                if dbg:
                    s.wait_ge(sB_fin, NKB)
                    s.dma_start(dbias_h[:, :], bias_st[:, :]).then_inc(sC_db, 16)

                def loads(qb):
                    for ic in range(4):
                        u = 4 * qb + ic
                        if qb >= 1:
                            s.wait_ge(sC_vt, 4 * (qb - 1) + ic + 1)
                        s.dma_start(x2[:, ic * 1024:(ic + 1) * 1024],
                                    x_h[u * 128:(u + 1) * 128, :]).then_inc(sC_x2[ic], 16)

                def stores(qb):
                    for ic in range(4):
                        u = 4 * qb + ic
                        s.wait_ge(sC_va, u + 1)
                        s.dma_start(h1_h[u * 128:(u + 1) * 128, :],
                                    h1s[:, (u % 2) * 1024:(u % 2 + 1) * 1024]
                                    ).then_inc(sC_sh[u % 2], 16)

                loads(0)
                for qb in range(1, NQT1):
                    stores(qb - 1)
                    loads(qb)
                stores(NQT1 - 1)
                s.wait_ge(sC_sh[0], 16 * 32)
                s.wait_ge(sC_sh[1], 16 * 32)
                if dbg:
                    s.wait_ge(sC_db, 16)

            @blkC.gpsimd
            def _(g):
                if "A" in phases:
                    g.wait_ge(sA_stv[0], 16 * 32)
                    g.wait_ge(sA_stv[1], 16 * 32)
                g.wait_ge(sB_fin, NKB)
                for s_ in range(4):
                    g.wait_ge(sB_sp[s_], 16 * (NB_TOT // 4))
                for s_ in range(2):
                    g.wait_ge(sB_pk[s_], 16 * 10)
                for kb in range(NKB):
                    g.dma_start(v_sb[:, kb * 1024:(kb + 1) * 1024],
                                v_h[kb * 128:(kb + 1) * 128, :]).then_inc(sC_v[kb // 16], 16)

            @blkC.tensor
            def _(t):
                for qb in range(NQT1):
                    nkb = 4 * qb + 4
                    for kb in range(nkb):
                        n = n_tile_b(qb, kb)
                        t.wait_ge(sC_a, n + 1)
                        t.wait_ge(sC_v[kb // 16], 16 * 16)
                        if kb == 0 and qb >= 1:
                            t.wait_ge(sC_vt, 4 * qb)
                        for ic in range(4):
                            for h in range(2):
                                mm = t.matmul(
                                    pr[ic][:, h * 512:(h + 1) * 512],
                                    p2[:, (n % 2) * 512 + ic * 128:(n % 2) * 512 + (ic + 1) * 128],
                                    v_sb[:, kb * 1024 + h * 512:kb * 1024 + (h + 1) * 512],
                                    start=(kb == 0), stop=(kb == nkb - 1))
                        mm.then_inc(sC_pr, 1)

            @blkC.scalar
            def _(sc):
                for s_ in range(4):
                    sc.wait_ge(sB_sp[s_], 16 * (NB_TOT // 4))
                for s_ in range(2):
                    sc.wait_ge(sB_pk[s_], 16 * 10)
                sc.wait_ge(sB_fin, NKB)

                def issue_s(i):
                    if i >= NB_TOT:
                        return
                    if i >= 6:
                        sc.wait_ge(sC_a, i - 5)
                    sc.dma_start(s2[:, (i % 6) * 512:(i % 6 + 1) * 512],
                                 s_h[i * 128:(i + 1) * 128, :]).then_inc(sC_s[i % 6], 16)

                for i in range(6):
                    issue_s(i)

                def ln_chain(qb):
                    for ic in range(4):
                        u = 4 * qb + ic
                        c0 = (u % 2) * 8
                        sc.wait_ge(sC_vt, u + 1)
                        if u >= 1:
                            sc.wait_ge(sC_ae, 3 * (u - 1) + 1)
                        sc.activation(sq2[:, :], t1s[:, (u % 2) * 1024:(u % 2 + 1) * 1024],
                                      AF.Square,
                                      accum_out=st2[:, c0 + 1:c0 + 2]).then_inc(sC_ae, 1)
                        sc.wait_ge(sC_vv, u + 1)
                        sc.activation(st2[:, c0 + 5:c0 + 6], st2[:, c0 + 4:c0 + 5], AF.Ln,
                                      bias=eps_t[:, 0:1], scale=1.0).then_inc(sC_ae, 1)
                        sc.wait_ge(sC_ae, 3 * u + 2)
                        sc.activation(st2[:, c0 + 6:c0 + 7], st2[:, c0 + 5:c0 + 6], AF.Exp,
                                      scale=-0.5).then_inc(sC_ae, 1)

                for qb in range(NQT1):
                    for kb in range(4 * qb + 4):
                        n = n_tile_b(qb, kb)
                        sc.wait_ge(sC_s[n % 6], 16 * (n // 6 + 1))
                        if n >= 2:
                            sc.wait_ge(sC_pr, n - 1)
                        sc.activation(p2[:, (n % 2) * 512:(n % 2 + 1) * 512],
                                      s2[:, (n % 6) * 512:(n % 6 + 1) * 512], AF.Exp,
                                      bias=bias_st[:, kb:kb + 1], scale=1.0).then_inc(sC_a, 1)
                        issue_s(n + 6)
                    ln_chain(qb)

            @blkC.vector
            def _(v):
                for s_ in range(2):
                    v.wait_ge(sB_pk[s_], 16 * 10)
                for s_ in range(4):
                    v.wait_ge(sB_sp[s_], 16 * (NB_TOT // 4))
                nvs = [0]

                def vself(instr):
                    nvs[0] += 1
                    instr.then_inc(sC_vs, 1)
                    v.wait_ge(sC_vs, nvs[0])

                for qb in range(NQT1):
                    for ic in range(4):
                        u = 4 * qb + ic
                        c0 = (u % 2) * 8
                        v.wait_ge(sC_pr, n_tile_b(qb, 0) + 4 * qb + 4)
                        v.wait_ge(sC_x2[ic], 16 * (qb + 1))
                        if u >= 2:
                            v.wait_ge(sC_ae, 3 * (u - 2) + 1)
                            v.wait_ge(sC_va, u - 1)
                        v.scalar_tensor_tensor(
                            t1s[:, (u % 2) * 1024:(u % 2 + 1) * 1024],
                            x2[:, ic * 1024:(ic + 1) * 1024],
                            2.0, pr[ic][:, :], op0=OP.mult, op1=OP.add).then_inc(sC_vt, 1)
                        v.wait_ge(sC_vt, u + 1)
                        vself(v.reduce_sum(st2[:, c0:c0 + 1],
                                           t1s[:, (u % 2) * 1024:(u % 2 + 1) * 1024], axis=AX.X))
                        v.wait_ge(sC_ae, 3 * u + 1)
                        vself(v.tensor_scalar(st2[:, c0 + 2:c0 + 3], st2[:, c0:c0 + 1],
                                              1.0 / D, None, op0=OP.mult))
                        vself(v.tensor_tensor(st2[:, c0 + 3:c0 + 4], st2[:, c0 + 2:c0 + 3],
                                              st2[:, c0 + 2:c0 + 3], op=OP.mult))
                        v.scalar_tensor_tensor(st2[:, c0 + 4:c0 + 5], st2[:, c0 + 1:c0 + 2],
                                               1.0 / D, st2[:, c0 + 3:c0 + 4],
                                               op0=OP.mult, op1=OP.subtract).then_inc(sC_vv, 1)
                        v.wait_ge(sC_ae, 3 * u + 3)
                        if u >= 2:
                            v.wait_ge(sC_sh[u % 2], 16 * (u // 2))
                        v.tensor_scalar(h1s[:, (u % 2) * 1024:(u % 2 + 1) * 1024],
                                        t1s[:, (u % 2) * 1024:(u % 2 + 1) * 1024],
                                        st2[:, c0 + 2:c0 + 3], st2[:, c0 + 6:c0 + 7],
                                        op0=OP.subtract, op1=OP.mult).then_inc(sC_va, 1)

    if "E" in phases:
        with ExitStack() as ps:
            wd_sb = ps.enter_context(nc.sbuf_tensor("wd_sb", [128, NPE * 4 * 1024], BF16))
            rw_sb = ps.enter_context(nc.sbuf_tensor("rw_sb", [128, 64], BF16))
            h1b = ps.enter_context(nc.sbuf_tensor("h1b", [128, 4 * 1024], BF16))
            h1t = ps.enter_context(nc.sbuf_tensor("h1t", [128, 2 * 4096], BF16))
            wgu = ps.enter_context(nc.sbuf_tensor("wgu", [128, 2 * 8192], BF16))
            a_sb = ps.enter_context(nc.sbuf_tensor("a_sb", [128, NPE * 4 * 512], BF16))
            wt_sb = ps.enter_context(nc.sbuf_tensor("wt_sb", [E, 512], BF16))
            wtil = ps.enter_context(nc.sbuf_tensor("wtil", [128, E * 512], BF16))
            sg_sb = ps.enter_context(nc.sbuf_tensor("sg_sb", [128, 2 * 512], BF16))
            ub_sb = ps.enter_context(nc.sbuf_tensor("ub_sb", [128, 512], BF16))
            h1f = ps.enter_context(nc.sbuf_tensor("h1f", [128, 2 * 1024], F32))
            t2s = ps.enter_context(nc.sbuf_tensor("t2s", [128, 1024], F32))
            st3 = ps.enter_context(nc.sbuf_tensor("st3", [128, 16], F32))
            wtb = ps.enter_context(nc.sbuf_tensor("wtb", [128, 4 * 8], BF16))
            lg_sb = ps.enter_context(nc.sbuf_tensor("lg_sb", [128, 8], F32))
            el_sb = ps.enter_context(nc.sbuf_tensor("el_sb", [128, 8], F32))
            rr = ps.enter_context(nc.sbuf_tensor("rr", [128, 32], F32))
            outs = ps.enter_context(nc.sbuf_tensor("outs", [128, 1024], F32))
            yA = ps.enter_context(nc.psum_tensor("yA", [128, 1024], F32))
            pg = ps.enter_context(nc.psum_tensor("pg", [128, 512], F32))
            pu = ps.enter_context(nc.psum_tensor("pu", [128, 512], F32))
            ptr2f = [ps.enter_context(nc.psum_tensor(f"ptr2f{i}", [128, 128], BF16))
                     for i in range(4)]

            sE_wrw = sem("sE_wrw")
            sE_wwd = sem("sE_wwd")
            sE_h1b = [sem(f"sE_h1b{s}") for s in range(4)]
            sE_wgu = [sem(f"sE_wgu{s}") for s in range(2)]
            sE_tr = sem("sE_tr")
            sE_trE = sem("sE_trE")
            sE_petc = sem("sE_petc")
            sE_pl = sem("sE_pl")
            sE_vcp = sem("sE_vcp")
            sE_ae2 = sem("sE_ae2")
            sE_vwt = sem("sE_vwt")
            sE_pw = sem("sE_pw")
            sE_aw = sem("sE_aw")
            sE_pg = sem("sE_pg")
            sE_pu = sem("sE_pu")
            sE_as = sem("sE_as")
            sE_vu = sem("sE_vu")
            sE_va = sem("sE_va")
            sE_py = sem("sE_py")
            sE_h1f = [sem(f"sE_h1f{s}") for s in range(2)]
            sE_vt2 = sem("sE_vt2")
            sE_ae3 = sem("sE_ae3")
            sE_vv3 = sem("sE_vv3")
            sE_vo = sem("sE_vo")
            sE_so = sem("sE_so")
            sE_vs = sem("sE_vs")

            blkE = ps.enter_context(nc.Block())

            def wd_src(pe, fc):
                if pe < 8:
                    return wd_h[pe * 512 + fc * 128:pe * 512 + (fc + 1) * 128, :]
                return wsd_h[(pe - 8) * 512 + fc * 128:(pe - 8) * 512 + (fc + 1) * 128, :]

            @blkE.gpsimd
            def _(g):
                if "C" in phases:
                    g.wait_ge(sC_sh[0], 16 * NQB2)
                    g.wait_ge(sC_sh[1], 16 * NQB2)
                if "B" in phases:
                    g.wait_ge(sB_pk[0], 16 * 10)
                    g.wait_ge(sB_pk[1], 16 * 10)
                for kc in range(KC):
                    g.dma_start(rw_sb[:, kc * 8:(kc + 1) * 8],
                                rw_h[kc * 128:(kc + 1) * 128, :]).then_inc(sE_wrw, 16)
                for gtb in range(4):
                    g.dma_start(h1b[:, (gtb % 4) * 1024:(gtb % 4 + 1) * 1024],
                                h1_h[gtb * 128:(gtb + 1) * 128, :]).then_inc(sE_h1b[gtb % 4], 16)
                for u in range(2):
                    g.dma_start(wgu[:, (u % 2) * 8192:(u % 2 + 1) * 8192],
                                wgupk_h[:, u * 8192:(u + 1) * 8192]
                                ).then_inc(sE_wgu[u % 2], 16)
                for pe in range(NPE):
                    for fc in range(4):
                        g.dma_start(wd_sb[:, (pe * 4 + fc) * 1024:(pe * 4 + fc + 1) * 1024],
                                    wd_src(pe, fc)).then_inc(sE_wwd, 16)
                for tc in range(NTCE):
                    for tbl in range(4):
                        gtb = tc * 4 + 4 + tbl
                        if gtb >= NTB:
                            continue
                        g.wait_ge(sE_tr, tc * 36 + (tbl + 1) * 8)
                        g.dma_start(h1b[:, (gtb % 4) * 1024:(gtb % 4 + 1) * 1024],
                                    h1_h[gtb * 128:(gtb + 1) * 128, :]).then_inc(sE_h1b[gtb % 4], 16)
                    for pe in range(NPE):
                        unit = tc * NPE + pe + 2
                        if unit >= NTCE * NPE:
                            continue
                        npe = unit % NPE
                        g.wait_ge(sE_pu, 4 * (unit - 1))
                        g.dma_start(wgu[:, (unit % 2) * 8192:(unit % 2 + 1) * 8192],
                                    wgupk_h[:, npe * 8192:(npe + 1) * 8192]
                                    ).then_inc(sE_wgu[unit % 2], 16)

            @blkE.tensor
            def _(t):
                def tr_chunk(tc2, grp):
                    for k in range(4):
                        j = grp * 4 + k
                        tbl, dc = j // 8, j % 8
                        gtb = tc2 * 4 + tbl
                        if dc == 0:
                            t.wait_ge(sE_h1b[gtb % 4], 16 * (gtb // 4 + 1))
                        i3 = tc2 * 36 + tbl * 8 + dc
                        if i3 >= 4:
                            t.wait_ge(sE_trE, i3 - 3)
                        t.transpose(
                            ptr2f[i3 % 4][:, :],
                            h1b[:, (gtb % 4) * 1024 + dc * 128:(gtb % 4) * 1024 + (dc + 1) * 128],
                            ident[:, :]).then_inc(sE_tr, 1)

                for tc in range(NTCE):
                    par = tc % 2
                    if tc == 0:
                        for grp in range(8):
                            tr_chunk(0, grp)
                    t.wait_ge(sE_trE, tc * 36 + 32)
                    if tc == 0:
                        t.wait_ge(sE_wrw, 16 * KC)
                    for tbl in range(4):
                        idx = tc * 4 + tbl
                        if idx >= 1:
                            t.wait_ge(sE_vcp, idx)
                        if tc >= 1 and tbl == 0:
                            t.wait_ge(sE_as, 40 * tc)
                        for kc in range(KC):
                            mm = t.matmul(
                                pg[:, 0:8],
                                h1t[:, par * 4096 + kc * 512 + tbl * 128:
                                    par * 4096 + kc * 512 + (tbl + 1) * 128],
                                rw_sb[:, kc * 8:(kc + 1) * 8],
                                start=(kc == 0), stop=(kc == KC - 1))
                        mm.then_inc(sE_pl, 1)
                    for tbl in range(4):
                        idx = tc * 4 + tbl
                        i3 = tc * 36 + 32 + tbl
                        t.wait_ge(sE_vwt, idx + 1)
                        if i3 >= 4:
                            t.wait_ge(sE_trE, i3 - 3)
                        t.transpose(ptr2f[i3 % 4][0:8, :],
                                    wtb[:, tbl * 8:(tbl + 1) * 8],
                                    ident[:, :]).then_inc(sE_tr, 1)
                    t.wait_ge(sE_trE, 36 * (tc + 1))
                    if tc >= 1:
                        t.wait_ge(sE_vu, 40 * tc)
                    for e in range(E):
                        if e >= 2:
                            t.wait_ge(sE_aw, 8 * tc + e - 1)
                        dst = pg if e % 2 == 0 else pu
                        mm = t.matmul(dst[:, :], sel[:, e * 128:(e + 1) * 128],
                                      wt_sb[:, :], start=True, stop=True)
                        mm.then_inc(sE_pw, 1)
                    for pe in range(NPE):
                        unit = tc * NPE + pe
                        w0 = tc * 40 + pe * 4
                        t.wait_ge(sE_wgu[unit % 2], 16 * (unit // 2 + 1))
                        for fc in range(4):
                            w = w0 + fc
                            if pe == 0 and fc == 0:
                                t.wait_ge(sE_aw, 8 * (tc + 1))
                            if w >= 1:
                                t.wait_ge(sE_as, w)
                            for kc in range(KC):
                                mm = t.matmul(
                                    pg[:, :],
                                    wgu[:, (unit % 2) * 8192 + kc * 512 + fc * 128:
                                        (unit % 2) * 8192 + kc * 512 + (fc + 1) * 128],
                                    h1t[:, par * 4096 + kc * 512:par * 4096 + (kc + 1) * 512],
                                    start=(kc == 0), stop=(kc == KC - 1))
                            mm.then_inc(sE_pg, 1)
                            if w >= 1:
                                t.wait_ge(sE_vu, w)
                            for kc in range(KC):
                                mm = t.matmul(
                                    pu[:, :],
                                    wgu[:, (unit % 2) * 8192 + 4096 + kc * 512 + fc * 128:
                                        (unit % 2) * 8192 + 4096 + kc * 512 + (fc + 1) * 128],
                                    h1t[:, par * 4096 + kc * 512:par * 4096 + (kc + 1) * 512],
                                    start=(kc == 0), stop=(kc == KC - 1))
                            mm.then_inc(sE_pu, 1)
                            if pe == NPE - 1 and fc == 3:
                                t.nop().then_inc(sE_petc, 1)
                        if tc + 1 < NTCE and pe < 8:
                            tr_chunk(tc + 1, pe)
                    t.wait_ge(sE_va, 40 * (tc + 1))
                    if tc == 0:
                        t.wait_ge(sE_wwd, 16 * NPE * 4)
                    for tbl in range(4):
                        gtb = tc * 4 + tbl
                        y = yA
                        if gtb >= 1:
                            t.wait_ge(sE_vt2, gtb)
                        for pe in range(NPE):
                            for fc in range(4):
                                for h in range(2):
                                    mm = t.matmul(
                                        y[:, h * 512:(h + 1) * 512],
                                        a_sb[:, (pe * 4 + fc) * 512 + tbl * 128:
                                             (pe * 4 + fc) * 512 + (tbl + 1) * 128],
                                        wd_sb[:, (pe * 4 + fc) * 1024 + h * 512:
                                              (pe * 4 + fc) * 1024 + (h + 1) * 512],
                                        start=(pe == 0 and fc == 0),
                                        stop=(pe == NPE - 1 and fc == 3))
                        mm.then_inc(sE_py, 1)

            @blkE.vector
            def _(v):
                nvs = [0]

                def vself(instr):
                    nvs[0] += 1
                    instr.then_inc(sE_vs, 1)
                    v.wait_ge(sE_vs, nvs[0])

                def ev_chunk(tc2, grp):
                    par2 = tc2 % 2
                    for k in range(4):
                        j = grp * 4 + k
                        tbl, dc = j // 8, j % 8
                        i3 = tc2 * 36 + tbl * 8 + dc
                        v.wait_ge(sE_tr, i3 + 1)
                        if j == 0 and tc2 >= 2:
                            v.wait_ge(sE_petc, tc2 - 1)
                        v.tensor_copy(
                            h1t[:, par2 * 4096 + dc * 512 + tbl * 128:
                                par2 * 4096 + dc * 512 + (tbl + 1) * 128],
                            ptr2f[i3 % 4][:, :]).then_inc(sE_trE, 1)

                for tc in range(NTCE):
                    par = tc % 2
                    if tc == 0:
                        for grp in range(8):
                            ev_chunk(0, grp)
                    for tbl in range(4):
                        idx = tc * 4 + tbl
                        v.wait_ge(sE_pl, idx + 1)
                        vself(v.tensor_copy(lg_sb[:, :], pg[:, 0:8]))
                        v.reduce_max(rr[:, 0:1], lg_sb[:, :], axis=AX.X,
                                     negate=True).then_inc(sE_vcp, 1)
                        v.wait_ge(sE_ae2, idx + 1)
                        vself(v.reciprocal(rr[:, 2:3], rr[:, 1:2]))
                        vself(v.tensor_scalar(rr[:, 8:16], el_sb[:, :], rr[:, 2:3], None,
                                              op0=OP.mult))
                        vself(v.tensor_reduce(rr[:, 3:4], rr[:, 8:16], axis=AX.X, op=OP.min))
                        vself(v.tensor_scalar(rr[:, 16:24], rr[:, 8:16], rr[:, 3:4], None,
                                              op0=OP.is_le))
                        vself(v.scalar_tensor_tensor(rr[:, 16:24], rr[:, 16:24], BIG,
                                                     rr[:, 8:16], op0=OP.mult, op1=OP.add))
                        vself(v.tensor_reduce(rr[:, 4:5], rr[:, 16:24], axis=AX.X, op=OP.min))
                        vself(v.tensor_scalar(rr[:, 24:32], rr[:, 16:24], rr[:, 4:5], None,
                                              op0=OP.is_le))
                        vself(v.scalar_tensor_tensor(rr[:, 24:32], rr[:, 24:32], BIG,
                                                     rr[:, 16:24], op0=OP.mult, op1=OP.add))
                        vself(v.tensor_reduce(rr[:, 5:6], rr[:, 24:32], axis=AX.X, op=OP.min))
                        vself(v.tensor_scalar(rr[:, 16:24], rr[:, 8:16], rr[:, 5:6], None,
                                              op0=OP.is_ge))
                        v.tensor_tensor(wtb[:, tbl * 8:(tbl + 1) * 8], rr[:, 8:16],
                                        rr[:, 16:24], op=OP.mult).then_inc(sE_vwt, 1)
                    for tbl in range(4):
                        i3 = tc * 36 + 32 + tbl
                        v.wait_ge(sE_tr, i3 + 1)
                        v.tensor_copy(wt_sb[:, tbl * 128:(tbl + 1) * 128],
                                      ptr2f[i3 % 4][0:8, :]).then_inc(sE_trE, 1)
                    for pe in range(NPE):
                        for fc in range(4):
                            w = tc * 40 + pe * 4 + fc
                            v.wait_ge(sE_pu, w + 1)
                            if w >= 1:
                                v.wait_ge(sE_va, w)
                            if pe == 0 and fc == 0 and tc >= 1:
                                v.wait_ge(sE_py, 4 * tc)
                            if pe < 8:
                                v.wait_ge(sE_aw, 8 * tc + pe + 1)
                                v.tensor_tensor(ub_sb[:, :],
                                                pu[:, :], wtil[:, pe * 512:(pe + 1) * 512],
                                                op=OP.mult).then_inc(sE_vu, 1)
                            else:
                                v.tensor_copy(ub_sb[:, :],
                                              pu[:, :]).then_inc(sE_vu, 1)
                            v.wait_ge(sE_vu, w + 1)
                            v.wait_ge(sE_as, w + 1)
                            v.tensor_tensor(a_sb[:, (pe * 4 + fc) * 512:
                                                 (pe * 4 + fc + 1) * 512],
                                            sg_sb[:, (w % 2) * 512:(w % 2 + 1) * 512],
                                            ub_sb[:, :],
                                            op=OP.mult).then_inc(sE_va, 1)
                        if tc + 1 < NTCE and pe < 8:
                            ev_chunk(tc + 1, pe)
                    for tbl in range(4):
                        gtb = tc * 4 + tbl
                        y = yA
                        c0 = (gtb % 2) * 8
                        v.wait_ge(sE_py, gtb + 1)
                        v.wait_ge(sE_h1f[gtb % 2], 16 * (gtb // 2 + 1))
                        if gtb >= 1:
                            v.wait_ge(sE_ae3, 3 * (gtb - 1) + 1)
                            v.wait_ge(sE_vo, gtb)
                        v.tensor_add(t2s[:, :],
                                     y[:, :],
                                     h1f[:, (gtb % 2) * 1024:(gtb % 2 + 1) * 1024]
                                     ).then_inc(sE_vt2, 1)
                        v.wait_ge(sE_vt2, gtb + 1)
                        vself(v.reduce_sum(st3[:, c0:c0 + 1], t2s[:, :], axis=AX.X))
                        v.wait_ge(sE_ae3, 3 * gtb + 1)
                        vself(v.tensor_scalar(st3[:, c0 + 2:c0 + 3], st3[:, c0:c0 + 1],
                                              1.0 / D, None, op0=OP.mult))
                        vself(v.tensor_tensor(st3[:, c0 + 3:c0 + 4], st3[:, c0 + 2:c0 + 3],
                                              st3[:, c0 + 2:c0 + 3], op=OP.mult))
                        v.scalar_tensor_tensor(st3[:, c0 + 4:c0 + 5], st3[:, c0 + 1:c0 + 2],
                                               1.0 / D, st3[:, c0 + 3:c0 + 4],
                                               op0=OP.mult, op1=OP.subtract).then_inc(sE_vv3, 1)
                        v.wait_ge(sE_ae3, 3 * gtb + 3)
                        if gtb >= 1:
                            v.wait_ge(sE_so, 16 * gtb)
                        v.tensor_scalar(outs[:, :],
                                        t2s[:, :],
                                        st3[:, c0 + 2:c0 + 3], st3[:, c0 + 6:c0 + 7],
                                        op0=OP.subtract, op1=OP.mult).then_inc(sE_vo, 1)

            @blkE.scalar
            def _(sc):
                for tc in range(NTCE):
                    for tbl in range(4):
                        idx = tc * 4 + tbl
                        sc.wait_ge(sE_vcp, idx + 1)
                        sc.activation(el_sb[:, :], lg_sb[:, :], AF.Exp,
                                      bias=rr[:, 0:1], scale=1.0,
                                      accum_out=rr[:, 1:2]).then_inc(sE_ae2, 1)
                    for e in range(E):
                        sc.wait_ge(sE_pw, 8 * tc + e + 1)
                        if e == 0 and tc >= 1:
                            sc.wait_ge(sE_vu, 40 * tc)
                        src = pg if e % 2 == 0 else pu
                        sc.activation(wtil[:, e * 512:(e + 1) * 512], src[:, :],
                                      AF.Copy).then_inc(sE_aw, 1)
                    for pe in range(NPE):
                        for fc in range(4):
                            w = tc * 40 + pe * 4 + fc
                            sc.wait_ge(sE_pg, w + 1)
                            if w >= 2:
                                sc.wait_ge(sE_va, w - 1)
                            sc.activation(sg_sb[:, (w % 2) * 512:(w % 2 + 1) * 512],
                                          pg[:, :], AF.Silu).then_inc(sE_as, 1)
                    for tbl in range(4):
                        gtb = tc * 4 + tbl
                        c0 = (gtb % 2) * 8
                        sc.wait_ge(sE_vt2, gtb + 1)
                        if gtb >= 1:
                            sc.wait_ge(sE_so, 16 * gtb)
                        sc.activation(outs[:, :], t2s[:, :], AF.Square,
                                      accum_out=st3[:, c0 + 1:c0 + 2]).then_inc(sE_ae3, 1)
                        sc.wait_ge(sE_vv3, gtb + 1)
                        sc.activation(st3[:, c0 + 5:c0 + 6], st3[:, c0 + 4:c0 + 5], AF.Ln,
                                      bias=eps_t[:, 0:1], scale=1.0).then_inc(sE_ae3, 1)
                        sc.wait_ge(sE_ae3, 3 * gtb + 2)
                        sc.activation(st3[:, c0 + 6:c0 + 7], st3[:, c0 + 5:c0 + 6], AF.Exp,
                                      scale=-0.5).then_inc(sE_ae3, 1)

            @blkE.sync
            def _(s):
                for tc in range(NTCE):
                    for tbl in range(4):
                        gtb = tc * 4 + tbl
                        if gtb >= 2:
                            s.wait_ge(sE_vt2, gtb - 1)
                        s.dma_start(h1f[:, (gtb % 2) * 1024:(gtb % 2 + 1) * 1024],
                                    h1_h[gtb * 128:(gtb + 1) * 128, :]).then_inc(sE_h1f[gtb % 2], 16)
                    for tbl in range(4):
                        gtb = tc * 4 + tbl
                        s.wait_ge(sE_vo, gtb + 1)
                        s.dma_start(out_h[gtb * 128:(gtb + 1) * 128, :],
                                    outs[:, :]).then_inc(sE_so, 16)
                if dbg:
                    s.wait_ge(sE_vwt, 64)
                s.wait_ge(sE_so, 16 * 64)

            if dbg:
                pass

    es.close()
    return nc


def make_consts():
    ident = np.eye(128, dtype=ml_dtypes.bfloat16)
    p = np.arange(128)[:, None]
    m = np.arange(896)[None, :]
    maskm = np.where(m >= p + 384, 0.0, NEGINF).astype(np.float32)
    sel = np.zeros((E, E * 128), dtype=ml_dtypes.bfloat16)
    for e in range(E):
        sel[e, e * 128:(e + 1) * 128] = 1.0
    return ident, maskm, sel


def kernel(**inputs):
    f = lambda k: np.ascontiguousarray(np.asarray(inputs[k], dtype=np.float32))
    x = f("L0_attn_minibatch")
    wq, wk, wv = f("L0_attn_Wq"), f("L0_attn_Wk"), f("L0_attn_Wv")
    bq = f("L0_attn_bq").reshape(D, 1)
    bk = f("L0_attn_bk").reshape(D, 1)
    bv = f("L0_attn_bv").reshape(1, D)
    rw = f("L0_ffn_router_w")
    wg = f("L0_ffn_w_gate").reshape(E * D, FF)
    wu = f("L0_ffn_w_up").reshape(E * D, FF)
    wd = f("L0_ffn_w_down").reshape(E * FF, D)
    wsg, wsu, wsd = f("L0_ffn_ws_gate"), f("L0_ffn_ws_up"), f("L0_ffn_ws_down")
    ident, maskm, sel = make_consts()

    nc = build_program()
    base = dict(wq=wq, wk=wk, wv=wv, bq=bq, bk=bk, bv=bv, rw=rw, wg=wg, wu=wu,
                wd=wd, wsg=wsg, wsu=wsu, wsd=wsd, ident=ident, maskm=maskm, sel=sel)
    in_maps = [dict(base, x=np.ascontiguousarray(x[b])) for b in range(8)]
    res = run_bass_kernel_spmd(nc, in_maps, core_ids=list(range(8)))
    return np.stack([np.asarray(r["out"]) for r in res.results], axis=0)
